# version 06
# baseline (speedup 1.0000x reference)
import numpy as np
import ml_dtypes
import concourse.bass as bass
import concourse.mybir as mybir
from concourse.bass_utils import run_bass_kernel_spmd

F32 = mybir.dt.float32
BF16 = mybir.dt.bfloat16
I32 = mybir.dt.int32
AF = mybir.ActivationFunctionType
ALU = mybir.AluOpType
AXX = mybir.AxisListType.X

D = 1024
DC = 8
T = 2048
S = 4096
H = 16
HD = 64
FF = 3072
FC = 24
NCORE = 8
ARENA = 100352

SCL_FFN = float(np.float32(0.5) * np.float32(1.0 / 127.0) * np.float32(1.0 / 127.0))
INV127 = float(np.float32(1.0 / 127.0))
EPS = 1e-5

O_X = 0
O_XN = 32768
O_HALL = 49152
O_QALL = 49152
O_WVT = 65536
O_RCQ, O_RSQ, O_RCK, O_RSK = 65536, 67584, 69632, 71680
O_KZ = (65536, 69632)
O_WG = (73728, 74752)
O_WU = (75776, 76800)
O_SG = (77824, 78848)
O_UP = (79872, 80896)
O_WD = (73728, 76800)
O_QRAW = 73728
O_KRAW = 75776
O_VOW = (77824, 79872)
O_WSL0 = 96256
O_VH = (73728, 77824)
O_XBF = (81920, 83968)
O_XSQ = (86016, 88064)
O_RAB = 94208
O_RBB = 96256
O_RTMP = 90112
O_RTM2 = 92160
O_QSQ = 94208
O_PA = (81920, 82944, 83968, 84992)
O_PB = (86016, 87040, 88064, 89088)
O_RROW = 90112
O_RROWB = 94208
O_OST = 96256
O_WSL1 = 98304
O_WO = (96256, 98304)
O_ROWA = 86016
O_ROWB = 90112


class Sem:
    def __init__(self, stack, nc, name):
        self.h = stack.enter_context(nc.semaphore(name))
        self.n = 0


class Eng:

    def __init__(self, name, chain_sem=None):
        self.name = name
        self.ops = []
        self.seen = {}
        self.chain = chain_sem

    def wait(self, sem, v=None):
        if v is None:
            v = sem.n
        if v <= 0 or self.seen.get(id(sem), -1) >= v:
            return
        self.seen[id(sem)] = v
        self.ops.append(("w", sem.h, v))

    def op(self, fn):
        if self.chain is not None:
            self.wait(self.chain, self.chain.n)
            self.chain.n += 1
            self.ops.append(("i", fn, self.chain.h, 1))
            return self.chain.n
        self.ops.append(("o", fn))

    def opi(self, fn, sem, k=1):
        if self.chain is not None:
            assert sem is self.chain and k == 1
            return self.op(fn)
        sem.n += k
        self.ops.append(("i", fn, sem.h, k))
        return sem.n

    def dma(self, fn, sem):
        sem.n += 16
        self.ops.append(("i", fn, sem.h, 16))
        return sem.n

    def replay(self):
        ops = self.ops

        def f(e):
            for it in ops:
                if it[0] == "w":
                    e.wait_ge(it[1], it[2])
                elif it[0] == "o":
                    it[1](e)
                else:
                    it[1](e).then_inc(it[2], it[3])
        return f


def build_nc(upto="full"):
    from contextlib import ExitStack
    nc = bass.Bass("TRN2", target_bir_lowering=False, num_devices=NCORE)

    dx = nc.dram_tensor("x", [D, T], F32, kind="ExternalInput").ap()
    dwqT = nc.dram_tensor("wqT", [D, D], F32, kind="ExternalInput").ap()
    dwkT = nc.dram_tensor("wkT", [D, D], F32, kind="ExternalInput").ap()
    dwvT = nc.dram_tensor("wvT", [D, D], F32, kind="ExternalInput").ap()
    dwoT = nc.dram_tensor("woT", [HD, H * D], F32, kind="ExternalInput").ap()
    dwgT = nc.dram_tensor("wgT", [D, FF], I32, kind="ExternalInput").ap()
    dwuT = nc.dram_tensor("wuT", [D, FF], I32, kind="ExternalInput").ap()
    dwdT = nc.dram_tensor("wdT", [FF, D], I32, kind="ExternalInput").ap()
    dbq = nc.dram_tensor("bq", [128, DC], F32, kind="ExternalInput").ap()
    dbk = nc.dram_tensor("bk", [128, DC], F32, kind="ExternalInput").ap()
    dbo = nc.dram_tensor("bo", [128, DC], F32, kind="ExternalInput").ap()
    dbv = nc.dram_tensor("bvrow", [1, D], F32, kind="ExternalInput").ap()
    dcq = nc.dram_tensor("ropeCq", [128, T], BF16, kind="ExternalInput").ap()
    dsq = nc.dram_tensor("ropeSq", [128, T], BF16, kind="ExternalInput").ap()
    dck = nc.dram_tensor("ropeCk", [128, T], BF16, kind="ExternalInput").ap()
    dsk = nc.dram_tensor("ropeSk", [128, T], BF16, kind="ExternalInput").ap()
    dpsw = nc.dram_tensor("pswapT", [128, 128], BF16, kind="ExternalInput").ap()
    dci = nc.dram_tensor("cinfo", [1, 8], I32, kind="ExternalInput").ap()
    dy = nc.dram_tensor("y", [D, T], F32, kind="ExternalOutput").ap()

    dKx = nc.dram_tensor("Kx", [D, S], BF16, kind="Internal",
                         addr_space="Shared").ap()
    dVx = nc.dram_tensor("Vx", [H, S, 128], BF16, kind="Internal",
                         addr_space="Shared").ap()
    dKM = nc.dram_tensor("KMx", [1, 2], F32, kind="Internal",
                         addr_space="Shared").ap()
    dXsp = nc.dram_tensor("Xsp", [D, T], F32, kind="Internal").ap()
    dccin = nc.dram_tensor("ccin", [1, 32], F32, kind="Internal").ap()
    dccout = nc.dram_tensor("ccout", [1, 32], F32, kind="Internal").ap()

    with ExitStack() as stk:
        en = stk.enter_context
        arena = en(nc.sbuf_tensor("arena", [128, ARENA], BF16))
        arh = arena[:].tensor
        ar32 = arh.bitcast(F32)

        def bf(off, n, p=128):
            return arh[0:p, off:off + n]

        def f32(off, n32, p=128):
            assert off % 2 == 0
            return ar32[0:p, off // 2: off // 2 + n32]

        X = f32(O_X, 16384)
        O_all = bf(O_X, 32768, p=64)
        O_128 = bf(O_X, 32768, p=128)
        XN = bf(O_XN, 16384)
        AOUT = bf(O_XN, 16384)
        HALL = bf(O_HALL, 24576)
        Q_all = bf(O_QALL, 16384)
        WVT = bf(O_WVT, 8192)
        RCQ, RSQ, RCK, RSK = (bf(O_RCQ, 2048), bf(O_RSQ, 2048),
                              bf(O_RCK, 2048), bf(O_RSK, 2048))
        KZb = [bf(o, 4096) for o in O_KZ]
        WGb = [bf(o, 1024) for o in O_WG]
        WUb = [bf(o, 1024) for o in O_WU]
        SGb = [bf(o, 1024) for o in O_SG]
        UPb = [bf(o, 1024) for o in O_UP]
        WDb = [bf(o, 3072) for o in O_WD]
        RAWK = [bf(O_QRAW, 2048), bf(O_KRAW, 2048)]
        VOWb = [bf(o, 2048) for o in O_VOW]
        WSLb = [bf(O_WSL0, 1024), bf(O_WSL1, 1024)]
        VHb = [bf(o, 4096) for o in O_VH]
        XBFb = [bf(o, 2048) for o in O_XBF]
        XSQb = [bf(o, 2048) for o in O_XSQ]
        rowAb = bf(O_RAB, 2048, p=1)
        rowBb = bf(O_RBB, 2048, p=1)
        RTMP = bf(O_RTMP, 2048)
        RTM2 = bf(O_RTM2, 2048)
        QSQ = bf(O_QSQ, 2048)
        PAb = [bf(o, 1024) for o in O_PA]
        PBb = [bf(o, 1024) for o in O_PB]
        RROW = f32(O_RROW, 2048)
        RROWB = bf(O_RROWB, 2048)
        OST = bf(O_OST, 2048)
        WOb = [bf(o, 2048, p=128) for o in O_WO]
        rowA = f32(O_ROWA, 2048, p=1)
        rowB = f32(O_ROWB, 2048, p=1)

        ones_t = en(nc.sbuf_tensor("ones_t", [128, 128], BF16))
        c128_t = en(nc.sbuf_tensor("c128_t", [128, 128], BF16))
        psw_t = en(nc.sbuf_tensor("psw_t", [128, 128], BF16))
        bqc = en(nc.sbuf_tensor("bqc", [128, DC], F32))
        bkc = en(nc.sbuf_tensor("bkc", [128, DC], F32))
        boc = en(nc.sbuf_tensor("boc", [128, DC], F32))
        bvrep = en(nc.sbuf_tensor("bvrep", [128, D], BF16))
        bvrow_s = en(nc.sbuf_tensor("bvrow_s", [1, D], BF16))
        negmh = en(nc.sbuf_tensor("negmh", [128, 1], F32))
        qk_s = en(nc.sbuf_tensor("qk_s", [1, 8], F32))
        qk_b = en(nc.sbuf_tensor("qk_b", [1, 2], BF16))
        km_s = en(nc.sbuf_tensor("km_s", [1, 2], F32))
        ci_s = en(nc.sbuf_tensor("ci_s", [1, 8], I32))
        cc_s = en(nc.sbuf_tensor("cc_s", [1, 32], F32))

        P0 = en(nc.psum_tensor("P0", [128, 1024], F32))
        P1 = en(nc.psum_tensor("P1", [128, 1024], F32))
        P2 = en(nc.psum_tensor("P2", [128, 1024], F32))
        P3 = en(nc.psum_tensor("P3", [128, 1024], F32))
        PS = [P0, P1, P2, P3]

        sXin = Sem(stk, nc, "sXin")
        sPre = Sem(stk, nc, "sPre")
        sRope = Sem(stk, nc, "sRope")
        sKMr = Sem(stk, nc, "sKMr")
        sXbk = Sem(stk, nc, "sXbk")
        sKP = [Sem(stk, nc, "sKP0"), Sem(stk, nc, "sKP1")]
        sVH = [Sem(stk, nc, "sVH0"), Sem(stk, nc, "sVH1")]
        sSp = Sem(stk, nc, "sSp")
        sSt = Sem(stk, nc, "sSt")
        sBv = Sem(stk, nc, "sBv")
        sWvt = Sem(stk, nc, "sWvt")
        sWg = [Sem(stk, nc, "sWg0"), Sem(stk, nc, "sWg1")]
        sWd = [Sem(stk, nc, "sWd0"), Sem(stk, nc, "sWd1")]
        sWsl = [Sem(stk, nc, "sWsl0"), Sem(stk, nc, "sWsl1")]
        sWo = [Sem(stk, nc, "sWo0"), Sem(stk, nc, "sWo1")]
        sGXv = [Sem(stk, nc, "sGXv0"), Sem(stk, nc, "sGXv1")]
        sGXk = [Sem(stk, nc, "sGXk0"), Sem(stk, nc, "sGXk1")]
        sGXm = Sem(stk, nc, "sGXm")
        sGXb = Sem(stk, nc, "sGXb")
        sGXd = Sem(stk, nc, "sGXd")
        sMM = Sem(stk, nc, "sMM")
        sVE = Sem(stk, nc, "sVE")
        sAC = Sem(stk, nc, "sAC")
        sXC = Sem(stk, nc, "sXC")

        rV = en(nc.gpsimd.register("rX"))
        rK = rV
        rM = rV

        sy = Eng("sync")
        tn = Eng("tensor")
        ve = Eng("vector", chain_sem=sVE)
        sc = Eng("scalar", chain_sem=sAC)
        gp = Eng("gpsimd")

        def mm(out, lhsT, rhs, first):
            def f(e, out=out, lhsT=lhsT, rhs=rhs, first=first):
                return e.matmul(out, lhsT, rhs, start=first, stop=True,
                                skip_group_check=(not first))
            return f

        for c in range(DC):
            sy.dma(lambda e, c=c: e.dma_start(
                out=X[:, c * T:(c + 1) * T],
                in_=dx[c * 128:(c + 1) * 128, :]), sXin)
        x_loaded = sXin.n
        sy.dma(lambda e: e.dma_start(out=bqc[:], in_=dbq[:, :]), sPre)
        sy.dma(lambda e: e.dma_start(out=bkc[:], in_=dbk[:, :]), sPre)
        sy.dma(lambda e: e.dma_start(out=boc[:], in_=dbo[:, :]), sPre)
        sy.dma(lambda e: e.dma_start(out=psw_t[:], in_=dpsw[:, :]), sPre)
        sy.dma(lambda e: e.dma_start(out=ci_s[:], in_=dci[:, :]), sPre)
        ci_loaded = sPre.n

        gp.dma(lambda e: e.dma_start(out=bvrow_s[:], in_=dbv[:, :]), sBv)
        bv_loaded = sBv.n

        ve.op(lambda e: e.memset(ones_t[:], 1.0))
        ve.op(lambda e: e.memset(c128_t[:], 1.0 / 128.0))
        ve.op(lambda e: e.memset(cc_s[:], 1.0))
        consts_ready = ve.opi(lambda e: e.memset(qk_s[:], 0.0), sVE)

        def emit_ln(xr_sem, xr_val):
            prev = [0, 0]
            for c in range(DC):
                p = c % 2
                ve.wait(xr_sem, xr_val)
                ve.wait(sMM, prev[p])
                v_bf = ve.opi(lambda e, p=p, c=c: e.tensor_copy(
                    XBFb[p][:, :], X[:, c * T:(c + 1) * T]), sVE)
                sc.wait(xr_sem, xr_val)
                sc.wait(sMM, prev[p])
                a_sq = sc.opi(lambda e, p=p, c=c: e.activation(
                    out=XSQb[p][:, :], in_=X[:, c * T:(c + 1) * T],
                    func=AF.Square), sAC)
                tn.wait(sVE, v_bf)
                tn.wait(sAC, a_sq)
                for sj in range(4):
                    tn.op(mm(PS[sj // 2][:, (sj % 2) * 512:(sj % 2) * 512 + 512],
                             ones_t[:, 0:128], XBFb[p][:, sj * 512:sj * 512 + 512],
                             first=(c == 0)))
                for sj in range(3):
                    tn.op(mm(PS[2 + sj // 2][:, (sj % 2) * 512:(sj % 2) * 512 + 512],
                             ones_t[:, 0:128], XSQb[p][:, sj * 512:sj * 512 + 512],
                             first=(c == 0)))
                prev[p] = tn.opi(mm(PS[3][:, 512:1024], ones_t[:, 0:128],
                                    XSQb[p][:, 1536:2048], first=(c == 0)), sMM)
            ve.wait(sMM, sMM.n)
            ve.op(lambda e: e.tensor_scalar(out=rowB[:, 0:1024], in0=P0[0:1, :],
                                            scalar1=1.0 / D, scalar2=None,
                                            op0=ALU.mult))
            v = ve.opi(lambda e: e.tensor_scalar(
                out=rowB[:, 1024:2048], in0=P1[0:1, :], scalar1=1.0 / D,
                scalar2=None, op0=ALU.mult), sVE)
            ve.wait(sVE, v)
            v = ve.opi(lambda e: e.tensor_mul(rowA[:, :], rowB[:, :],
                                              rowB[:, :]), sVE)
            ve.wait(sVE, v)
            ve.op(lambda e: e.scalar_tensor_tensor(
                out=rowA[:, 0:1024], in0=P2[0:1, :], scalar=1.0 / D,
                in1=rowA[:, 0:1024], op0=ALU.mult, op1=ALU.subtract))
            v = ve.opi(lambda e: e.scalar_tensor_tensor(
                out=rowA[:, 1024:2048], in0=P3[0:1, :], scalar=1.0 / D,
                in1=rowA[:, 1024:2048], op0=ALU.mult, op1=ALU.subtract), sVE)
            ve.wait(sVE, v)
            v = ve.opi(lambda e: e.tensor_scalar(
                out=rowA[:, :], in0=rowA[:, :], scalar1=EPS, scalar2=None,
                op0=ALU.add), sVE)
            ve.wait(sVE, v)
            v = ve.opi(lambda e: e.reciprocal(rowA[:, :], rowA[:, :]), sVE)
            sc.wait(sVE, v)
            a_s = sc.opi(lambda e: e.activation(out=rowA[:, :], in_=rowA[:, :],
                                                func=AF.Sqrt), sAC)
            ve.wait(sAC, a_s)
            v = ve.opi(lambda e: e.scalar_tensor_tensor(
                out=rowB[:, :], in0=rowB[:, :], scalar=-1.0, in1=rowA[:, :],
                op0=ALU.mult, op1=ALU.mult), sVE)
            ve.wait(sVE, v)
            ve.op(lambda e: e.tensor_copy(rowAb[:, :], rowA[:, :]))
            rows_done = ve.opi(lambda e: e.tensor_copy(rowBb[:, :], rowB[:, :]),
                               sVE)
            tn.wait(sVE, rows_done)
            for sj in range(4):
                tn.op(mm(PS[sj // 2][:, (sj % 2) * 512:(sj % 2) * 512 + 512],
                         ones_t[0:1, 0:128], rowAb[:, sj * 512:sj * 512 + 512],
                         first=True))
            for sj in range(3):
                tn.op(mm(PS[2 + sj // 2][:, (sj % 2) * 512:(sj % 2) * 512 + 512],
                         ones_t[0:1, 0:128], rowBb[:, sj * 512:sj * 512 + 512],
                         first=True))
            return tn.opi(mm(PS[3][:, 512:1024], ones_t[0:1, 0:128],
                             rowBb[:, 1536:2048], first=True), sMM)

        def emit_apply(bcast):
            ve.wait(sMM, bcast)
            last = 0
            for c in range(DC):
                for half in range(2):
                    v = ve.opi(lambda e, c=c, half=half: e.tensor_tensor(
                        RTMP[:, 0:1024],
                        X[:, c * T + half * 1024: c * T + half * 1024 + 1024],
                        PS[half][:, :], ALU.mult), sVE)
                    ve.wait(sVE, v)
                    last = ve.opi(lambda e, c=c, half=half: e.tensor_tensor(
                        XN[:, c * T + half * 1024: c * T + half * 1024 + 1024],
                        RTMP[:, 0:1024], PS[2 + half][:, :], ALU.add), sVE)
            return last

        def emit_ffn(xr_sem, xr_val, w_init_mm=0):
            bcast = emit_ln(xr_sem, xr_val)
            xn_done = emit_apply(bcast)
            carry_mm = w_init_mm
            for half in range(2):
                wfree = [carry_mm, carry_mm]
                silu_d = [sAC.n, sAC.n]
                up_d = [sVE.n, sVE.n]
                g_mm = u_mm = 0
                for hc in range(FC):
                    p = hc % 2
                    gp.wait(sMM, wfree[p])
                    gp.dma(lambda e, p=p, hc=hc: e.dma_start(
                        out=WGb[p][:].rearrange("p (a k) -> p a k", k=128),
                        in_=dwgT.rearrange("(a p) f -> p a f", p=128)
                        [:, :, hc * 128:(hc + 1) * 128]), sWg[p])
                    w_rdy = gp.dma(lambda e, p=p, hc=hc: e.dma_start(
                        out=WUb[p][:].rearrange("p (a k) -> p a k", k=128),
                        in_=dwuT.rearrange("(a p) f -> p a f", p=128)
                        [:, :, hc * 128:(hc + 1) * 128]), sWg[p])
                    tn.wait(sVE, xn_done)
                    tn.wait(sWg[p], w_rdy)
                    tn.wait(sAC, silu_d[p])
                    tn.wait(sVE, up_d[p])
                    for dc in range(DC):
                        for sj in range(2):
                            f = mm(PS[p][:, sj * 512:sj * 512 + 512],
                                   WGb[p][:, dc * 128:dc * 128 + 128],
                                   XN[:, dc * T + half * 1024 + sj * 512:
                                      dc * T + half * 1024 + sj * 512 + 512],
                                   first=(dc == 0))
                            if dc == DC - 1 and sj == 1:
                                g_mm = tn.opi(f, sMM)
                            else:
                                tn.op(f)
                    for dc in range(DC):
                        for sj in range(2):
                            f = mm(PS[2 + p][:, sj * 512:sj * 512 + 512],
                                   WUb[p][:, dc * 128:dc * 128 + 128],
                                   XN[:, dc * T + half * 1024 + sj * 512:
                                      dc * T + half * 1024 + sj * 512 + 512],
                                   first=(dc == 0))
                            if dc == DC - 1 and sj == 1:
                                u_mm = tn.opi(f, sMM)
                            else:
                                tn.op(f)
                    wfree[p] = u_mm
                    sc.wait(sMM, g_mm)
                    silu_d[p] = sc.opi(lambda e, p=p: e.activation(
                        out=SGb[p][:, :], in_=PS[p][:, :], func=AF.Silu,
                        scale=INV127), sAC)
                    ve.wait(sMM, u_mm)
                    v_up = ve.opi(lambda e, p=p: e.tensor_copy(
                        UPb[p][:, :], PS[2 + p][:, :]), sVE)
                    ve.wait(sAC, silu_d[p])
                    ve.wait(sVE, v_up)
                    up_d[p] = ve.opi(lambda e, p=p, hc=hc: e.tensor_tensor(
                        HALL[:, hc * 1024:(hc + 1) * 1024], SGb[p][:, :],
                        UPb[p][:, :], ALU.mult), sVE)
                h_done = sVE.n
                silu_all = sAC.n
                wd_free = [0, 0]
                ev_d = [sVE.n, sVE.n]
                d_mm = 0
                for ec in range(DC):
                    p = ec % 2
                    gp.wait(sMM, wd_free[p])
                    gp.wait(sVE, h_done)
                    wd_rdy = gp.dma(lambda e, p=p, ec=ec: e.dma_start(
                        out=WDb[p][:].rearrange("p (a k) -> p a k", k=128),
                        in_=dwdT.rearrange("(a p) f -> p a f", p=128)
                        [:, :, ec * 128:(ec + 1) * 128]), sWd[p])
                    tn.wait(sVE, h_done)
                    tn.wait(sAC, silu_all)
                    tn.wait(sWd[p], wd_rdy)
                    tn.wait(sVE, ev_d[p])
                    for hc in range(FC):
                        for sj in range(2):
                            f = mm(PS[p][:, sj * 512:sj * 512 + 512],
                                   WDb[p][:, hc * 128:hc * 128 + 128],
                                   HALL[:, hc * 1024 + sj * 512:
                                        hc * 1024 + sj * 512 + 512],
                                   first=(hc == 0))
                            if hc == FC - 1 and sj == 1:
                                d_mm = tn.opi(f, sMM)
                            else:
                                tn.op(f)
                    wd_free[p] = d_mm
                    carry_mm = d_mm
                    ve.wait(sMM, d_mm)
                    ev_d[p] = ve.opi(
                        lambda e, p=p, ec=ec, half=half: e.scalar_tensor_tensor(
                            out=X[:, ec * T + half * 1024:
                                  ec * T + half * 1024 + 1024],
                            in0=PS[p][:, :], scalar=SCL_FFN,
                            in1=X[:, ec * T + half * 1024:
                                  ec * T + half * 1024 + 1024],
                            op0=ALU.mult, op1=ALU.add), sVE)
            return sVE.n

        x1_done = emit_ffn(sXin, x_loaded)

        def store_X(ready_val):
            sy.wait(sVE, ready_val)
            for c in range(DC):
                sy.dma(lambda e, c=c: e.dma_start(
                    out=dy[c * 128:(c + 1) * 128, :],
                    in_=X[:, c * T:(c + 1) * T]), sSt)
            sy.wait(sSt, sSt.n)

        def finish():
            blk = en(nc.Block())
            blk.sync(sy.replay())
            blk.tensor(tn.replay())
            blk.vector(ve.replay())
            blk.scalar(sc.replay())
            blk.gpsimd(gp.replay())
            return nc

        if upto == "ffn1":
            store_X(x1_done)
            return finish()

        bcast2 = emit_ln(sVE, x1_done)
        xn2_done = emit_apply(bcast2)

        sy.wait(sVE, xn2_done)
        for c in range(DC):
            sy.dma(lambda e, c=c: e.dma_start(
                out=dXsp[c * 128:(c + 1) * 128, :],
                in_=X[:, c * T:(c + 1) * T]), sSp)
        x_spilled = sSp.n

        ve.wait(sBv, bv_loaded)
        tn.wait(sBv, bv_loaded)
        tn.wait(sVE, consts_ready)
        tn.wait(sVE, xn2_done)
        tn.op(mm(P0[:, 0:512], ones_t[0:1, 0:128], bvrow_s[0:1, 0:512], True))
        bv_mm = tn.opi(mm(P0[:, 512:1024], ones_t[0:1, 0:128],
                          bvrow_s[0:1, 512:1024], True), sMM)
        ve.wait(sMM, bv_mm)
        bvrep_done = ve.opi(lambda e: e.tensor_copy(bvrep[:, :], P0[:, :]), sVE)

        gp.wait(sMM, sMM.n)
        for dc in range(DC):
            gp.dma(lambda e, dc=dc: e.dma_start(
                out=WVT[:, dc * 1024:(dc + 1) * 1024],
                in_=dwvT[dc * 128:(dc + 1) * 128, :]), sWvt)
        wv_ready = sWvt.n

        ve.op(lambda e: e.memset(VOWb[0][:, :], 0.0))
        ve.op(lambda e: e.memset(VOWb[1][:, :], 0.0))
        ve.op(lambda e: e.memset(
            VOWb[0][:].rearrange("p (t j) -> p t j", j=128)[:, :, 64:65], 1.0))
        vow_ones = ve.op(lambda e: e.memset(
            VOWb[1][:].rearrange("p (t j) -> p t j", j=128)[:, :, 64:65], 1.0))

        gp.wait(sPre, ci_loaded)
        gp.op(lambda e: e.reg_load(rV, ci_s[0:1, 1:2]))
        v_evac = [0, 0]
        vow_free = [0, 0]
        for tt in range(16):
            p = tt % 2
            pv = PS[1 + p]
            tn.wait(sWvt, wv_ready)
            tn.wait(sVE, bvrep_done)
            tn.wait(sVE, v_evac[p])
            for dc in range(DC):
                for dj in range(2):
                    tn.op(mm(pv[:, dj * 512:dj * 512 + 512],
                             XN[:, dc * T + tt * 128: dc * T + tt * 128 + 128],
                             WVT[:, dc * 1024 + dj * 512:
                                 dc * 1024 + dj * 512 + 512],
                             first=(dc == 0)))
            tn.op(mm(pv[:, 0:512], c128_t[:, :], bvrep[:, 0:512], first=False))
            v_mm = tn.opi(mm(pv[:, 512:1024], c128_t[:, :], bvrep[:, 512:1024],
                             first=False), sMM)
            ve.wait(sMM, v_mm)
            ve.wait(sVE, vow_ones)
            ve.wait(sGXv[p], vow_free[p])
            v_evac[p] = ve.opi(lambda e, p=p, pv=pv: e.tensor_copy(
                VOWb[p][:].rearrange("p (t j) -> p t j", j=128)[:, :, 0:64],
                pv[:].rearrange("p (t j) -> p t j", j=64)), sVE)
            gp.wait(sVE, v_evac[p])
            vow_free[p] = gp.dma(lambda e, p=p: e.dma_start(
                out=bass.AP(dVx.tensor, rV,
                            [[128, 128], [S * 128, 16], [1, 128]]),
                in_=bass.AP(arh, O_VOW[p],
                            [[ARENA, 128], [128, 16], [1, 128]]),
                bounds_check=None), sGXv[p])
            if tt < 15:
                gp.op(lambda e: e.reg_add(rV, rV, 128 * 128))
        wvt_mm_done = sMM.n

        sy.wait(sMM, wvt_mm_done)
        for (d_t, s_t) in ((dcq, RCQ), (dsq, RSQ), (dck, RCK), (dsk, RSK)):
            sy.dma(lambda e, d_t=d_t, s_t=s_t: e.dma_start(
                out=s_t[:, :], in_=d_t[:, :]), sRope)
        ropes_ready = sRope.n

        def emit_qk(wT_d, bias_col, CT, STt, is_q, psin, export_cb=None):
            wfree = [psin, psin]
            psum_free = [sVE.n, sVE.n]
            slot_gx = [0, 0]
            slot_sq = [0, 0]
            qk_mm_v = [0] * DC
            sw_mm = nm_mm = 0

            def emit_proj(ecc):
                p = ecc % 2
                gp.wait(sMM, wfree[p])
                w_rdy = gp.dma(lambda e, p=p, ecc=ecc: e.dma_start(
                    out=WSLb[p][:].rearrange("p (a k) -> p a k", k=128),
                    in_=wT_d.rearrange("(a p) f -> p a f", p=128)
                    [:, :, ecc * 128:(ecc + 1) * 128]), sWsl[p])
                pq, pq2 = PS[2 * p], PS[2 * p + 1]
                tn.wait(sWsl[p], w_rdy)
                tn.wait(sRope, ropes_ready)
                tn.wait(sVE, psum_free[p])
                last = 0
                for dc in range(DC):
                    for sj in range(4):
                        pt = pq if sj < 2 else pq2
                        f = mm(pt[:, (sj % 2) * 512:(sj % 2) * 512 + 512],
                               WSLb[p][:, dc * 128:dc * 128 + 128],
                               XN[:, dc * T + sj * 512: dc * T + sj * 512 + 512],
                               first=(dc == 0))
                        if dc == DC - 1 and sj == 3:
                            last = tn.opi(f, sMM)
                        else:
                            tn.op(f)
                qk_mm_v[ecc] = last

            emit_proj(0)
            for ecc in range(DC):
                p = ecc % 2
                pq, pq2 = PS[2 * p], PS[2 * p + 1]
                ve.wait(sMM, qk_mm_v[ecc])
                ve.op(lambda e, pq=pq, ecc=ecc: e.tensor_scalar(
                    out=XBFb[0][:, 0:1024], in0=pq[:, :],
                    scalar1=bias_col[:, ecc:ecc + 1], scalar2=None,
                    op0=ALU.add))
                raw_done = ve.op(lambda e, pq2=pq2, ecc=ecc: e.tensor_scalar(
                    out=XBFb[0][:, 1024:2048], in0=pq2[:, :],
                    scalar1=bias_col[:, ecc:ecc + 1], scalar2=None,
                    op0=ALU.add))
                if ecc + 1 < DC:
                    emit_proj(ecc + 1)
                tn.wait(sVE, raw_done)
                for sj in range(4):
                    pt = pq if sj < 2 else pq2
                    f = mm(pt[:, (sj % 2) * 512:(sj % 2) * 512 + 512],
                           psw_t[:, :], XBFb[0][:, sj * 512:sj * 512 + 512],
                           first=True)
                    if sj == 3:
                        sw_mm = tn.opi(f, sMM)
                    else:
                        tn.op(f)
                ve.wait(sMM, sw_mm)
                ve.op(lambda e: e.tensor_tensor(
                    RTMP[:, :], XBFb[0][:, :], CT[:, :], ALU.mult))
                ve.op(lambda e, pq=pq: e.tensor_tensor(
                    RTM2[:, 0:1024], pq[:, :], STt[:, 0:1024], ALU.mult))
                v2 = ve.op(lambda e, pq2=pq2: e.tensor_tensor(
                    RTM2[:, 1024:2048], pq2[:, :], STt[:, 1024:2048],
                    ALU.mult))
                ve.wait(sVE, v2)
                if is_q:
                    dst = Q_all[:, ecc * T:(ecc + 1) * T]
                else:
                    dst = RAWK[p][:, :]
                    ve.wait(sGXk[0], slot_gx[p])
                    ve.wait(sAC, slot_sq[p])
                rope_done = ve.op(lambda e, dst=dst: e.tensor_tensor(
                    dst, RTMP[:, :], RTM2[:, :], ALU.add))
                if export_cb is not None:
                    ex_v = export_cb(ecc, rope_done)
                    if ex_v:
                        slot_gx[0] = slot_gx[1] = ex_v
                sc.wait(sVE, rope_done)
                sq_d = sc.op(lambda e, dst=dst: e.activation(
                    out=QSQ[:, :], in_=dst, func=AF.Square))
                if not is_q:
                    slot_sq[p] = sq_d
                tn.wait(sAC, sq_d)
                tn.wait(sVE, rope_done)
                for sj in range(4):
                    pt = pq if sj < 2 else pq2
                    f = mm(pt[:, (sj % 2) * 512:(sj % 2) * 512 + 512],
                           ones_t[:, 0:128], QSQ[:, sj * 512:sj * 512 + 512],
                           first=True)
                    if sj == 3:
                        nm_mm = tn.opi(f, sMM)
                    else:
                        tn.op(f)
                wfree[p] = nm_mm
                slot = 0 if is_q else 1
                ve.wait(sMM, nm_mm)
                v = ve.op(lambda e, pq=pq: e.reduce_max(
                    qk_s[0:1, 4:5], pq[0:1, :], axis=AXX))
                ve.wait(sVE, v)
                v = ve.op(lambda e, pq2=pq2: e.reduce_max(
                    qk_s[0:1, 5:6], pq2[0:1, :], axis=AXX))
                ve.wait(sVE, v)
                v = ve.op(lambda e: e.tensor_tensor(
                    qk_s[0:1, 4:5], qk_s[0:1, 4:5], qk_s[0:1, 5:6], ALU.max))
                ve.wait(sVE, v)
                v = ve.op(lambda e, slot=slot: e.tensor_tensor(
                    qk_s[0:1, slot:slot + 1], qk_s[0:1, slot:slot + 1],
                    qk_s[0:1, 4:5], ALU.max))
                psum_free[p] = v
            return sVE.n

        q_done = emit_qk(dwqT, bqc, RCQ, RSQ, True, psin=wvt_mm_done)

        gp.op(lambda e: e.reg_load(rK, ci_s[0:1, 0:1]))
        kx_first = [True]

        def k_export(ecc, rope_done):
            if ecc % 2 == 0:
                return 0
            gp.wait(sVE, rope_done)
            if not kx_first[0]:
                gp.op(lambda e: e.reg_add(rK, rK, 2 * 128 * S))
            kx_first[0] = False
            return gp.dma(lambda e: e.dma_start(
                out=bass.AP(dKx.tensor, rK,
                            [[S, 128], [128 * S, 2], [1, 2048]]),
                in_=bass.AP(arh, O_QRAW,
                            [[ARENA, 128], [2048, 2], [1, 2048]]),
                bounds_check=None), sGXk[0])

        kmax_done = emit_qk(dwkT, bkc, RCK, RSK, False, psin=sMM.n,
                            export_cb=k_export)

        gp.op(lambda e: e.reg_load(rM, ci_s[0:1, 2:3]))
        gp.wait(sVE, kmax_done)
        gp.dma(lambda e: e.dma_start(
            out=bass.AP(dKM.tensor, rM, [[1, 1], [1, 1]]),
            in_=qk_s[0:1, 1:2], bounds_check=None), sGXm)

        for s_ in (sGXv[0], sGXv[1], sGXk[0], sGXm):
            gp.wait(s_, s_.n)
        gp.dma(lambda e: e.dma_start(out=dccin[:, :], in_=cc_s[0:1, :]), sGXb)
        gp.wait(sGXb, sGXb.n)
        barrier = gp.opi(lambda e: e.collective_compute(
            "AllReduce", ALU.add, ins=[dccin[:, :]], outs=[dccout[:, :]],
            replica_groups=[list(range(NCORE))]), sXC)

        if upto == "qkv":
            gp.wait(sXC, barrier)
            for c in range(DC):
                gp.dma(lambda e, c=c: e.dma_start(
                    out=dy[c * 128:(c + 1) * 128, :],
                    in_=Q_all[:, c * T:(c + 1) * T]), sGXd)
            gp.wait(sGXd, sGXd.n)
            return finish()

        sy.wait(sXC, barrier)
        km_ready = sy.dma(lambda e: e.dma_start(out=km_s[:], in_=dKM[:, :]), sKMr)
        ve.wait(sKMr, km_ready)
        v = ve.opi(lambda e: e.reduce_max(qk_s[0:1, 6:7], km_s[0:1, 0:2],
                                          axis=AXX), sVE)
        ve.wait(sVE, v)
        v = ve.opi(lambda e: e.tensor_mul(qk_s[0:1, 7:8], qk_s[0:1, 0:1],
                                          qk_s[0:1, 6:7]), sVE)
        sc.wait(sVE, v)
        a_v = sc.opi(lambda e: e.activation(out=qk_s[0:1, 7:8],
                                            in_=qk_s[0:1, 7:8], func=AF.Sqrt),
                     sAC)
        ve.wait(sAC, a_v)
        v_qb = ve.opi(lambda e: e.tensor_copy(qk_b[0:1, 0:1], qk_s[0:1, 7:8]),
                      sVE)
        tn.wait(sVE, v_qb)
        tn.wait(sVE, kmax_done)
        mh_mm = tn.opi(mm(P0[:, 0:1], ones_t[0:1, 0:128], qk_b[0:1, 0:1],
                          True), sMM)
        ve.wait(sMM, mh_mm)
        negmh_done = ve.opi(lambda e: e.tensor_scalar(
            out=negmh[:, 0:1], in0=P0[:, 0:1], scalar1=-1.0, scalar2=None,
            op0=ALU.mult), sVE)

        if upto == "mhat":
            gp.wait(sVE, negmh_done)
            for c in range(DC):
                gp.dma(lambda e, c=c: e.dma_start(
                    out=dy[c * 128:(c + 1) * 128, :],
                    in_=Q_all[:, c * T:(c + 1) * T]), sGXd)
            gp.wait(sGXd, sGXd.n)
            return finish()

        ve.wait(sVE, negmh_done)
        ve.wait(sSp, x_spilled)
        ve.op(lambda e: e.memset(O_128[64:128, :], 0.0))
        ve.op(lambda e: e.memset(KZb[0][:, :], 0.0))
        kz_zeroed = ve.op(lambda e: e.memset(KZb[1][:, :], 0.0))
        LAG = 3
        kp_ready = [0, 0]
        kp_free = [0, 0]
        vh_ready = [0, 0]
        vh_free = [0, 0]
        pab_free = [0, 0, 0, 0]
        pbb_free = [0, 0, 0, 0]
        pv_free = negmh_done
        expA_prev = 0
        expB_prev = 0
        p01_init = negmh_done
        pending_epi = [None]

        def emit_epilogue_mms(hprev, rb_rdy):
            tn.wait(sVE, rb_rdy)
            rb = 0
            for qj in range(4):
                pdst = P2 if qj < 2 else P3
                sl = (qj % 2) * 512
                f = mm(pdst[:, sl:sl + 512], ones_t[64:65, 0:128],
                       RROWB[64:65, qj * 512:qj * 512 + 512], True)
                if qj == 3:
                    rb = tn.opi(f, sMM)
                else:
                    tn.op(f)
            ve.wait(sMM, rb)
            ve.op(lambda e, hprev=hprev: e.tensor_tensor(
                O_all[0:64, hprev * T: hprev * T + 1024],
                OST[0:64, 0:1024], P2[0:64, :], ALU.mult))
            o_ev = ve.op(lambda e, hprev=hprev: e.tensor_tensor(
                O_all[0:64, hprev * T + 1024: hprev * T + 2048],
                OST[0:64, 1024:2048], P3[0:64, :], ALU.mult))
            return o_ev

        for pr in range(DC):
            sy.wait(sXC, barrier)
            sy.wait(sVE, kz_zeroed)
            sy.wait(sMM, kp_free[0])
            sy.dma(lambda e, pr=pr: e.dma_start(
                out=KZb[0][0:64, :], in_=dKx[pr * 128:pr * 128 + 64, :]),
                sKP[0])
            kp_ready[0] = sy.dma(lambda e, pr=pr: e.dma_start(
                out=KZb[1][64:128, :],
                in_=dKx[pr * 128 + 64:pr * 128 + 128, :]), sKP[0])
            kp_ready[1] = kp_ready[0]
            for hh in range(2):
                h = 2 * pr + hh
                vb = h % 2
                base = 64 * hh
                sy.wait(sMM, vh_free[vb])
                vh_ready[vb] = sy.dma(lambda e, vb=vb, h=h: e.dma_start(
                    out=VHb[vb][:].rearrange("p (t j) -> p t j", j=128),
                    in_=dVx[h].rearrange("(t p) j -> p t j", p=128)), sVH[vb])
                KZ = KZb[hh]
                expA = [0] * 32
                expB = [0] * 32
                for tc in range(32):
                    tn.wait(sKP[0], kp_ready[0])
                    tn.wait(sVH[vb], vh_ready[vb])
                    tn.wait(sVE, p01_init)
                    tn.wait(sAC, expA[tc - 1] if tc > 0 else expA_prev)
                    tn.op(mm(P0[:, 0:512],
                             KZ[:, tc * 128:tc * 128 + 128],
                             Q_all[:, pr * T: pr * T + 512],
                             first=True))
                    scA = tn.opi(mm(P0[:, 512:1024],
                                    KZ[:, tc * 128:tc * 128 + 128],
                                    Q_all[:, pr * T + 512: pr * T + 1024],
                                    first=True), sMM)
                    tn.wait(sAC, expB[tc - 1] if tc > 0 else expB_prev)
                    tn.op(mm(P1[:, 0:512],
                             KZ[:, tc * 128:tc * 128 + 128],
                             Q_all[:, pr * T + 1024: pr * T + 1536],
                             first=True))
                    scB = tn.opi(mm(P1[:, 512:1024],
                                    KZ[:, tc * 128:tc * 128 + 128],
                                    Q_all[:, pr * T + 1536: pr * T + 2048],
                                    first=True), sMM)
                    sc.wait(sMM, scA)
                    sc.wait(sVE, negmh_done)
                    sc.wait(sMM, pab_free[tc % 4])
                    expA[tc] = sc.opi(lambda e, tc=tc: e.activation(
                        out=PAb[tc % 4][:, :], in_=P0[:, :], func=AF.Exp,
                        bias=negmh[:, 0:1]), sAC)
                    sc.wait(sMM, scB)
                    sc.wait(sMM, pbb_free[tc % 4])
                    expB[tc] = sc.opi(lambda e, tc=tc: e.activation(
                        out=PBb[tc % 4][:, :], in_=P1[:, :], func=AF.Exp,
                        bias=negmh[:, 0:1]), sAC)
                    if tc == LAG and pending_epi[0] is not None:
                        pv_free = pending_epi[0]()
                        pending_epi[0] = None
                    if tc >= LAG:
                        tv = tc - LAG
                        tn.wait(sAC, expA[tv])
                        if tv == 0:
                            tn.wait(sVE, pv_free)
                        tn.op(mm(P2[:, 0:512],
                                 VHb[vb][:, tv * 128:tv * 128 + 128],
                                 PAb[tv % 4][:, 0:512], first=(tv == 0)))
                        pab_free[tv % 4] = tn.opi(
                            mm(P2[:, 512:1024],
                               VHb[vb][:, tv * 128:tv * 128 + 128],
                               PAb[tv % 4][:, 512:1024], first=(tv == 0)), sMM)
                        tn.wait(sAC, expB[tv])
                        tn.op(mm(P3[:, 0:512],
                                 VHb[vb][:, tv * 128:tv * 128 + 128],
                                 PBb[tv % 4][:, 0:512], first=(tv == 0)))
                        pbb_free[tv % 4] = tn.opi(
                            mm(P3[:, 512:1024],
                               VHb[vb][:, tv * 128:tv * 128 + 128],
                               PBb[tv % 4][:, 512:1024], first=(tv == 0)), sMM)
                for tv in range(32 - LAG, 32):
                    tn.wait(sAC, expA[tv])
                    tn.op(mm(P2[:, 0:512], VHb[vb][:, tv * 128:tv * 128 + 128],
                             PAb[tv % 4][:, 0:512], first=(tv == 0)))
                    pab_free[tv % 4] = tn.opi(
                        mm(P2[:, 512:1024], VHb[vb][:, tv * 128:tv * 128 + 128],
                           PAb[tv % 4][:, 512:1024], first=(tv == 0)), sMM)
                    tn.wait(sAC, expB[tv])
                    tn.op(mm(P3[:, 0:512], VHb[vb][:, tv * 128:tv * 128 + 128],
                             PBb[tv % 4][:, 0:512], first=(tv == 0)))
                    pbb_free[tv % 4] = tn.opi(
                        mm(P3[:, 512:1024], VHb[vb][:, tv * 128:tv * 128 + 128],
                           PBb[tv % 4][:, 512:1024], first=(tv == 0)), sMM)
                pv_last = sMM.n
                kp_free[0] = pv_last
                vh_free[vb] = pv_last
                ve.wait(sMM, pv_last)
                ve.wait(sSp, x_spilled)
                ve.op(lambda e: e.reciprocal(RROW[64:65, 0:1024], P2[64:65, :]))
                ve.op(lambda e: e.reciprocal(RROW[64:65, 1024:2048],
                                             P3[64:65, :]))
                ve.op(lambda e: e.tensor_copy(RROWB[64:65, :], RROW[64:65, :]))
                ve.op(lambda e: e.tensor_copy(OST[0:64, 0:1024], P2[0:64, :]))
                rb_rdy = ve.op(lambda e: e.tensor_copy(OST[0:64, 1024:2048],
                                                       P3[0:64, :]))
                expA_prev = expA[31]
                expB_prev = expB[31]
                pending_epi[0] = (lambda h=h, rb_rdy=rb_rdy:
                                  emit_epilogue_mms(h, rb_rdy))
        pv_free = pending_epi[0]()
        pending_epi[0] = None
        attn_done = sVE.n
        vh_last_mm = sMM.n

        if upto in ("attn", "attn1"):
            gp.wait(sVE, attn_done)
            for hh2 in range(H):
                gp.dma(lambda e, hh2=hh2: e.dma_start(
                    out=dy[hh2 * 64:(hh2 + 1) * 64, :],
                    in_=O_all[0:64, hh2 * T:(hh2 + 1) * T]), sGXd)
            gp.wait(sGXd, sGXd.n)
            return finish()

        ve.op(lambda e: e.memset(WOb[0][64:128, :], 0.0))
        wo_z = ve.op(lambda e: e.memset(WOb[1][64:128, :], 0.0))
        gp.wait(sVE, wo_z)
        wo_free = [0, 0]
        ev_free = [0, 0, 0, 0]
        wo_last_mm = 0
        for ecc in range(DC):
            p = ecc % 2
            gp.wait(sXC, barrier)
            gp.wait(sMM, wo_free[p])
            if ecc < 2:
                gp.wait(sMM, vh_last_mm)
                gp.wait(sVE, attn_done)
            wo_rdy = gp.dma(lambda e, p=p, ecc=ecc: e.dma_start(
                out=WOb[p][0:64, :].rearrange("p (hh k) -> p hh k", k=128),
                in_=dwoT.rearrange("p (hh ee) -> p hh ee", ee=D)
                [:, :, ecc * 128:(ecc + 1) * 128]), sWo[p])
            tn.wait(sWo[p], wo_rdy)
            tn.wait(sVE, attn_done)
            for qj in range(4):
                pt = PS[2 * p + qj // 2]
                sl = (qj % 2) * 512
                tn.wait(sVE, ev_free[2 * p + qj // 2])
                w_mm = 0
                for hh in range(H):
                    f = mm(pt[:, sl:sl + 512],
                           WOb[p][0:128, hh * 128:hh * 128 + 128],
                           O_128[0:128,
                                 hh * T + qj * 512: hh * T + qj * 512 + 512],
                           first=(hh == 0))
                    if hh == H - 1:
                        w_mm = tn.opi(f, sMM)
                    else:
                        tn.op(f)
                wo_free[p] = w_mm
                wo_last_mm = w_mm
                ve.wait(sMM, w_mm)
                ev_free[2 * p + qj // 2] = ve.opi(
                    lambda e, ecc=ecc, qj=qj, pt=pt, sl=sl: e.tensor_scalar(
                        out=AOUT[:, ecc * T + qj * 512:
                                 ecc * T + qj * 512 + 512],
                        in0=pt[:, sl:sl + 512], scalar1=boc[:, ecc:ecc + 1],
                        scalar2=None, op0=ALU.add), sVE)
        aout_done = sVE.n

        sy.wait(sMM, wo_last_mm)
        sy.wait(sSp, x_spilled)
        for c in range(DC):
            sy.dma(lambda e, c=c: e.dma_start(
                out=X[:, c * T:(c + 1) * T],
                in_=dXsp[c * 128:(c + 1) * 128, :]), sXbk)
        x_back = sXbk.n
        ve.wait(sXbk, x_back)
        ve.wait(sVE, aout_done)
        x2_done = 0
        for c in range(DC):
            x2_done = ve.opi(lambda e, c=c: e.tensor_tensor(
                X[:, c * T:(c + 1) * T], X[:, c * T:(c + 1) * T],
                AOUT[:, c * T:(c + 1) * T], ALU.add), sVE)

        if upto == "wo":
            store_X(x2_done)
            return finish()

        x3_done = emit_ffn(sVE, x2_done, w_init_mm=wo_last_mm)
        store_X(x3_done)
        return finish()


def _rope_tables(pos0, scale):
    j = np.arange(32)
    f = (10000.0 ** (-j / 32.0)).astype(np.float64)
    t = (pos0 + np.arange(T)).astype(np.float64)
    ang = np.outer(f, t)
    c32 = np.cos(ang)
    s32 = np.sin(ang)
    Cq = np.zeros((64, T))
    Sq = np.zeros((64, T))
    Cq[0::2] = c32
    Cq[1::2] = c32
    Sq[0::2] = -s32
    Sq[1::2] = s32
    Cf = (np.tile(Cq, (2, 1)) * scale).astype(ml_dtypes.bfloat16)
    Sf = (np.tile(Sq, (2, 1)) * scale).astype(ml_dtypes.bfloat16)
    return Cf, Sf


def _pswap():
    P = np.zeros((128, 128), np.float32)
    for jj in range(64):
        P[2 * jj, 2 * jj + 1] = 1.0
        P[2 * jj + 1, 2 * jj] = 1.0
    return P.astype(ml_dtypes.bfloat16)


_NC_CACHE = {}


def _get_nc(upto):
    if upto not in _NC_CACHE:
        _NC_CACHE[upto] = build_nc(upto)
    return _NC_CACHE[upto]


def make_in_maps(L0_attn_x, L0_attn_Wq, L0_attn_bq, L0_attn_Wk, L0_attn_bk,
                 L0_attn_Wv, L0_attn_bv, L0_attn_Wo, L0_attn_bo,
                 L0_ffn_w_gate, L0_ffn_w_up, L0_ffn_w_down):
    xx = np.asarray(L0_attn_x, np.float32)
    wqT = np.ascontiguousarray(np.asarray(L0_attn_Wq, np.float32).T)
    wkT = np.ascontiguousarray(np.asarray(L0_attn_Wk, np.float32).T)
    wvT = np.ascontiguousarray(np.asarray(L0_attn_Wv, np.float32).T)
    wo = np.asarray(L0_attn_Wo, np.float32)
    woT = np.ascontiguousarray(
        wo.T.reshape(H, HD, D).transpose(1, 0, 2).reshape(HD, H * D))
    wgT = np.ascontiguousarray(np.asarray(L0_ffn_w_gate, np.int32).T)
    wuT = np.ascontiguousarray(np.asarray(L0_ffn_w_up, np.int32).T)
    wdT = np.ascontiguousarray(np.asarray(L0_ffn_w_down, np.int32).T)
    bq = np.ascontiguousarray(np.asarray(L0_attn_bq, np.float32).reshape(DC, 128).T)
    bk = np.ascontiguousarray(np.asarray(L0_attn_bk, np.float32).reshape(DC, 128).T)
    bo = np.ascontiguousarray(np.asarray(L0_attn_bo, np.float32).reshape(DC, 128).T)
    bvr = np.asarray(L0_attn_bv, np.float32).reshape(1, D)
    psw = _pswap()
    in_maps = []
    for c in range(NCORE):
        b, hf = c // 2, c % 2
        xs = np.ascontiguousarray(xx[b, hf * T:(hf + 1) * T, :].T)
        Cq, Sq = _rope_tables(hf * T, 0.125)
        Ck, Sk = _rope_tables(hf * T, 1.0)
        ci = np.zeros((1, 8), np.int32)
        ci[0, 0] = hf * T
        ci[0, 1] = hf * T * 128
        ci[0, 2] = hf
        in_maps.append({
            "x": xs, "wqT": wqT, "wkT": wkT, "wvT": wvT, "woT": woT,
            "wgT": wgT, "wuT": wuT, "wdT": wdT,
            "bq": bq, "bk": bk, "bo": bo, "bvrow": bvr,
            "ropeCq": Cq, "ropeSq": Sq, "ropeCk": Ck, "ropeSk": Sk,
            "pswapT": psw, "cinfo": ci,
        })
    return in_maps


def kernel(**inputs):
    in_maps = make_in_maps(**inputs)
    nc = _get_nc("full")
    res = run_bass_kernel_spmd(nc, in_maps, core_ids=list(range(NCORE)))
    out = np.zeros((4, S, D), np.float32)
    for c in range(NCORE):
        b, hf = c // 2, c % 2
        out[b, hf * T:(hf + 1) * T, :] = np.asarray(res.results[c]["y"]).T
    return out


# version 07
# speedup vs baseline: 1.0568x; 1.0568x over previous
import numpy as np
import ml_dtypes
import concourse.bass as bass
import concourse.mybir as mybir
from concourse.bass_utils import run_bass_kernel_spmd

F32 = mybir.dt.float32
BF16 = mybir.dt.bfloat16
I32 = mybir.dt.int32
AF = mybir.ActivationFunctionType
ALU = mybir.AluOpType
AXX = mybir.AxisListType.X

D = 1024
DC = 8
T = 2048
S = 4096
H = 16
HD = 64
FF = 3072
FC = 24
NCORE = 8
ARENA = 100352

SCL_FFN = float(np.float32(0.5) * np.float32(1.0 / 127.0) * np.float32(1.0 / 127.0))
INV127 = float(np.float32(1.0 / 127.0))
EPS = 1e-5

O_X = 0
O_XN = 32768
O_HALL = 49152
O_QALL = 49152
O_WVT = 65536
O_RCQ, O_RSQ, O_RCK, O_RSK = 65536, 67584, 69632, 71680
O_KZ = (65536, 69632)
O_WG = (73728, 74752)
O_WU = (75776, 76800)
O_SG = (77824, 78848)
O_UP = (79872, 80896)
O_WD = (73728, 76800)
O_QRAW = 73728
O_KRAW = 75776
O_VOW = (77824, 79872)
O_WSL0 = 96256
O_VH = (73728, 77824)
O_XBF = (81920, 83968)
O_XSQ = (86016, 88064)
O_RAB = 94208
O_RBB = 96256
O_RTMP = 90112
O_RTM2 = 92160
O_QSQ = 94208
O_PA = (81920, 82944, 83968, 84992)
O_PB = (86016, 87040, 88064, 89088)
O_RROW = 90112
O_RROWB = 94208
O_OST = 96256
O_WSL1 = 98304
O_WO = (96256, 98304)
O_ROWA = 86016
O_ROWB = 90112


class Sem:
    def __init__(self, stack, nc, name):
        self.h = stack.enter_context(nc.semaphore(name))
        self.n = 0


class Eng:

    def __init__(self, name, chain_sem=None):
        self.name = name
        self.ops = []
        self.seen = {}
        self.chain = chain_sem

    def wait(self, sem, v=None):
        if v is None:
            v = sem.n
        if v <= 0 or self.seen.get(id(sem), -1) >= v:
            return
        self.seen[id(sem)] = v
        self.ops.append(("w", sem.h, v))

    def op(self, fn):
        if self.chain is not None:
            self.wait(self.chain, self.chain.n)
            self.chain.n += 1
            self.ops.append(("i", fn, self.chain.h, 1))
            return self.chain.n
        self.ops.append(("o", fn))

    def opi(self, fn, sem, k=1):
        if self.chain is not None:
            assert sem is self.chain and k == 1
            return self.op(fn)
        sem.n += k
        self.ops.append(("i", fn, sem.h, k))
        return sem.n

    def dma(self, fn, sem):
        sem.n += 16
        self.ops.append(("i", fn, sem.h, 16))
        return sem.n

    def replay(self):
        ops = self.ops

        def f(e):
            for it in ops:
                if it[0] == "w":
                    e.wait_ge(it[1], it[2])
                elif it[0] == "o":
                    it[1](e)
                else:
                    it[1](e).then_inc(it[2], it[3])
        return f


def build_nc(upto="full"):
    from contextlib import ExitStack
    nc = bass.Bass("TRN2", target_bir_lowering=False, num_devices=NCORE)

    dx = nc.dram_tensor("x", [D, T], F32, kind="ExternalInput").ap()
    dwqT = nc.dram_tensor("wqT", [D, D], F32, kind="ExternalInput").ap()
    dwkT = nc.dram_tensor("wkT", [D, D], F32, kind="ExternalInput").ap()
    dwvT = nc.dram_tensor("wvT", [D, D], F32, kind="ExternalInput").ap()
    dwoT = nc.dram_tensor("woT", [HD, H * D], F32, kind="ExternalInput").ap()
    dwgT = nc.dram_tensor("wgT", [D, FF], I32, kind="ExternalInput").ap()
    dwuT = nc.dram_tensor("wuT", [D, FF], I32, kind="ExternalInput").ap()
    dwdT = nc.dram_tensor("wdT", [FF, D], I32, kind="ExternalInput").ap()
    dbq = nc.dram_tensor("bq", [128, DC], F32, kind="ExternalInput").ap()
    dbk = nc.dram_tensor("bk", [128, DC], F32, kind="ExternalInput").ap()
    dbo = nc.dram_tensor("bo", [128, DC], F32, kind="ExternalInput").ap()
    dbv = nc.dram_tensor("bvrow", [1, D], F32, kind="ExternalInput").ap()
    dcq = nc.dram_tensor("ropeCq", [128, T], BF16, kind="ExternalInput").ap()
    dsq = nc.dram_tensor("ropeSq", [128, T], BF16, kind="ExternalInput").ap()
    dck = nc.dram_tensor("ropeCk", [128, T], BF16, kind="ExternalInput").ap()
    dsk = nc.dram_tensor("ropeSk", [128, T], BF16, kind="ExternalInput").ap()
    dpsw = nc.dram_tensor("pswapT", [128, 128], BF16, kind="ExternalInput").ap()
    dci = nc.dram_tensor("cinfo", [1, 8], I32, kind="ExternalInput").ap()
    dy = nc.dram_tensor("y", [D, T], F32, kind="ExternalOutput").ap()

    dKx = nc.dram_tensor("Kx", [D, S], BF16, kind="Internal",
                         addr_space="Shared").ap()
    dVx = nc.dram_tensor("Vx", [H, S, 128], BF16, kind="Internal",
                         addr_space="Shared").ap()
    dKM = nc.dram_tensor("KMx", [1, 2], F32, kind="Internal",
                         addr_space="Shared").ap()
    dXsp = nc.dram_tensor("Xsp", [D, T], F32, kind="Internal").ap()
    dccin = nc.dram_tensor("ccin", [1, 32], F32, kind="Internal").ap()
    dccout = nc.dram_tensor("ccout", [1, 32], F32, kind="Internal").ap()

    with ExitStack() as stk:
        en = stk.enter_context
        arena = en(nc.sbuf_tensor("arena", [128, ARENA], BF16))
        arh = arena[:].tensor
        ar32 = arh.bitcast(F32)

        def bf(off, n, p=128):
            return arh[0:p, off:off + n]

        def f32(off, n32, p=128):
            assert off % 2 == 0
            return ar32[0:p, off // 2: off // 2 + n32]

        X = f32(O_X, 16384)
        O_all = bf(O_X, 32768, p=64)
        O_128 = bf(O_X, 32768, p=128)
        XN = bf(O_XN, 16384)
        AOUT = bf(O_XN, 16384)
        HALL = bf(O_HALL, 24576)
        Q_all = bf(O_QALL, 16384)
        WVT = bf(O_WVT, 8192)
        RCQ, RSQ, RCK, RSK = (bf(O_RCQ, 2048), bf(O_RSQ, 2048),
                              bf(O_RCK, 2048), bf(O_RSK, 2048))
        KZb = [bf(o, 4096) for o in O_KZ]
        WGb = [bf(o, 1024) for o in O_WG]
        WUb = [bf(o, 1024) for o in O_WU]
        SGb = [bf(o, 1024) for o in O_SG]
        UPb = [bf(o, 1024) for o in O_UP]
        WDb = [bf(o, 3072) for o in O_WD]
        RAWK = [bf(O_QRAW, 2048), bf(O_KRAW, 2048)]
        VOWb = [bf(o, 2048) for o in O_VOW]
        WSLb = [bf(O_WSL0, 1024), bf(O_WSL1, 1024)]
        VHb = [bf(o, 4096) for o in O_VH]
        XBFb = [bf(o, 2048) for o in O_XBF]
        XSQb = [bf(o, 2048) for o in O_XSQ]
        rowAb = bf(O_RAB, 2048, p=1)
        rowBb = bf(O_RBB, 2048, p=1)
        RTMP = bf(O_RTMP, 2048)
        RTM2 = bf(O_RTM2, 2048)
        QSQ = bf(O_QSQ, 2048)
        PAb = [bf(o, 1024) for o in O_PA]
        PBb = [bf(o, 1024) for o in O_PB]
        RROW = f32(O_RROW, 2048)
        RROWB = bf(O_RROWB, 2048)
        OST = bf(O_OST, 2048)
        RBC = bf(O_RROW, 2048, p=64)
        WOb = [bf(o, 2048, p=128) for o in O_WO]
        rowA = f32(O_ROWA, 2048, p=1)
        rowB = f32(O_ROWB, 2048, p=1)

        ones_t = en(nc.sbuf_tensor("ones_t", [128, 128], BF16))
        c128_t = en(nc.sbuf_tensor("c128_t", [128, 128], BF16))
        psw_t = en(nc.sbuf_tensor("psw_t", [128, 128], BF16))
        bqc = en(nc.sbuf_tensor("bqc", [128, DC], F32))
        bkc = en(nc.sbuf_tensor("bkc", [128, DC], F32))
        boc = en(nc.sbuf_tensor("boc", [128, DC], F32))
        bvrep = en(nc.sbuf_tensor("bvrep", [128, D], BF16))
        bvrow_s = en(nc.sbuf_tensor("bvrow_s", [1, D], BF16))
        negmh = en(nc.sbuf_tensor("negmh", [128, 1], F32))
        qk_s = en(nc.sbuf_tensor("qk_s", [1, 8], F32))
        qk_b = en(nc.sbuf_tensor("qk_b", [1, 2], BF16))
        km_s = en(nc.sbuf_tensor("km_s", [1, 2], F32))
        ci_s = en(nc.sbuf_tensor("ci_s", [1, 8], I32))
        cc_s = en(nc.sbuf_tensor("cc_s", [1, 32], F32))

        P0 = en(nc.psum_tensor("P0", [128, 1024], F32))
        P1 = en(nc.psum_tensor("P1", [128, 1024], F32))
        P2 = en(nc.psum_tensor("P2", [128, 1024], F32))
        P3 = en(nc.psum_tensor("P3", [128, 1024], F32))
        PS = [P0, P1, P2, P3]

        sXin = Sem(stk, nc, "sXin")
        sPre = Sem(stk, nc, "sPre")
        sRope = Sem(stk, nc, "sRope")
        sKMr = Sem(stk, nc, "sKMr")
        sXbk = Sem(stk, nc, "sXbk")
        sKP = [Sem(stk, nc, "sKP0"), Sem(stk, nc, "sKP1")]
        sVH = [Sem(stk, nc, "sVH0"), Sem(stk, nc, "sVH1")]
        sSp = Sem(stk, nc, "sSp")
        sSt = Sem(stk, nc, "sSt")
        sBv = Sem(stk, nc, "sBv")
        sWvt = Sem(stk, nc, "sWvt")
        sWg = [Sem(stk, nc, "sWg0"), Sem(stk, nc, "sWg1")]
        sWd = [Sem(stk, nc, "sWd0"), Sem(stk, nc, "sWd1")]
        sWsl = [Sem(stk, nc, "sWsl0"), Sem(stk, nc, "sWsl1")]
        sWo = [Sem(stk, nc, "sWo0"), Sem(stk, nc, "sWo1")]
        sGXv = [Sem(stk, nc, "sGXv0"), Sem(stk, nc, "sGXv1")]
        sGXk = [Sem(stk, nc, "sGXk0"), Sem(stk, nc, "sGXk1")]
        sGXm = Sem(stk, nc, "sGXm")
        sGXb = Sem(stk, nc, "sGXb")
        sGXd = Sem(stk, nc, "sGXd")
        sRbc = Sem(stk, nc, "sRbc")
        sMM = Sem(stk, nc, "sMM")
        sVE = Sem(stk, nc, "sVE")
        sAC = Sem(stk, nc, "sAC")
        sXC = Sem(stk, nc, "sXC")

        rV = en(nc.gpsimd.register("rX"))
        rK = rV
        rM = rV

        sy = Eng("sync")
        tn = Eng("tensor")
        ve = Eng("vector", chain_sem=sVE)
        sc = Eng("scalar", chain_sem=sAC)
        gp = Eng("gpsimd")

        def mm(out, lhsT, rhs, first):
            def f(e, out=out, lhsT=lhsT, rhs=rhs, first=first):
                return e.matmul(out, lhsT, rhs, start=first, stop=True,
                                skip_group_check=(not first))
            return f

        for c in range(DC):
            sy.dma(lambda e, c=c: e.dma_start(
                out=X[:, c * T:(c + 1) * T],
                in_=dx[c * 128:(c + 1) * 128, :]), sXin)
        x_loaded = sXin.n
        sy.dma(lambda e: e.dma_start(out=bqc[:], in_=dbq[:, :]), sPre)
        sy.dma(lambda e: e.dma_start(out=bkc[:], in_=dbk[:, :]), sPre)
        sy.dma(lambda e: e.dma_start(out=boc[:], in_=dbo[:, :]), sPre)
        sy.dma(lambda e: e.dma_start(out=psw_t[:], in_=dpsw[:, :]), sPre)
        sy.dma(lambda e: e.dma_start(out=ci_s[:], in_=dci[:, :]), sPre)
        ci_loaded = sPre.n

        gp.dma(lambda e: e.dma_start(out=bvrow_s[:], in_=dbv[:, :]), sBv)
        bv_loaded = sBv.n

        ve.op(lambda e: e.memset(ones_t[:], 1.0))
        ve.op(lambda e: e.memset(c128_t[:], 1.0 / 128.0))
        ve.op(lambda e: e.memset(cc_s[:], 1.0))
        consts_ready = ve.opi(lambda e: e.memset(qk_s[:], 0.0), sVE)

        def emit_ln(xr_sem, xr_val):
            prev = [0, 0]
            for c in range(DC):
                p = c % 2
                ve.wait(xr_sem, xr_val)
                ve.wait(sMM, prev[p])
                v_bf = ve.opi(lambda e, p=p, c=c: e.tensor_copy(
                    XBFb[p][:, :], X[:, c * T:(c + 1) * T]), sVE)
                sc.wait(xr_sem, xr_val)
                sc.wait(sMM, prev[p])
                a_sq = sc.opi(lambda e, p=p, c=c: e.activation(
                    out=XSQb[p][:, :], in_=X[:, c * T:(c + 1) * T],
                    func=AF.Square), sAC)
                tn.wait(sVE, v_bf)
                tn.wait(sAC, a_sq)
                for sj in range(4):
                    tn.op(mm(PS[sj // 2][:, (sj % 2) * 512:(sj % 2) * 512 + 512],
                             ones_t[:, 0:128], XBFb[p][:, sj * 512:sj * 512 + 512],
                             first=(c == 0)))
                for sj in range(3):
                    tn.op(mm(PS[2 + sj // 2][:, (sj % 2) * 512:(sj % 2) * 512 + 512],
                             ones_t[:, 0:128], XSQb[p][:, sj * 512:sj * 512 + 512],
                             first=(c == 0)))
                prev[p] = tn.opi(mm(PS[3][:, 512:1024], ones_t[:, 0:128],
                                    XSQb[p][:, 1536:2048], first=(c == 0)), sMM)
            ve.wait(sMM, sMM.n)
            ve.op(lambda e: e.tensor_scalar(out=rowB[:, 0:1024], in0=P0[0:1, :],
                                            scalar1=1.0 / D, scalar2=None,
                                            op0=ALU.mult))
            v = ve.opi(lambda e: e.tensor_scalar(
                out=rowB[:, 1024:2048], in0=P1[0:1, :], scalar1=1.0 / D,
                scalar2=None, op0=ALU.mult), sVE)
            ve.wait(sVE, v)
            v = ve.opi(lambda e: e.tensor_mul(rowA[:, :], rowB[:, :],
                                              rowB[:, :]), sVE)
            ve.wait(sVE, v)
            ve.op(lambda e: e.scalar_tensor_tensor(
                out=rowA[:, 0:1024], in0=P2[0:1, :], scalar=1.0 / D,
                in1=rowA[:, 0:1024], op0=ALU.mult, op1=ALU.subtract))
            v = ve.opi(lambda e: e.scalar_tensor_tensor(
                out=rowA[:, 1024:2048], in0=P3[0:1, :], scalar=1.0 / D,
                in1=rowA[:, 1024:2048], op0=ALU.mult, op1=ALU.subtract), sVE)
            ve.wait(sVE, v)
            v = ve.opi(lambda e: e.tensor_scalar(
                out=rowA[:, :], in0=rowA[:, :], scalar1=EPS, scalar2=None,
                op0=ALU.add), sVE)
            ve.wait(sVE, v)
            sc.wait(sVE, v)
            a_s = sc.opi(lambda e: e.activation(out=rowA[:, :], in_=rowA[:, :],
                                                func=AF.Ln), sAC)
            sc.wait(sAC, a_s)
            a_s = sc.opi(lambda e: e.activation(out=rowA[:, :], in_=rowA[:, :],
                                                func=AF.Exp, scale=-0.5), sAC)
            ve.wait(sAC, a_s)
            v = ve.opi(lambda e: e.scalar_tensor_tensor(
                out=rowB[:, :], in0=rowB[:, :], scalar=-1.0, in1=rowA[:, :],
                op0=ALU.mult, op1=ALU.mult), sVE)
            ve.wait(sVE, v)
            ve.op(lambda e: e.tensor_copy(rowAb[:, :], rowA[:, :]))
            rows_done = ve.opi(lambda e: e.tensor_copy(rowBb[:, :], rowB[:, :]),
                               sVE)
            tn.wait(sVE, rows_done)
            for sj in range(4):
                tn.op(mm(PS[sj // 2][:, (sj % 2) * 512:(sj % 2) * 512 + 512],
                         ones_t[0:1, 0:128], rowAb[:, sj * 512:sj * 512 + 512],
                         first=True))
            for sj in range(3):
                tn.op(mm(PS[2 + sj // 2][:, (sj % 2) * 512:(sj % 2) * 512 + 512],
                         ones_t[0:1, 0:128], rowBb[:, sj * 512:sj * 512 + 512],
                         first=True))
            return tn.opi(mm(PS[3][:, 512:1024], ones_t[0:1, 0:128],
                             rowBb[:, 1536:2048], first=True), sMM)

        def emit_apply(bcast):
            ve.wait(sMM, bcast)
            last = 0
            for c in range(DC):
                for half in range(2):
                    v = ve.opi(lambda e, c=c, half=half: e.tensor_tensor(
                        RTMP[:, 0:1024],
                        X[:, c * T + half * 1024: c * T + half * 1024 + 1024],
                        PS[half][:, :], ALU.mult), sVE)
                    ve.wait(sVE, v)
                    last = ve.opi(lambda e, c=c, half=half: e.tensor_tensor(
                        XN[:, c * T + half * 1024: c * T + half * 1024 + 1024],
                        RTMP[:, 0:1024], PS[2 + half][:, :], ALU.add), sVE)
            return last

        def emit_ffn(xr_sem, xr_val, w_init_mm=0):
            bcast = emit_ln(xr_sem, xr_val)
            xn_done = emit_apply(bcast)
            carry_mm = w_init_mm
            for half in range(2):
                wfree = [carry_mm, carry_mm]
                silu_d = [sAC.n, sAC.n]
                up_d = [sVE.n, sVE.n]
                g_mm = u_mm = 0
                for hc in range(FC):
                    p = hc % 2
                    gp.wait(sMM, wfree[p])
                    gp.dma(lambda e, p=p, hc=hc: e.dma_start(
                        out=WGb[p][:].rearrange("p (a k) -> p a k", k=128),
                        in_=dwgT.rearrange("(a p) f -> p a f", p=128)
                        [:, :, hc * 128:(hc + 1) * 128]), sWg[p])
                    w_rdy = gp.dma(lambda e, p=p, hc=hc: e.dma_start(
                        out=WUb[p][:].rearrange("p (a k) -> p a k", k=128),
                        in_=dwuT.rearrange("(a p) f -> p a f", p=128)
                        [:, :, hc * 128:(hc + 1) * 128]), sWg[p])
                    tn.wait(sVE, xn_done)
                    tn.wait(sWg[p], w_rdy)
                    tn.wait(sAC, silu_d[p])
                    tn.wait(sVE, up_d[p])
                    for dc in range(DC):
                        for sj in range(2):
                            f = mm(PS[p][:, sj * 512:sj * 512 + 512],
                                   WGb[p][:, dc * 128:dc * 128 + 128],
                                   XN[:, dc * T + half * 1024 + sj * 512:
                                      dc * T + half * 1024 + sj * 512 + 512],
                                   first=(dc == 0))
                            if dc == DC - 1 and sj == 1:
                                g_mm = tn.opi(f, sMM)
                            else:
                                tn.op(f)
                    for dc in range(DC):
                        for sj in range(2):
                            f = mm(PS[2 + p][:, sj * 512:sj * 512 + 512],
                                   WUb[p][:, dc * 128:dc * 128 + 128],
                                   XN[:, dc * T + half * 1024 + sj * 512:
                                      dc * T + half * 1024 + sj * 512 + 512],
                                   first=(dc == 0))
                            if dc == DC - 1 and sj == 1:
                                u_mm = tn.opi(f, sMM)
                            else:
                                tn.op(f)
                    wfree[p] = u_mm
                    sc.wait(sMM, g_mm)
                    silu_d[p] = sc.opi(lambda e, p=p: e.activation(
                        out=SGb[p][:, :], in_=PS[p][:, :], func=AF.Silu,
                        scale=INV127), sAC)
                    ve.wait(sMM, u_mm)
                    v_up = ve.opi(lambda e, p=p: e.tensor_copy(
                        UPb[p][:, :], PS[2 + p][:, :]), sVE)
                    ve.wait(sAC, silu_d[p])
                    ve.wait(sVE, v_up)
                    up_d[p] = ve.opi(lambda e, p=p, hc=hc: e.tensor_tensor(
                        HALL[:, hc * 1024:(hc + 1) * 1024], SGb[p][:, :],
                        UPb[p][:, :], ALU.mult), sVE)
                h_done = sVE.n
                silu_all = sAC.n
                wd_free = [0, 0]
                ev_d = [sVE.n, sVE.n]
                d_mm = 0
                for ec in range(DC):
                    p = ec % 2
                    gp.wait(sMM, wd_free[p])
                    gp.wait(sVE, h_done)
                    wd_rdy = gp.dma(lambda e, p=p, ec=ec: e.dma_start(
                        out=WDb[p][:].rearrange("p (a k) -> p a k", k=128),
                        in_=dwdT.rearrange("(a p) f -> p a f", p=128)
                        [:, :, ec * 128:(ec + 1) * 128]), sWd[p])
                    tn.wait(sVE, h_done)
                    tn.wait(sAC, silu_all)
                    tn.wait(sWd[p], wd_rdy)
                    tn.wait(sVE, ev_d[p])
                    for hc in range(FC):
                        for sj in range(2):
                            f = mm(PS[p][:, sj * 512:sj * 512 + 512],
                                   WDb[p][:, hc * 128:hc * 128 + 128],
                                   HALL[:, hc * 1024 + sj * 512:
                                        hc * 1024 + sj * 512 + 512],
                                   first=(hc == 0))
                            if hc == FC - 1 and sj == 1:
                                d_mm = tn.opi(f, sMM)
                            else:
                                tn.op(f)
                    wd_free[p] = d_mm
                    carry_mm = d_mm
                    ve.wait(sMM, d_mm)
                    ev_d[p] = ve.opi(
                        lambda e, p=p, ec=ec, half=half: e.scalar_tensor_tensor(
                            out=X[:, ec * T + half * 1024:
                                  ec * T + half * 1024 + 1024],
                            in0=PS[p][:, :], scalar=SCL_FFN,
                            in1=X[:, ec * T + half * 1024:
                                  ec * T + half * 1024 + 1024],
                            op0=ALU.mult, op1=ALU.add), sVE)
            return sVE.n

        x1_done = emit_ffn(sXin, x_loaded)

        def store_X(ready_val):
            sy.wait(sVE, ready_val)
            for c in range(DC):
                sy.dma(lambda e, c=c: e.dma_start(
                    out=dy[c * 128:(c + 1) * 128, :],
                    in_=X[:, c * T:(c + 1) * T]), sSt)
            sy.wait(sSt, sSt.n)

        def finish():
            blk = en(nc.Block())
            blk.sync(sy.replay())
            blk.tensor(tn.replay())
            blk.vector(ve.replay())
            blk.scalar(sc.replay())
            blk.gpsimd(gp.replay())
            return nc

        if upto == "ffn1":
            store_X(x1_done)
            return finish()

        bcast2 = emit_ln(sVE, x1_done)
        xn2_done = emit_apply(bcast2)

        sy.wait(sVE, xn2_done)
        for c in range(DC):
            sy.dma(lambda e, c=c: e.dma_start(
                out=dXsp[c * 128:(c + 1) * 128, :],
                in_=X[:, c * T:(c + 1) * T]), sSp)
        x_spilled = sSp.n

        ve.wait(sBv, bv_loaded)
        tn.wait(sBv, bv_loaded)
        tn.wait(sVE, consts_ready)
        tn.wait(sVE, xn2_done)
        tn.op(mm(P0[:, 0:512], ones_t[0:1, 0:128], bvrow_s[0:1, 0:512], True))
        bv_mm = tn.opi(mm(P0[:, 512:1024], ones_t[0:1, 0:128],
                          bvrow_s[0:1, 512:1024], True), sMM)
        ve.wait(sMM, bv_mm)
        bvrep_done = ve.opi(lambda e: e.tensor_copy(bvrep[:, :], P0[:, :]), sVE)

        gp.wait(sMM, sMM.n)
        for dc in range(DC):
            gp.dma(lambda e, dc=dc: e.dma_start(
                out=WVT[:, dc * 1024:(dc + 1) * 1024],
                in_=dwvT[dc * 128:(dc + 1) * 128, :]), sWvt)
        wv_ready = sWvt.n

        ve.op(lambda e: e.memset(VOWb[0][:, :], 0.0))
        ve.op(lambda e: e.memset(VOWb[1][:, :], 0.0))
        ve.op(lambda e: e.memset(
            VOWb[0][:].rearrange("p (t j) -> p t j", j=128)[:, :, 64:65], 1.0))
        vow_ones = ve.op(lambda e: e.memset(
            VOWb[1][:].rearrange("p (t j) -> p t j", j=128)[:, :, 64:65], 1.0))

        gp.wait(sPre, ci_loaded)
        gp.op(lambda e: e.reg_load(rV, ci_s[0:1, 1:2]))
        v_evac = [0, 0]
        vow_free = [0, 0]
        for tt in range(16):
            p = tt % 2
            pv = PS[1 + p]
            tn.wait(sWvt, wv_ready)
            tn.wait(sVE, bvrep_done)
            tn.wait(sVE, v_evac[p])
            for dc in range(DC):
                for dj in range(2):
                    tn.op(mm(pv[:, dj * 512:dj * 512 + 512],
                             XN[:, dc * T + tt * 128: dc * T + tt * 128 + 128],
                             WVT[:, dc * 1024 + dj * 512:
                                 dc * 1024 + dj * 512 + 512],
                             first=(dc == 0)))
            tn.op(mm(pv[:, 0:512], c128_t[:, :], bvrep[:, 0:512], first=False))
            v_mm = tn.opi(mm(pv[:, 512:1024], c128_t[:, :], bvrep[:, 512:1024],
                             first=False), sMM)
            ve.wait(sMM, v_mm)
            ve.wait(sVE, vow_ones)
            ve.wait(sGXv[p], vow_free[p])
            v_evac[p] = ve.opi(lambda e, p=p, pv=pv: e.tensor_copy(
                VOWb[p][:].rearrange("p (t j) -> p t j", j=128)[:, :, 0:64],
                pv[:].rearrange("p (t j) -> p t j", j=64)), sVE)
            gp.wait(sVE, v_evac[p])
            vow_free[p] = gp.dma(lambda e, p=p: e.dma_start(
                out=bass.AP(dVx.tensor, rV,
                            [[128, 128], [S * 128, 16], [1, 128]]),
                in_=bass.AP(arh, O_VOW[p],
                            [[ARENA, 128], [128, 16], [1, 128]]),
                bounds_check=None), sGXv[p])
            if tt < 15:
                gp.op(lambda e: e.reg_add(rV, rV, 128 * 128))
        wvt_mm_done = sMM.n

        sy.wait(sMM, wvt_mm_done)
        for (d_t, s_t) in ((dcq, RCQ), (dsq, RSQ), (dck, RCK), (dsk, RSK)):
            sy.dma(lambda e, d_t=d_t, s_t=s_t: e.dma_start(
                out=s_t[:, :], in_=d_t[:, :]), sRope)
        ropes_ready = sRope.n

        def emit_qk(wT_d, bias_col, CT, STt, is_q, psin, export_cb=None):
            wfree = [psin, psin]
            psum_free = [sVE.n, sVE.n]
            slot_gx = [0, 0]
            slot_sq = [0, 0]
            qk_mm_v = [0] * DC
            sw_mm = nm_mm = 0

            def emit_proj(ecc):
                p = ecc % 2
                gp.wait(sMM, wfree[p])
                w_rdy = gp.dma(lambda e, p=p, ecc=ecc: e.dma_start(
                    out=WSLb[p][:].rearrange("p (a k) -> p a k", k=128),
                    in_=wT_d.rearrange("(a p) f -> p a f", p=128)
                    [:, :, ecc * 128:(ecc + 1) * 128]), sWsl[p])
                pq, pq2 = PS[2 * p], PS[2 * p + 1]
                tn.wait(sWsl[p], w_rdy)
                tn.wait(sRope, ropes_ready)
                tn.wait(sVE, psum_free[p])
                last = 0
                for dc in range(DC):
                    for sj in range(4):
                        pt = pq if sj < 2 else pq2
                        f = mm(pt[:, (sj % 2) * 512:(sj % 2) * 512 + 512],
                               WSLb[p][:, dc * 128:dc * 128 + 128],
                               XN[:, dc * T + sj * 512: dc * T + sj * 512 + 512],
                               first=(dc == 0))
                        if dc == DC - 1 and sj == 3:
                            last = tn.opi(f, sMM)
                        else:
                            tn.op(f)
                qk_mm_v[ecc] = last

            emit_proj(0)
            for ecc in range(DC):
                p = ecc % 2
                pq, pq2 = PS[2 * p], PS[2 * p + 1]
                ve.wait(sMM, qk_mm_v[ecc])
                ve.op(lambda e, pq=pq, ecc=ecc: e.tensor_scalar(
                    out=XBFb[0][:, 0:1024], in0=pq[:, :],
                    scalar1=bias_col[:, ecc:ecc + 1], scalar2=None,
                    op0=ALU.add))
                raw_done = ve.op(lambda e, pq2=pq2, ecc=ecc: e.tensor_scalar(
                    out=XBFb[0][:, 1024:2048], in0=pq2[:, :],
                    scalar1=bias_col[:, ecc:ecc + 1], scalar2=None,
                    op0=ALU.add))
                if ecc + 1 < DC:
                    emit_proj(ecc + 1)
                tn.wait(sVE, raw_done)
                for sj in range(4):
                    pt = pq if sj < 2 else pq2
                    f = mm(pt[:, (sj % 2) * 512:(sj % 2) * 512 + 512],
                           psw_t[:, :], XBFb[0][:, sj * 512:sj * 512 + 512],
                           first=True)
                    if sj == 3:
                        sw_mm = tn.opi(f, sMM)
                    else:
                        tn.op(f)
                ve.wait(sMM, sw_mm)
                ve.op(lambda e: e.tensor_tensor(
                    RTMP[:, :], XBFb[0][:, :], CT[:, :], ALU.mult))
                ve.op(lambda e, pq=pq: e.tensor_tensor(
                    RTM2[:, 0:1024], pq[:, :], STt[:, 0:1024], ALU.mult))
                v2 = ve.op(lambda e, pq2=pq2: e.tensor_tensor(
                    RTM2[:, 1024:2048], pq2[:, :], STt[:, 1024:2048],
                    ALU.mult))
                ve.wait(sVE, v2)
                if is_q:
                    dst = Q_all[:, ecc * T:(ecc + 1) * T]
                else:
                    dst = RAWK[p][:, :]
                    ve.wait(sGXk[0], slot_gx[p])
                    ve.wait(sAC, slot_sq[p])
                rope_done = ve.op(lambda e, dst=dst: e.tensor_tensor(
                    dst, RTMP[:, :], RTM2[:, :], ALU.add))
                if export_cb is not None:
                    ex_v = export_cb(ecc, rope_done)
                    if ex_v:
                        slot_gx[0] = slot_gx[1] = ex_v
                sc.wait(sVE, rope_done)
                sq_d = sc.op(lambda e, dst=dst: e.activation(
                    out=QSQ[:, :], in_=dst, func=AF.Square))
                if not is_q:
                    slot_sq[p] = sq_d
                tn.wait(sAC, sq_d)
                tn.wait(sVE, rope_done)
                for sj in range(4):
                    pt = pq if sj < 2 else pq2
                    f = mm(pt[:, (sj % 2) * 512:(sj % 2) * 512 + 512],
                           ones_t[:, 0:128], QSQ[:, sj * 512:sj * 512 + 512],
                           first=True)
                    if sj == 3:
                        nm_mm = tn.opi(f, sMM)
                    else:
                        tn.op(f)
                wfree[p] = nm_mm
                slot = 0 if is_q else 1
                ve.wait(sMM, nm_mm)
                v = ve.op(lambda e, pq=pq: e.reduce_max(
                    qk_s[0:1, 4:5], pq[0:1, :], axis=AXX))
                ve.wait(sVE, v)
                v = ve.op(lambda e, pq2=pq2: e.reduce_max(
                    qk_s[0:1, 5:6], pq2[0:1, :], axis=AXX))
                ve.wait(sVE, v)
                v = ve.op(lambda e: e.tensor_tensor(
                    qk_s[0:1, 4:5], qk_s[0:1, 4:5], qk_s[0:1, 5:6], ALU.max))
                ve.wait(sVE, v)
                v = ve.op(lambda e, slot=slot: e.tensor_tensor(
                    qk_s[0:1, slot:slot + 1], qk_s[0:1, slot:slot + 1],
                    qk_s[0:1, 4:5], ALU.max))
                psum_free[p] = v
            return sVE.n

        q_done = emit_qk(dwqT, bqc, RCQ, RSQ, True, psin=wvt_mm_done)

        gp.op(lambda e: e.reg_load(rK, ci_s[0:1, 0:1]))
        kx_first = [True]

        def k_export(ecc, rope_done):
            if ecc % 2 == 0:
                return 0
            gp.wait(sVE, rope_done)
            if not kx_first[0]:
                gp.op(lambda e: e.reg_add(rK, rK, 2 * 128 * S))
            kx_first[0] = False
            return gp.dma(lambda e: e.dma_start(
                out=bass.AP(dKx.tensor, rK,
                            [[S, 128], [128 * S, 2], [1, 2048]]),
                in_=bass.AP(arh, O_QRAW,
                            [[ARENA, 128], [2048, 2], [1, 2048]]),
                bounds_check=None), sGXk[0])

        kmax_done = emit_qk(dwkT, bkc, RCK, RSK, False, psin=sMM.n,
                            export_cb=k_export)

        gp.op(lambda e: e.reg_load(rM, ci_s[0:1, 2:3]))
        gp.wait(sVE, kmax_done)
        gp.dma(lambda e: e.dma_start(
            out=bass.AP(dKM.tensor, rM, [[1, 1], [1, 1]]),
            in_=qk_s[0:1, 1:2], bounds_check=None), sGXm)

        for s_ in (sGXv[0], sGXv[1], sGXk[0], sGXm):
            gp.wait(s_, s_.n)
        gp.dma(lambda e: e.dma_start(out=dccin[:, :], in_=cc_s[0:1, :]), sGXb)
        gp.wait(sGXb, sGXb.n)
        barrier = gp.opi(lambda e: e.collective_compute(
            "AllReduce", ALU.add, ins=[dccin[:, :]], outs=[dccout[:, :]],
            replica_groups=[list(range(NCORE))]), sXC)

        if upto == "qkv":
            gp.wait(sXC, barrier)
            for c in range(DC):
                gp.dma(lambda e, c=c: e.dma_start(
                    out=dy[c * 128:(c + 1) * 128, :],
                    in_=Q_all[:, c * T:(c + 1) * T]), sGXd)
            gp.wait(sGXd, sGXd.n)
            return finish()

        sy.wait(sXC, barrier)
        km_ready = sy.dma(lambda e: e.dma_start(out=km_s[:], in_=dKM[:, :]), sKMr)
        ve.wait(sKMr, km_ready)
        v = ve.opi(lambda e: e.reduce_max(qk_s[0:1, 6:7], km_s[0:1, 0:2],
                                          axis=AXX), sVE)
        ve.wait(sVE, v)
        v = ve.opi(lambda e: e.tensor_mul(qk_s[0:1, 7:8], qk_s[0:1, 0:1],
                                          qk_s[0:1, 6:7]), sVE)
        sc.wait(sVE, v)
        a_v = sc.opi(lambda e: e.activation(out=qk_s[0:1, 7:8],
                                            in_=qk_s[0:1, 7:8], func=AF.Ln),
                     sAC)
        sc.wait(sAC, a_v)
        a_v = sc.opi(lambda e: e.activation(out=qk_s[0:1, 7:8],
                                            in_=qk_s[0:1, 7:8], func=AF.Exp,
                                            scale=0.5), sAC)
        ve.wait(sAC, a_v)
        v_qb = ve.opi(lambda e: e.tensor_copy(qk_b[0:1, 0:1], qk_s[0:1, 7:8]),
                      sVE)
        tn.wait(sVE, v_qb)
        tn.wait(sVE, kmax_done)
        mh_mm = tn.opi(mm(P0[:, 0:1], ones_t[0:1, 0:128], qk_b[0:1, 0:1],
                          True), sMM)
        ve.wait(sMM, mh_mm)
        negmh_done = ve.opi(lambda e: e.tensor_scalar(
            out=negmh[:, 0:1], in0=P0[:, 0:1], scalar1=-1.0, scalar2=None,
            op0=ALU.mult), sVE)

        if upto == "mhat":
            gp.wait(sVE, negmh_done)
            for c in range(DC):
                gp.dma(lambda e, c=c: e.dma_start(
                    out=dy[c * 128:(c + 1) * 128, :],
                    in_=Q_all[:, c * T:(c + 1) * T]), sGXd)
            gp.wait(sGXd, sGXd.n)
            return finish()

        ve.wait(sVE, negmh_done)
        ve.wait(sSp, x_spilled)
        ve.op(lambda e: e.memset(O_128[64:128, :], 0.0))
        ve.op(lambda e: e.memset(KZb[0][:, :], 0.0))
        kz_zeroed = ve.op(lambda e: e.memset(KZb[1][:, :], 0.0))
        LAG = 3
        kp_ready = [0, 0]
        kp_free = [0, 0]
        vh_ready = [0, 0]
        vh_free = [0, 0]
        pab_free = [0, 0, 0, 0]
        pbb_free = [0, 0, 0, 0]
        pv_free = negmh_done
        expA_prev = 0
        expB_prev = 0
        p01_init = negmh_done

        for pr in range(DC):
            sy.wait(sXC, barrier)
            sy.wait(sVE, kz_zeroed)
            sy.wait(sMM, kp_free[0])
            sy.dma(lambda e, pr=pr: e.dma_start(
                out=KZb[0][0:64, :], in_=dKx[pr * 128:pr * 128 + 64, :]),
                sKP[0])
            kp_ready[0] = sy.dma(lambda e, pr=pr: e.dma_start(
                out=KZb[1][64:128, :],
                in_=dKx[pr * 128 + 64:pr * 128 + 128, :]), sKP[0])
            kp_ready[1] = kp_ready[0]
            for hh in range(2):
                h = 2 * pr + hh
                vb = h % 2
                base = 64 * hh
                sy.wait(sMM, vh_free[vb])
                vh_ready[vb] = sy.dma(lambda e, vb=vb, h=h: e.dma_start(
                    out=VHb[vb][:].rearrange("p (t j) -> p t j", j=128),
                    in_=dVx[h].rearrange("(t p) j -> p t j", p=128)), sVH[vb])
                KZ = KZb[hh]
                expA = [0] * 32
                expB = [0] * 32
                for tc in range(32):
                    tn.wait(sKP[0], kp_ready[0])
                    tn.wait(sVH[vb], vh_ready[vb])
                    tn.wait(sVE, p01_init)
                    tn.wait(sAC, expA[tc - 1] if tc > 0 else expA_prev)
                    tn.op(mm(P0[:, 0:512],
                             KZ[:, tc * 128:tc * 128 + 128],
                             Q_all[:, pr * T: pr * T + 512],
                             first=True))
                    scA = tn.opi(mm(P0[:, 512:1024],
                                    KZ[:, tc * 128:tc * 128 + 128],
                                    Q_all[:, pr * T + 512: pr * T + 1024],
                                    first=True), sMM)
                    tn.wait(sAC, expB[tc - 1] if tc > 0 else expB_prev)
                    tn.op(mm(P1[:, 0:512],
                             KZ[:, tc * 128:tc * 128 + 128],
                             Q_all[:, pr * T + 1024: pr * T + 1536],
                             first=True))
                    scB = tn.opi(mm(P1[:, 512:1024],
                                    KZ[:, tc * 128:tc * 128 + 128],
                                    Q_all[:, pr * T + 1536: pr * T + 2048],
                                    first=True), sMM)
                    sc.wait(sMM, scA)
                    sc.wait(sVE, negmh_done)
                    sc.wait(sMM, pab_free[tc % 4])
                    expA[tc] = sc.opi(lambda e, tc=tc: e.activation(
                        out=PAb[tc % 4][:, :], in_=P0[:, :], func=AF.Exp,
                        bias=negmh[:, 0:1]), sAC)
                    sc.wait(sMM, scB)
                    sc.wait(sMM, pbb_free[tc % 4])
                    expB[tc] = sc.opi(lambda e, tc=tc: e.activation(
                        out=PBb[tc % 4][:, :], in_=P1[:, :], func=AF.Exp,
                        bias=negmh[:, 0:1]), sAC)
                    if tc >= LAG:
                        tv = tc - LAG
                        tn.wait(sAC, expA[tv])
                        if tv == 0:
                            tn.wait(sVE, pv_free)
                        tn.op(mm(P2[:, 0:512],
                                 VHb[vb][:, tv * 128:tv * 128 + 128],
                                 PAb[tv % 4][:, 0:512], first=(tv == 0)))
                        pab_free[tv % 4] = tn.opi(
                            mm(P2[:, 512:1024],
                               VHb[vb][:, tv * 128:tv * 128 + 128],
                               PAb[tv % 4][:, 512:1024], first=(tv == 0)), sMM)
                        tn.wait(sAC, expB[tv])
                        tn.op(mm(P3[:, 0:512],
                                 VHb[vb][:, tv * 128:tv * 128 + 128],
                                 PBb[tv % 4][:, 0:512], first=(tv == 0)))
                        pbb_free[tv % 4] = tn.opi(
                            mm(P3[:, 512:1024],
                               VHb[vb][:, tv * 128:tv * 128 + 128],
                               PBb[tv % 4][:, 512:1024], first=(tv == 0)), sMM)
                for tv in range(32 - LAG, 32):
                    tn.wait(sAC, expA[tv])
                    tn.op(mm(P2[:, 0:512], VHb[vb][:, tv * 128:tv * 128 + 128],
                             PAb[tv % 4][:, 0:512], first=(tv == 0)))
                    pab_free[tv % 4] = tn.opi(
                        mm(P2[:, 512:1024], VHb[vb][:, tv * 128:tv * 128 + 128],
                           PAb[tv % 4][:, 512:1024], first=(tv == 0)), sMM)
                    tn.wait(sAC, expB[tv])
                    tn.op(mm(P3[:, 0:512], VHb[vb][:, tv * 128:tv * 128 + 128],
                             PBb[tv % 4][:, 0:512], first=(tv == 0)))
                    pbb_free[tv % 4] = tn.opi(
                        mm(P3[:, 512:1024], VHb[vb][:, tv * 128:tv * 128 + 128],
                           PBb[tv % 4][:, 512:1024], first=(tv == 0)), sMM)
                pv_last = sMM.n
                kp_free[0] = pv_last
                vh_free[vb] = pv_last
                ve.wait(sMM, pv_last)
                ve.wait(sSp, x_spilled)
                ve.op(lambda e: e.tensor_copy(OST[0:64, 0:1024], P2[0:64, :]))
                ve.op(lambda e: e.tensor_copy(OST[0:64, 1024:2048],
                                              P3[0:64, :]))
                ve.op(lambda e: e.tensor_copy(RROW[64:65, 0:1024],
                                              P2[64:65, :]))
                pv_free = ve.op(lambda e: e.tensor_copy(
                    RROW[64:65, 1024:2048], P3[64:65, :]))
                ve.op(lambda e: e.reciprocal(RROW[64:65, :], RROW[64:65, :]))
                rb_rdy = ve.op(lambda e: e.tensor_copy(RROWB[64:65, :],
                                                       RROW[64:65, :]))
                gp.wait(sVE, rb_rdy)
                rbc_rdy = gp.dma(lambda e: e.dma_start(
                    out=RBC[:, :],
                    in_=bass.AP(arh, 64 * ARENA + O_RROWB,
                                [[ARENA, 1], [0, 64], [1, 2048]])), sRbc)
                ve.wait(sRbc, rbc_rdy)
                ve.op(lambda e, h=h: e.tensor_tensor(
                    O_all[0:64, h * T: h * T + 1024],
                    OST[0:64, 0:1024], RBC[:, 0:1024], ALU.mult))
                ve.op(lambda e, h=h: e.tensor_tensor(
                    O_all[0:64, h * T + 1024: h * T + 2048],
                    OST[0:64, 1024:2048], RBC[:, 1024:2048], ALU.mult))
                expA_prev = expA[31]
                expB_prev = expB[31]
        attn_done = sVE.n
        vh_last_mm = sMM.n

        if upto in ("attn", "attn1"):
            gp.wait(sVE, attn_done)
            for hh2 in range(H):
                gp.dma(lambda e, hh2=hh2: e.dma_start(
                    out=dy[hh2 * 64:(hh2 + 1) * 64, :],
                    in_=O_all[0:64, hh2 * T:(hh2 + 1) * T]), sGXd)
            gp.wait(sGXd, sGXd.n)
            return finish()

        ve.op(lambda e: e.memset(WOb[0][64:128, :], 0.0))
        wo_z = ve.op(lambda e: e.memset(WOb[1][64:128, :], 0.0))
        gp.wait(sVE, wo_z)
        wo_free = [0, 0]
        ev_free = [0, 0, 0, 0]
        wo_last_mm = 0
        for ecc in range(DC):
            p = ecc % 2
            gp.wait(sXC, barrier)
            gp.wait(sMM, wo_free[p])
            if ecc < 2:
                gp.wait(sMM, vh_last_mm)
                gp.wait(sVE, attn_done)
            wo_rdy = gp.dma(lambda e, p=p, ecc=ecc: e.dma_start(
                out=WOb[p][0:64, :].rearrange("p (hh k) -> p hh k", k=128),
                in_=dwoT.rearrange("p (hh ee) -> p hh ee", ee=D)
                [:, :, ecc * 128:(ecc + 1) * 128]), sWo[p])
            tn.wait(sWo[p], wo_rdy)
            tn.wait(sVE, attn_done)
            for qj in range(4):
                pt = PS[2 * p + qj // 2]
                sl = (qj % 2) * 512
                tn.wait(sVE, ev_free[2 * p + qj // 2])
                w_mm = 0
                for hh in range(H):
                    f = mm(pt[:, sl:sl + 512],
                           WOb[p][0:128, hh * 128:hh * 128 + 128],
                           O_128[0:128,
                                 hh * T + qj * 512: hh * T + qj * 512 + 512],
                           first=(hh == 0))
                    if hh == H - 1:
                        w_mm = tn.opi(f, sMM)
                    else:
                        tn.op(f)
                wo_free[p] = w_mm
                wo_last_mm = w_mm
                ve.wait(sMM, w_mm)
                ev_free[2 * p + qj // 2] = ve.opi(
                    lambda e, ecc=ecc, qj=qj, pt=pt, sl=sl: e.tensor_scalar(
                        out=AOUT[:, ecc * T + qj * 512:
                                 ecc * T + qj * 512 + 512],
                        in0=pt[:, sl:sl + 512], scalar1=boc[:, ecc:ecc + 1],
                        scalar2=None, op0=ALU.add), sVE)
        aout_done = sVE.n

        sy.wait(sMM, wo_last_mm)
        sy.wait(sSp, x_spilled)
        for c in range(DC):
            sy.dma(lambda e, c=c: e.dma_start(
                out=X[:, c * T:(c + 1) * T],
                in_=dXsp[c * 128:(c + 1) * 128, :]), sXbk)
        x_back = sXbk.n
        ve.wait(sXbk, x_back)
        ve.wait(sVE, aout_done)
        x2_done = 0
        for c in range(DC):
            x2_done = ve.opi(lambda e, c=c: e.tensor_tensor(
                X[:, c * T:(c + 1) * T], X[:, c * T:(c + 1) * T],
                AOUT[:, c * T:(c + 1) * T], ALU.add), sVE)

        if upto == "wo":
            store_X(x2_done)
            return finish()

        x3_done = emit_ffn(sVE, x2_done, w_init_mm=wo_last_mm)
        store_X(x3_done)
        return finish()


def _rope_tables(pos0, scale):
    j = np.arange(32)
    f = (10000.0 ** (-j / 32.0)).astype(np.float64)
    t = (pos0 + np.arange(T)).astype(np.float64)
    ang = np.outer(f, t)
    c32 = np.cos(ang)
    s32 = np.sin(ang)
    Cq = np.zeros((64, T))
    Sq = np.zeros((64, T))
    Cq[0::2] = c32
    Cq[1::2] = c32
    Sq[0::2] = -s32
    Sq[1::2] = s32
    Cf = (np.tile(Cq, (2, 1)) * scale).astype(ml_dtypes.bfloat16)
    Sf = (np.tile(Sq, (2, 1)) * scale).astype(ml_dtypes.bfloat16)
    return Cf, Sf


def _pswap():
    P = np.zeros((128, 128), np.float32)
    for jj in range(64):
        P[2 * jj, 2 * jj + 1] = 1.0
        P[2 * jj + 1, 2 * jj] = 1.0
    return P.astype(ml_dtypes.bfloat16)


_NC_CACHE = {}


def _get_nc(upto):
    if upto not in _NC_CACHE:
        _NC_CACHE[upto] = build_nc(upto)
    return _NC_CACHE[upto]


def make_in_maps(L0_attn_x, L0_attn_Wq, L0_attn_bq, L0_attn_Wk, L0_attn_bk,
                 L0_attn_Wv, L0_attn_bv, L0_attn_Wo, L0_attn_bo,
                 L0_ffn_w_gate, L0_ffn_w_up, L0_ffn_w_down):
    xx = np.asarray(L0_attn_x, np.float32)
    wqT = np.ascontiguousarray(np.asarray(L0_attn_Wq, np.float32).T)
    wkT = np.ascontiguousarray(np.asarray(L0_attn_Wk, np.float32).T)
    wvT = np.ascontiguousarray(np.asarray(L0_attn_Wv, np.float32).T)
    wo = np.asarray(L0_attn_Wo, np.float32)
    woT = np.ascontiguousarray(
        wo.T.reshape(H, HD, D).transpose(1, 0, 2).reshape(HD, H * D))
    wgT = np.ascontiguousarray(np.asarray(L0_ffn_w_gate, np.int32).T)
    wuT = np.ascontiguousarray(np.asarray(L0_ffn_w_up, np.int32).T)
    wdT = np.ascontiguousarray(np.asarray(L0_ffn_w_down, np.int32).T)
    bq = np.ascontiguousarray(np.asarray(L0_attn_bq, np.float32).reshape(DC, 128).T)
    bk = np.ascontiguousarray(np.asarray(L0_attn_bk, np.float32).reshape(DC, 128).T)
    bo = np.ascontiguousarray(np.asarray(L0_attn_bo, np.float32).reshape(DC, 128).T)
    bvr = np.asarray(L0_attn_bv, np.float32).reshape(1, D)
    psw = _pswap()
    in_maps = []
    for c in range(NCORE):
        b, hf = c // 2, c % 2
        xs = np.ascontiguousarray(xx[b, hf * T:(hf + 1) * T, :].T)
        Cq, Sq = _rope_tables(hf * T, 0.125)
        Ck, Sk = _rope_tables(hf * T, 1.0)
        ci = np.zeros((1, 8), np.int32)
        ci[0, 0] = hf * T
        ci[0, 1] = hf * T * 128
        ci[0, 2] = hf
        in_maps.append({
            "x": xs, "wqT": wqT, "wkT": wkT, "wvT": wvT, "woT": woT,
            "wgT": wgT, "wuT": wuT, "wdT": wdT,
            "bq": bq, "bk": bk, "bo": bo, "bvrow": bvr,
            "ropeCq": Cq, "ropeSq": Sq, "ropeCk": Ck, "ropeSk": Sk,
            "pswapT": psw, "cinfo": ci,
        })
    return in_maps


def kernel(**inputs):
    in_maps = make_in_maps(**inputs)
    nc = _get_nc("full")
    res = run_bass_kernel_spmd(nc, in_maps, core_ids=list(range(NCORE)))
    out = np.zeros((4, S, D), np.float32)
    for c in range(NCORE):
        b, hf = c // 2, c % 2
        out[b, hf * T:(hf + 1) * T, :] = np.asarray(res.results[c]["y"]).T
    return out


# version 08
# speedup vs baseline: 1.0710x; 1.0134x over previous
import numpy as np
import ml_dtypes
import concourse.bass as bass
import concourse.mybir as mybir
from concourse.bass_utils import run_bass_kernel_spmd

F32 = mybir.dt.float32
BF16 = mybir.dt.bfloat16
I32 = mybir.dt.int32
AF = mybir.ActivationFunctionType
ALU = mybir.AluOpType
AXX = mybir.AxisListType.X

D = 1024
DC = 8
T = 2048
S = 4096
H = 16
HD = 64
FF = 3072
FC = 24
NCORE = 8
ARENA = 100352

SCL_FFN = float(np.float32(0.5) * np.float32(1.0 / 127.0) * np.float32(1.0 / 127.0))
INV127 = float(np.float32(1.0 / 127.0))
EPS = 1e-5

O_X = 0
O_XN = 32768
O_HALL = 49152
O_QALL = 49152
O_WVT = 65536
O_RCQ, O_RSQ, O_RCK, O_RSK = 65536, 67584, 69632, 71680
O_KZ = (65536, 69632)
O_WG = (73728, 74752)
O_WU = (75776, 76800)
O_SG = (77824, 78848)
O_UP = (79872, 80896)
O_WD = (73728, 76800)
O_QRAW = 73728
O_KRAW = 75776
O_VOW = (77824, 79872)
O_WSL0 = 96256
O_VH = (73728, 77824)
O_XBF = (81920, 83968)
O_XSQ = (86016, 88064)
O_RAB = 94208
O_RBB = 96256
O_RTMP = 90112
O_RTM2 = 92160
O_QSQ = 94208
O_PA = (81920, 82944, 83968, 84992)
O_PB = (86016, 87040, 88064, 89088)
O_RROW = 90112
O_RROWB = 94208
O_OST = 96256
O_WSL1 = 98304
O_WO = (96256, 98304)
O_ROWA = 86016
O_ROWB = 90112


class Sem:
    def __init__(self, stack, nc, name):
        self.h = stack.enter_context(nc.semaphore(name))
        self.n = 0


class Eng:

    def __init__(self, name, chain_sem=None):
        self.name = name
        self.ops = []
        self.seen = {}
        self.chain = chain_sem

    def wait(self, sem, v=None):
        if v is None:
            v = sem.n
        if v <= 0 or self.seen.get(id(sem), -1) >= v:
            return
        self.seen[id(sem)] = v
        self.ops.append(("w", sem.h, v))

    def op(self, fn):
        if self.chain is not None:
            self.wait(self.chain, self.chain.n)
            self.chain.n += 1
            self.ops.append(("i", fn, self.chain.h, 1))
            return self.chain.n
        self.ops.append(("o", fn))

    def opi(self, fn, sem, k=1):
        if self.chain is not None:
            assert sem is self.chain and k == 1
            return self.op(fn)
        sem.n += k
        self.ops.append(("i", fn, sem.h, k))
        return sem.n

    def dma(self, fn, sem):
        sem.n += 16
        self.ops.append(("i", fn, sem.h, 16))
        return sem.n

    def replay(self):
        ops = self.ops

        def f(e):
            for it in ops:
                if it[0] == "w":
                    e.wait_ge(it[1], it[2])
                elif it[0] == "o":
                    it[1](e)
                else:
                    it[1](e).then_inc(it[2], it[3])
        return f


def build_nc(upto="full"):
    from contextlib import ExitStack
    nc = bass.Bass("TRN2", target_bir_lowering=False, num_devices=NCORE)

    dx = nc.dram_tensor("x", [D, T], F32, kind="ExternalInput").ap()
    dwqT = nc.dram_tensor("wqT", [D, D], F32, kind="ExternalInput").ap()
    dwkT = nc.dram_tensor("wkT", [D, D], F32, kind="ExternalInput").ap()
    dwvT = nc.dram_tensor("wvT", [D, D], F32, kind="ExternalInput").ap()
    dwoT = nc.dram_tensor("woT", [HD, H * D], F32, kind="ExternalInput").ap()
    dwgT = nc.dram_tensor("wgT", [D, FF], I32, kind="ExternalInput").ap()
    dwuT = nc.dram_tensor("wuT", [D, FF], I32, kind="ExternalInput").ap()
    dwdT = nc.dram_tensor("wdT", [FF, D], I32, kind="ExternalInput").ap()
    dbq = nc.dram_tensor("bq", [128, DC], F32, kind="ExternalInput").ap()
    dbk = nc.dram_tensor("bk", [128, DC], F32, kind="ExternalInput").ap()
    dbo = nc.dram_tensor("bo", [128, DC], F32, kind="ExternalInput").ap()
    dbv = nc.dram_tensor("bvrow", [1, D], F32, kind="ExternalInput").ap()
    dcq = nc.dram_tensor("ropeCq", [128, T], BF16, kind="ExternalInput").ap()
    dsq = nc.dram_tensor("ropeSq", [128, T], BF16, kind="ExternalInput").ap()
    dck = nc.dram_tensor("ropeCk", [128, T], BF16, kind="ExternalInput").ap()
    dsk = nc.dram_tensor("ropeSk", [128, T], BF16, kind="ExternalInput").ap()
    dpsw = nc.dram_tensor("pswapT", [128, 128], BF16, kind="ExternalInput").ap()
    dci = nc.dram_tensor("cinfo", [1, 8], I32, kind="ExternalInput").ap()
    dy = nc.dram_tensor("y", [D, T], F32, kind="ExternalOutput").ap()

    dKx = nc.dram_tensor("Kx", [D, S], BF16, kind="Internal",
                         addr_space="Shared").ap()
    dVx = nc.dram_tensor("Vx", [H, S, 128], BF16, kind="Internal",
                         addr_space="Shared").ap()
    dKM = nc.dram_tensor("KMx", [1, 2], F32, kind="Internal",
                         addr_space="Shared").ap()
    dXsp = nc.dram_tensor("Xsp", [D, T], F32, kind="Internal").ap()
    dccin = nc.dram_tensor("ccin", [1, 32], F32, kind="Internal").ap()
    dccout = nc.dram_tensor("ccout", [1, 32], F32, kind="Internal").ap()

    with ExitStack() as stk:
        en = stk.enter_context
        arena = en(nc.sbuf_tensor("arena", [128, ARENA], BF16))
        arh = arena[:].tensor
        ar32 = arh.bitcast(F32)

        def bf(off, n, p=128):
            return arh[0:p, off:off + n]

        def f32(off, n32, p=128):
            assert off % 2 == 0
            return ar32[0:p, off // 2: off // 2 + n32]

        X = f32(O_X, 16384)
        O_all = bf(O_X, 32768, p=64)
        O_128 = bf(O_X, 32768, p=128)
        XN = bf(O_XN, 16384)
        AOUT = bf(O_XN, 16384)
        HALL = bf(O_HALL, 24576)
        Q_all = bf(O_QALL, 16384)
        WVT = bf(O_WVT, 8192)
        RCQ, RSQ, RCK, RSK = (bf(O_RCQ, 2048), bf(O_RSQ, 2048),
                              bf(O_RCK, 2048), bf(O_RSK, 2048))
        KZb = [bf(o, 4096) for o in O_KZ]
        WGb = [bf(o, 1024) for o in O_WG]
        WUb = [bf(o, 1024) for o in O_WU]
        SGb = [bf(o, 1024) for o in O_SG]
        UPb = [bf(o, 1024) for o in O_UP]
        WDb = [bf(o, 3072) for o in O_WD]
        RAWK = [bf(O_QRAW, 2048), bf(O_KRAW, 2048)]
        VOWb = [bf(o, 2048) for o in O_VOW]
        WSLb = [bf(O_WSL0, 1024), bf(O_WSL1, 1024)]
        VHb = [bf(o, 4096) for o in O_VH]
        XBFb = [bf(o, 2048) for o in O_XBF]
        XSQb = [bf(o, 2048) for o in O_XSQ]
        rowAb = bf(O_RAB, 2048, p=1)
        rowBb = bf(O_RBB, 2048, p=1)
        RTMP = bf(O_RTMP, 2048)
        RTM2 = bf(O_RTM2, 2048)
        QSQ = bf(O_QSQ, 2048)
        PAb = [bf(o, 1024) for o in O_PA]
        PBb = [bf(o, 1024) for o in O_PB]
        RROW = f32(O_RROW, 2048)
        RROWB = bf(O_RROWB, 2048)
        OST = bf(O_OST, 2048)
        RBC = bf(O_RROW, 2048, p=64)
        Abc = bf(98304, 2048)
        Bbc = bf(O_RTM2, 2048)
        WOb = [bf(o, 2048, p=128) for o in O_WO]
        rowA = f32(O_ROWA, 2048, p=1)
        rowB = f32(O_ROWB, 2048, p=1)

        ones_t = en(nc.sbuf_tensor("ones_t", [128, 128], BF16))
        c128_t = en(nc.sbuf_tensor("c128_t", [128, 128], BF16))
        psw_t = en(nc.sbuf_tensor("psw_t", [128, 128], BF16))
        bqc = en(nc.sbuf_tensor("bqc", [128, DC], F32))
        bkc = en(nc.sbuf_tensor("bkc", [128, DC], F32))
        boc = en(nc.sbuf_tensor("boc", [128, DC], F32))
        bvrep = en(nc.sbuf_tensor("bvrep", [128, D], BF16))
        bvrow_s = en(nc.sbuf_tensor("bvrow_s", [1, D], BF16))
        negmh = en(nc.sbuf_tensor("negmh", [128, 1], F32))
        qk_s = en(nc.sbuf_tensor("qk_s", [1, 8], F32))
        qk_b = en(nc.sbuf_tensor("qk_b", [1, 2], BF16))
        km_s = en(nc.sbuf_tensor("km_s", [1, 2], F32))
        ci_s = en(nc.sbuf_tensor("ci_s", [1, 8], I32))
        cc_s = en(nc.sbuf_tensor("cc_s", [1, 32], F32))

        P0 = en(nc.psum_tensor("P0", [128, 1024], F32))
        P1 = en(nc.psum_tensor("P1", [128, 1024], F32))
        P2 = en(nc.psum_tensor("P2", [128, 1024], F32))
        P3 = en(nc.psum_tensor("P3", [128, 1024], F32))
        PS = [P0, P1, P2, P3]

        sXin = Sem(stk, nc, "sXin")
        sPre = Sem(stk, nc, "sPre")
        sRope = Sem(stk, nc, "sRope")
        sKMr = Sem(stk, nc, "sKMr")
        sXbk = Sem(stk, nc, "sXbk")
        sKP = [Sem(stk, nc, "sKP0"), Sem(stk, nc, "sKP1")]
        sVH = [Sem(stk, nc, "sVH0"), Sem(stk, nc, "sVH1")]
        sSp = Sem(stk, nc, "sSp")
        sSt = Sem(stk, nc, "sSt")
        sBv = Sem(stk, nc, "sBv")
        sWvt = Sem(stk, nc, "sWvt")
        sWg = [Sem(stk, nc, "sWg0"), Sem(stk, nc, "sWg1")]
        sWd = [Sem(stk, nc, "sWd0"), Sem(stk, nc, "sWd1")]
        sWsl = [Sem(stk, nc, "sWsl0"), Sem(stk, nc, "sWsl1")]
        sWo = [Sem(stk, nc, "sWo0"), Sem(stk, nc, "sWo1")]
        sGXv = [Sem(stk, nc, "sGXv0"), Sem(stk, nc, "sGXv1")]
        sGXk = [Sem(stk, nc, "sGXk0"), Sem(stk, nc, "sGXk1")]
        sGXm = Sem(stk, nc, "sGXm")
        sGXb = Sem(stk, nc, "sGXb")
        sGXd = Sem(stk, nc, "sGXd")
        sRbc = Sem(stk, nc, "sRbc")
        sXc = [Sem(stk, nc, f"sXc{c}") for c in range(DC)]
        sXr = [Sem(stk, nc, f"sXr{c}") for c in range(DC)]
        sAb = Sem(stk, nc, "sAb")
        sBb = Sem(stk, nc, "sBb")
        sMM = Sem(stk, nc, "sMM")
        sVE = Sem(stk, nc, "sVE")
        sAC = Sem(stk, nc, "sAC")
        sXC = Sem(stk, nc, "sXC")

        rV = en(nc.gpsimd.register("rX"))
        rK = rV
        rM = rV

        sy = Eng("sync")
        tn = Eng("tensor")
        ve = Eng("vector", chain_sem=sVE)
        sc = Eng("scalar", chain_sem=sAC)
        gp = Eng("gpsimd")

        def mm(out, lhsT, rhs, first):
            def f(e, out=out, lhsT=lhsT, rhs=rhs, first=first):
                return e.matmul(out, lhsT, rhs, start=first, stop=True,
                                skip_group_check=(not first))
            return f

        for c in range(DC):
            sy.dma(lambda e, c=c: e.dma_start(
                out=X[:, c * T:(c + 1) * T],
                in_=dx[c * 128:(c + 1) * 128, :]), sXc[c])
        sy.dma(lambda e: e.dma_start(out=bqc[:], in_=dbq[:, :]), sPre)
        sy.dma(lambda e: e.dma_start(out=bkc[:], in_=dbk[:, :]), sPre)
        sy.dma(lambda e: e.dma_start(out=boc[:], in_=dbo[:, :]), sPre)
        sy.dma(lambda e: e.dma_start(out=psw_t[:], in_=dpsw[:, :]), sPre)
        sy.dma(lambda e: e.dma_start(out=ci_s[:], in_=dci[:, :]), sPre)
        ci_loaded = sPre.n

        gp.dma(lambda e: e.dma_start(out=bvrow_s[:], in_=dbv[:, :]), sBv)
        bv_loaded = sBv.n

        ve.op(lambda e: e.memset(ones_t[:], 1.0))
        ve.op(lambda e: e.memset(c128_t[:], 1.0 / 128.0))
        ve.op(lambda e: e.memset(cc_s[:], 1.0))
        consts_ready = ve.opi(lambda e: e.memset(qk_s[:], 0.0), sVE)

        def emit_ln(gates):
            prev = [0, 0]
            for c in range(DC):
                p = c % 2
                g_sem, g_val = gates[c]
                ve.wait(g_sem, g_val)
                ve.wait(sMM, prev[p])
                v_bf = ve.opi(lambda e, p=p, c=c: e.tensor_copy(
                    XBFb[p][:, :], X[:, c * T:(c + 1) * T]), sVE)
                sc.wait(g_sem, g_val)
                sc.wait(sMM, prev[p])
                a_sq = sc.opi(lambda e, p=p, c=c: e.activation(
                    out=XSQb[p][:, :], in_=X[:, c * T:(c + 1) * T],
                    func=AF.Square), sAC)
                tn.wait(sVE, v_bf)
                tn.wait(sAC, a_sq)
                for sj in range(4):
                    tn.op(mm(PS[sj // 2][:, (sj % 2) * 512:(sj % 2) * 512 + 512],
                             ones_t[:, 0:128], XBFb[p][:, sj * 512:sj * 512 + 512],
                             first=(c == 0)))
                for sj in range(3):
                    tn.op(mm(PS[2 + sj // 2][:, (sj % 2) * 512:(sj % 2) * 512 + 512],
                             ones_t[:, 0:128], XSQb[p][:, sj * 512:sj * 512 + 512],
                             first=(c == 0)))
                prev[p] = tn.opi(mm(PS[3][:, 512:1024], ones_t[:, 0:128],
                                    XSQb[p][:, 1536:2048], first=(c == 0)), sMM)
            ve.wait(sMM, sMM.n)
            ve.op(lambda e: e.tensor_scalar(out=rowB[:, 0:1024], in0=P0[0:1, :],
                                            scalar1=1.0 / D, scalar2=None,
                                            op0=ALU.mult))
            v = ve.opi(lambda e: e.tensor_scalar(
                out=rowB[:, 1024:2048], in0=P1[0:1, :], scalar1=1.0 / D,
                scalar2=None, op0=ALU.mult), sVE)
            ve.wait(sVE, v)
            v = ve.opi(lambda e: e.tensor_mul(rowA[:, :], rowB[:, :],
                                              rowB[:, :]), sVE)
            ve.wait(sVE, v)
            ve.op(lambda e: e.scalar_tensor_tensor(
                out=rowA[:, 0:1024], in0=P2[0:1, :], scalar=1.0 / D,
                in1=rowA[:, 0:1024], op0=ALU.mult, op1=ALU.subtract))
            v = ve.opi(lambda e: e.scalar_tensor_tensor(
                out=rowA[:, 1024:2048], in0=P3[0:1, :], scalar=1.0 / D,
                in1=rowA[:, 1024:2048], op0=ALU.mult, op1=ALU.subtract), sVE)
            ve.wait(sVE, v)
            v = ve.opi(lambda e: e.tensor_scalar(
                out=rowA[:, :], in0=rowA[:, :], scalar1=EPS, scalar2=None,
                op0=ALU.add), sVE)
            ve.wait(sVE, v)
            sc.wait(sVE, v)
            a_s = sc.opi(lambda e: e.activation(out=rowA[:, :], in_=rowA[:, :],
                                                func=AF.Ln), sAC)
            sc.wait(sAC, a_s)
            a_s = sc.opi(lambda e: e.activation(out=rowA[:, :], in_=rowA[:, :],
                                                func=AF.Exp, scale=-0.5), sAC)
            ve.wait(sAC, a_s)
            v = ve.opi(lambda e: e.scalar_tensor_tensor(
                out=rowB[:, :], in0=rowB[:, :], scalar=-1.0, in1=rowA[:, :],
                op0=ALU.mult, op1=ALU.mult), sVE)
            ve.wait(sVE, v)
            ve.op(lambda e: e.tensor_copy(rowAb[:, :], rowA[:, :]))
            rows_done = ve.opi(lambda e: e.tensor_copy(rowBb[:, :], rowB[:, :]),
                               sVE)
            gp.wait(sVE, rows_done)
            abv = gp.dma(lambda e: e.dma_start(
                out=Abc[:, :],
                in_=bass.AP(arh, O_RAB, [[ARENA, 1], [0, 128], [1, 2048]])),
                sAb)
            bbv = gp.dma(lambda e: e.dma_start(
                out=Bbc[:, :],
                in_=bass.AP(arh, O_RBB, [[ARENA, 1], [0, 128], [1, 2048]])),
                sBb)
            return (abv, bbv)

        def emit_apply(bcast):
            abv, bbv = bcast
            ve.wait(sAb, abv)
            ve.wait(sBb, bbv)
            done = [[0, 0] for _ in range(DC)]
            for c in range(DC):
                for half in range(2):
                    v = ve.opi(lambda e, c=c, half=half: e.tensor_tensor(
                        RTMP[:, 0:1024],
                        X[:, c * T + half * 1024: c * T + half * 1024 + 1024],
                        Abc[:, half * 1024:half * 1024 + 1024], ALU.mult), sVE)
                    ve.wait(sVE, v)
                    done[c][half] = ve.opi(
                        lambda e, c=c, half=half: e.tensor_tensor(
                            XN[:, c * T + half * 1024:
                               c * T + half * 1024 + 1024],
                            RTMP[:, 0:1024],
                            Bbc[:, half * 1024:half * 1024 + 1024], ALU.add),
                        sVE)
            return done

        def emit_ffn(gates, w_init_mm=0):
            bcast = emit_ln(gates)
            apply_done = emit_apply(bcast)
            carry_mm = w_init_mm
            for half in range(2):
                wfree = [carry_mm, carry_mm]
                silu_d = [sAC.n, sAC.n]
                up_d = [sVE.n, sVE.n]
                g_mm = u_mm = 0
                for hc in range(FC):
                    p = hc % 2
                    gp.wait(sMM, wfree[p])
                    gp.dma(lambda e, p=p, hc=hc: e.dma_start(
                        out=WGb[p][:].rearrange("p (a k) -> p a k", k=128),
                        in_=dwgT.rearrange("(a p) f -> p a f", p=128)
                        [:, :, hc * 128:(hc + 1) * 128]), sWg[p])
                    w_rdy = gp.dma(lambda e, p=p, hc=hc: e.dma_start(
                        out=WUb[p][:].rearrange("p (a k) -> p a k", k=128),
                        in_=dwuT.rearrange("(a p) f -> p a f", p=128)
                        [:, :, hc * 128:(hc + 1) * 128]), sWg[p])
                    tn.wait(sWg[p], w_rdy)
                    tn.wait(sAC, silu_d[p])
                    tn.wait(sVE, up_d[p])
                    for dc in range(DC):
                        tn.wait(sVE, apply_done[dc][half])
                        for sj in range(2):
                            f = mm(PS[p][:, sj * 512:sj * 512 + 512],
                                   WGb[p][:, dc * 128:dc * 128 + 128],
                                   XN[:, dc * T + half * 1024 + sj * 512:
                                      dc * T + half * 1024 + sj * 512 + 512],
                                   first=(dc == 0))
                            if dc == DC - 1 and sj == 1:
                                g_mm = tn.opi(f, sMM)
                            else:
                                tn.op(f)
                    for dc in range(DC):
                        for sj in range(2):
                            f = mm(PS[2 + p][:, sj * 512:sj * 512 + 512],
                                   WUb[p][:, dc * 128:dc * 128 + 128],
                                   XN[:, dc * T + half * 1024 + sj * 512:
                                      dc * T + half * 1024 + sj * 512 + 512],
                                   first=(dc == 0))
                            if dc == DC - 1 and sj == 1:
                                u_mm = tn.opi(f, sMM)
                            else:
                                tn.op(f)
                    wfree[p] = u_mm
                    sc.wait(sMM, g_mm)
                    silu_d[p] = sc.opi(lambda e, p=p: e.activation(
                        out=SGb[p][:, :], in_=PS[p][:, :], func=AF.Silu,
                        scale=INV127), sAC)
                    ve.wait(sMM, u_mm)
                    v_up = ve.opi(lambda e, p=p: e.tensor_copy(
                        UPb[p][:, :], PS[2 + p][:, :]), sVE)
                    ve.wait(sAC, silu_d[p])
                    ve.wait(sVE, v_up)
                    up_d[p] = ve.opi(lambda e, p=p, hc=hc: e.tensor_tensor(
                        HALL[:, hc * 1024:(hc + 1) * 1024], SGb[p][:, :],
                        UPb[p][:, :], ALU.mult), sVE)
                h_done = sVE.n
                silu_all = sAC.n
                wd_free = [0, 0]
                ev_d = [sVE.n, sVE.n]
                d_mm = 0
                for ec in range(DC):
                    p = ec % 2
                    gp.wait(sMM, wd_free[p])
                    gp.wait(sVE, h_done)
                    wd_rdy = gp.dma(lambda e, p=p, ec=ec: e.dma_start(
                        out=WDb[p][:].rearrange("p (a k) -> p a k", k=128),
                        in_=dwdT.rearrange("(a p) f -> p a f", p=128)
                        [:, :, ec * 128:(ec + 1) * 128]), sWd[p])
                    tn.wait(sVE, h_done)
                    tn.wait(sAC, silu_all)
                    tn.wait(sWd[p], wd_rdy)
                    tn.wait(sVE, ev_d[p])
                    for hc in range(FC):
                        for sj in range(2):
                            f = mm(PS[p][:, sj * 512:sj * 512 + 512],
                                   WDb[p][:, hc * 128:hc * 128 + 128],
                                   HALL[:, hc * 1024 + sj * 512:
                                        hc * 1024 + sj * 512 + 512],
                                   first=(hc == 0))
                            if hc == FC - 1 and sj == 1:
                                d_mm = tn.opi(f, sMM)
                            else:
                                tn.op(f)
                    wd_free[p] = d_mm
                    carry_mm = d_mm
                    ve.wait(sMM, d_mm)
                    ev_d[p] = ve.opi(
                        lambda e, p=p, ec=ec, half=half: e.scalar_tensor_tensor(
                            out=X[:, ec * T + half * 1024:
                                  ec * T + half * 1024 + 1024],
                            in0=PS[p][:, :], scalar=SCL_FFN,
                            in1=X[:, ec * T + half * 1024:
                                  ec * T + half * 1024 + 1024],
                            op0=ALU.mult, op1=ALU.add), sVE)
            return sVE.n

        x1_done = emit_ffn([(sXc[c], 16) for c in range(DC)])

        def store_X(ready_val):
            sy.wait(sVE, ready_val)
            for c in range(DC):
                sy.dma(lambda e, c=c: e.dma_start(
                    out=dy[c * 128:(c + 1) * 128, :],
                    in_=X[:, c * T:(c + 1) * T]), sSt)
            sy.wait(sSt, sSt.n)

        def finish():
            blk = en(nc.Block())
            blk.sync(sy.replay())
            blk.tensor(tn.replay())
            blk.vector(ve.replay())
            blk.scalar(sc.replay())
            blk.gpsimd(gp.replay())
            return nc

        if upto == "ffn1":
            store_X(x1_done)
            return finish()

        ve.wait(sBv, bv_loaded)
        tn.wait(sBv, bv_loaded)
        tn.wait(sVE, consts_ready)
        tn.wait(sVE, x1_done)
        tn.op(mm(P0[:, 0:512], ones_t[0:1, 0:128], bvrow_s[0:1, 0:512], True))
        bv_mm = tn.opi(mm(P0[:, 512:1024], ones_t[0:1, 0:128],
                          bvrow_s[0:1, 512:1024], True), sMM)
        ve.wait(sMM, bv_mm)
        bvrep_done = ve.opi(lambda e: e.tensor_copy(bvrep[:, :], P0[:, :]), sVE)

        bcast2 = emit_ln([(sVE, x1_done)] * DC)
        apply2 = emit_apply(bcast2)
        xn2_done = apply2[DC - 1][1]

        for c in range(DC):
            sy.wait(sVE, apply2[c][1])
            sy.dma(lambda e, c=c: e.dma_start(
                out=dXsp[c * 128:(c + 1) * 128, :],
                in_=X[:, c * T:(c + 1) * T]), sSp)
        x_spilled = sSp.n

        gp.wait(sMM, sMM.n)
        for dc in range(DC):
            gp.dma(lambda e, dc=dc: e.dma_start(
                out=WVT[:, dc * 1024:(dc + 1) * 1024],
                in_=dwvT[dc * 128:(dc + 1) * 128, :]), sWvt)
        wv_ready = sWvt.n

        ve.op(lambda e: e.memset(VOWb[0][:, :], 0.0))
        ve.op(lambda e: e.memset(VOWb[1][:, :], 0.0))
        ve.op(lambda e: e.memset(
            VOWb[0][:].rearrange("p (t j) -> p t j", j=128)[:, :, 64:65], 1.0))
        vow_ones = ve.op(lambda e: e.memset(
            VOWb[1][:].rearrange("p (t j) -> p t j", j=128)[:, :, 64:65], 1.0))

        gp.wait(sPre, ci_loaded)
        gp.op(lambda e: e.reg_load(rV, ci_s[0:1, 1:2]))
        v_evac = [0, 0]
        vow_free = [0, 0]
        for tt in range(16):
            p = tt % 2
            pv = PS[1 + p]
            tn.wait(sWvt, wv_ready)
            tn.wait(sVE, bvrep_done)
            tn.wait(sVE, v_evac[p])
            for dc in range(DC):
                tn.wait(sVE, apply2[dc][tt // 8])
                for dj in range(2):
                    tn.op(mm(pv[:, dj * 512:dj * 512 + 512],
                             XN[:, dc * T + tt * 128: dc * T + tt * 128 + 128],
                             WVT[:, dc * 1024 + dj * 512:
                                 dc * 1024 + dj * 512 + 512],
                             first=(dc == 0)))
            tn.op(mm(pv[:, 0:512], c128_t[:, :], bvrep[:, 0:512], first=False))
            v_mm = tn.opi(mm(pv[:, 512:1024], c128_t[:, :], bvrep[:, 512:1024],
                             first=False), sMM)
            ve.wait(sMM, v_mm)
            ve.wait(sVE, vow_ones)
            ve.wait(sGXv[p], vow_free[p])
            v_evac[p] = ve.opi(lambda e, p=p, pv=pv: e.tensor_copy(
                VOWb[p][:].rearrange("p (t j) -> p t j", j=128)[:, :, 0:64],
                pv[:].rearrange("p (t j) -> p t j", j=64)), sVE)
            gp.wait(sVE, v_evac[p])
            vow_free[p] = gp.dma(lambda e, p=p: e.dma_start(
                out=bass.AP(dVx.tensor, rV,
                            [[128, 128], [S * 128, 16], [1, 128]]),
                in_=bass.AP(arh, O_VOW[p],
                            [[ARENA, 128], [128, 16], [1, 128]]),
                bounds_check=None), sGXv[p])
            if tt < 15:
                gp.op(lambda e: e.reg_add(rV, rV, 128 * 128))
        wvt_mm_done = sMM.n

        sy.wait(sMM, wvt_mm_done)
        for (d_t, s_t) in ((dcq, RCQ), (dsq, RSQ), (dck, RCK), (dsk, RSK)):
            sy.dma(lambda e, d_t=d_t, s_t=s_t: e.dma_start(
                out=s_t[:, :], in_=d_t[:, :]), sRope)
        ropes_ready = sRope.n

        def emit_qk(wT_d, bias_col, CT, STt, is_q, psin, export_cb=None):
            wfree = [psin, psin]
            psum_free = [sVE.n, sVE.n]
            slot_gx = [0, 0]
            slot_sq = [0, 0]
            qk_mm_v = [0] * DC
            sw_mm = nm_mm = 0

            def emit_proj(ecc):
                p = ecc % 2
                gp.wait(sMM, wfree[p])
                w_rdy = gp.dma(lambda e, p=p, ecc=ecc: e.dma_start(
                    out=WSLb[p][:].rearrange("p (a k) -> p a k", k=128),
                    in_=wT_d.rearrange("(a p) f -> p a f", p=128)
                    [:, :, ecc * 128:(ecc + 1) * 128]), sWsl[p])
                pq, pq2 = PS[2 * p], PS[2 * p + 1]
                tn.wait(sWsl[p], w_rdy)
                tn.wait(sRope, ropes_ready)
                tn.wait(sVE, psum_free[p])
                last = 0
                for dc in range(DC):
                    for sj in range(4):
                        pt = pq if sj < 2 else pq2
                        f = mm(pt[:, (sj % 2) * 512:(sj % 2) * 512 + 512],
                               WSLb[p][:, dc * 128:dc * 128 + 128],
                               XN[:, dc * T + sj * 512: dc * T + sj * 512 + 512],
                               first=(dc == 0))
                        if dc == DC - 1 and sj == 3:
                            last = tn.opi(f, sMM)
                        else:
                            tn.op(f)
                qk_mm_v[ecc] = last

            emit_proj(0)
            for ecc in range(DC):
                p = ecc % 2
                pq, pq2 = PS[2 * p], PS[2 * p + 1]
                ve.wait(sMM, qk_mm_v[ecc])
                ve.op(lambda e, pq=pq, ecc=ecc: e.tensor_scalar(
                    out=XBFb[0][:, 0:1024], in0=pq[:, :],
                    scalar1=bias_col[:, ecc:ecc + 1], scalar2=None,
                    op0=ALU.add))
                raw_done = ve.op(lambda e, pq2=pq2, ecc=ecc: e.tensor_scalar(
                    out=XBFb[0][:, 1024:2048], in0=pq2[:, :],
                    scalar1=bias_col[:, ecc:ecc + 1], scalar2=None,
                    op0=ALU.add))
                if ecc + 1 < DC:
                    emit_proj(ecc + 1)
                tn.wait(sVE, raw_done)
                for sj in range(4):
                    pt = pq if sj < 2 else pq2
                    f = mm(pt[:, (sj % 2) * 512:(sj % 2) * 512 + 512],
                           psw_t[:, :], XBFb[0][:, sj * 512:sj * 512 + 512],
                           first=True)
                    if sj == 3:
                        sw_mm = tn.opi(f, sMM)
                    else:
                        tn.op(f)
                ve.wait(sMM, sw_mm)
                ve.op(lambda e: e.tensor_tensor(
                    RTMP[:, :], XBFb[0][:, :], CT[:, :], ALU.mult))
                ve.op(lambda e, pq=pq: e.tensor_tensor(
                    RTM2[:, 0:1024], pq[:, :], STt[:, 0:1024], ALU.mult))
                v2 = ve.op(lambda e, pq2=pq2: e.tensor_tensor(
                    RTM2[:, 1024:2048], pq2[:, :], STt[:, 1024:2048],
                    ALU.mult))
                ve.wait(sVE, v2)
                if is_q:
                    dst = Q_all[:, ecc * T:(ecc + 1) * T]
                else:
                    dst = RAWK[p][:, :]
                    ve.wait(sGXk[0], slot_gx[p])
                    ve.wait(sAC, slot_sq[p])
                rope_done = ve.op(lambda e, dst=dst: e.tensor_tensor(
                    dst, RTMP[:, :], RTM2[:, :], ALU.add))
                if export_cb is not None:
                    ex_v = export_cb(ecc, rope_done)
                    if ex_v:
                        slot_gx[0] = slot_gx[1] = ex_v
                sc.wait(sVE, rope_done)
                sq_d = sc.op(lambda e, dst=dst: e.activation(
                    out=QSQ[:, :], in_=dst, func=AF.Square))
                if not is_q:
                    slot_sq[p] = sq_d
                tn.wait(sAC, sq_d)
                tn.wait(sVE, rope_done)
                for sj in range(4):
                    pt = pq if sj < 2 else pq2
                    f = mm(pt[:, (sj % 2) * 512:(sj % 2) * 512 + 512],
                           ones_t[:, 0:128], QSQ[:, sj * 512:sj * 512 + 512],
                           first=True)
                    if sj == 3:
                        nm_mm = tn.opi(f, sMM)
                    else:
                        tn.op(f)
                wfree[p] = nm_mm
                slot = 0 if is_q else 1
                ve.wait(sMM, nm_mm)
                v = ve.op(lambda e, pq=pq: e.reduce_max(
                    qk_s[0:1, 4:5], pq[0:1, :], axis=AXX))
                ve.wait(sVE, v)
                v = ve.op(lambda e, pq2=pq2: e.reduce_max(
                    qk_s[0:1, 5:6], pq2[0:1, :], axis=AXX))
                ve.wait(sVE, v)
                v = ve.op(lambda e: e.tensor_tensor(
                    qk_s[0:1, 4:5], qk_s[0:1, 4:5], qk_s[0:1, 5:6], ALU.max))
                ve.wait(sVE, v)
                v = ve.op(lambda e, slot=slot: e.tensor_tensor(
                    qk_s[0:1, slot:slot + 1], qk_s[0:1, slot:slot + 1],
                    qk_s[0:1, 4:5], ALU.max))
                psum_free[p] = v
            return sVE.n

        q_done = emit_qk(dwqT, bqc, RCQ, RSQ, True, psin=wvt_mm_done)

        gp.op(lambda e: e.reg_load(rK, ci_s[0:1, 0:1]))
        kx_first = [True]

        def k_export(ecc, rope_done):
            if ecc % 2 == 0:
                return 0
            gp.wait(sVE, rope_done)
            if not kx_first[0]:
                gp.op(lambda e: e.reg_add(rK, rK, 2 * 128 * S))
            kx_first[0] = False
            return gp.dma(lambda e: e.dma_start(
                out=bass.AP(dKx.tensor, rK,
                            [[S, 128], [128 * S, 2], [1, 2048]]),
                in_=bass.AP(arh, O_QRAW,
                            [[ARENA, 128], [2048, 2], [1, 2048]]),
                bounds_check=None), sGXk[0])

        kmax_done = emit_qk(dwkT, bkc, RCK, RSK, False, psin=sMM.n,
                            export_cb=k_export)

        gp.op(lambda e: e.reg_load(rM, ci_s[0:1, 2:3]))
        gp.wait(sVE, kmax_done)
        gp.dma(lambda e: e.dma_start(
            out=bass.AP(dKM.tensor, rM, [[1, 1], [1, 1]]),
            in_=qk_s[0:1, 1:2], bounds_check=None), sGXm)

        for s_ in (sGXv[0], sGXv[1], sGXk[0], sGXm):
            gp.wait(s_, s_.n)
        gp.dma(lambda e: e.dma_start(out=dccin[:, :], in_=cc_s[0:1, :]), sGXb)
        gp.wait(sGXb, sGXb.n)
        barrier = gp.opi(lambda e: e.collective_compute(
            "AllReduce", ALU.add, ins=[dccin[:, :]], outs=[dccout[:, :]],
            replica_groups=[list(range(NCORE))]), sXC)

        if upto == "qkv":
            gp.wait(sXC, barrier)
            for c in range(DC):
                gp.dma(lambda e, c=c: e.dma_start(
                    out=dy[c * 128:(c + 1) * 128, :],
                    in_=Q_all[:, c * T:(c + 1) * T]), sGXd)
            gp.wait(sGXd, sGXd.n)
            return finish()

        ve.wait(sVE, kmax_done)
        ve.wait(sSp, x_spilled)
        ve.op(lambda e: e.memset(O_128[64:128, 0:16384], 0.0))
        ve.op(lambda e: e.memset(O_128[64:128, 16384:32768], 0.0))
        ve.op(lambda e: e.memset(KZb[0][:, :], 0.0))
        kz_zeroed = ve.op(lambda e: e.memset(KZb[1][:, :], 0.0))

        sy.wait(sXC, barrier)
        km_ready = sy.dma(lambda e: e.dma_start(out=km_s[:], in_=dKM[:, :]), sKMr)
        ve.wait(sKMr, km_ready)
        v = ve.opi(lambda e: e.reduce_max(qk_s[0:1, 6:7], km_s[0:1, 0:2],
                                          axis=AXX), sVE)
        ve.wait(sVE, v)
        v = ve.opi(lambda e: e.tensor_mul(qk_s[0:1, 7:8], qk_s[0:1, 0:1],
                                          qk_s[0:1, 6:7]), sVE)
        sc.wait(sVE, v)
        a_v = sc.opi(lambda e: e.activation(out=qk_s[0:1, 7:8],
                                            in_=qk_s[0:1, 7:8], func=AF.Ln),
                     sAC)
        sc.wait(sAC, a_v)
        a_v = sc.opi(lambda e: e.activation(out=qk_s[0:1, 7:8],
                                            in_=qk_s[0:1, 7:8], func=AF.Exp,
                                            scale=0.5), sAC)
        ve.wait(sAC, a_v)
        v_qb = ve.opi(lambda e: e.tensor_copy(qk_b[0:1, 0:1], qk_s[0:1, 7:8]),
                      sVE)
        tn.wait(sVE, v_qb)
        tn.wait(sVE, kmax_done)
        mh_mm = tn.opi(mm(P0[:, 0:1], ones_t[0:1, 0:128], qk_b[0:1, 0:1],
                          True), sMM)
        ve.wait(sMM, mh_mm)
        negmh_done = ve.opi(lambda e: e.tensor_scalar(
            out=negmh[:, 0:1], in0=P0[:, 0:1], scalar1=-1.0, scalar2=None,
            op0=ALU.mult), sVE)

        if upto == "mhat":
            gp.wait(sVE, negmh_done)
            for c in range(DC):
                gp.dma(lambda e, c=c: e.dma_start(
                    out=dy[c * 128:(c + 1) * 128, :],
                    in_=Q_all[:, c * T:(c + 1) * T]), sGXd)
            gp.wait(sGXd, sGXd.n)
            return finish()

        LAG = 3
        kp_ready = [0, 0]
        kp_free = [0, 0]
        vh_ready = [0, 0]
        vh_free = [0, 0]
        pab_free = [0, 0, 0, 0]
        pbb_free = [0, 0, 0, 0]
        pv_free = negmh_done
        expA_prev = 0
        expB_prev = 0
        p01_init = negmh_done

        for pr in range(DC):
            sy.wait(sXC, barrier)
            sy.wait(sVE, kz_zeroed)
            sy.wait(sMM, kp_free[0])
            sy.dma(lambda e, pr=pr: e.dma_start(
                out=KZb[0][0:64, :], in_=dKx[pr * 128:pr * 128 + 64, :]),
                sKP[0])
            kp_ready[0] = sy.dma(lambda e, pr=pr: e.dma_start(
                out=KZb[1][64:128, :],
                in_=dKx[pr * 128 + 64:pr * 128 + 128, :]), sKP[0])
            kp_ready[1] = kp_ready[0]
            for hh in range(2):
                h = 2 * pr + hh
                vb = h % 2
                base = 64 * hh
                sy.wait(sMM, vh_free[vb])
                vh_ready[vb] = sy.dma(lambda e, vb=vb, h=h: e.dma_start(
                    out=VHb[vb][:].rearrange("p (t j) -> p t j", j=128),
                    in_=dVx[h].rearrange("(t p) j -> p t j", p=128)), sVH[vb])
                KZ = KZb[hh]
                expA = [0] * 32
                expB = [0] * 32
                for tc in range(32):
                    tn.wait(sKP[0], kp_ready[0])
                    tn.wait(sVH[vb], vh_ready[vb])
                    tn.wait(sVE, p01_init)
                    tn.wait(sAC, expA[tc - 1] if tc > 0 else expA_prev)
                    tn.op(mm(P0[:, 0:512],
                             KZ[:, tc * 128:tc * 128 + 128],
                             Q_all[:, pr * T: pr * T + 512],
                             first=True))
                    scA = tn.opi(mm(P0[:, 512:1024],
                                    KZ[:, tc * 128:tc * 128 + 128],
                                    Q_all[:, pr * T + 512: pr * T + 1024],
                                    first=True), sMM)
                    tn.wait(sAC, expB[tc - 1] if tc > 0 else expB_prev)
                    tn.op(mm(P1[:, 0:512],
                             KZ[:, tc * 128:tc * 128 + 128],
                             Q_all[:, pr * T + 1024: pr * T + 1536],
                             first=True))
                    scB = tn.opi(mm(P1[:, 512:1024],
                                    KZ[:, tc * 128:tc * 128 + 128],
                                    Q_all[:, pr * T + 1536: pr * T + 2048],
                                    first=True), sMM)
                    sc.wait(sMM, scA)
                    sc.wait(sVE, negmh_done)
                    sc.wait(sMM, pab_free[tc % 4])
                    expA[tc] = sc.opi(lambda e, tc=tc: e.activation(
                        out=PAb[tc % 4][:, :], in_=P0[:, :], func=AF.Exp,
                        bias=negmh[:, 0:1]), sAC)
                    sc.wait(sMM, scB)
                    sc.wait(sMM, pbb_free[tc % 4])
                    expB[tc] = sc.opi(lambda e, tc=tc: e.activation(
                        out=PBb[tc % 4][:, :], in_=P1[:, :], func=AF.Exp,
                        bias=negmh[:, 0:1]), sAC)
                    if tc >= LAG:
                        tv = tc - LAG
                        tn.wait(sAC, expA[tv])
                        if tv == 0:
                            tn.wait(sVE, pv_free)
                        tn.op(mm(P2[:, 0:512],
                                 VHb[vb][:, tv * 128:tv * 128 + 128],
                                 PAb[tv % 4][:, 0:512], first=(tv == 0)))
                        pab_free[tv % 4] = tn.opi(
                            mm(P2[:, 512:1024],
                               VHb[vb][:, tv * 128:tv * 128 + 128],
                               PAb[tv % 4][:, 512:1024], first=(tv == 0)), sMM)
                        tn.wait(sAC, expB[tv])
                        tn.op(mm(P3[:, 0:512],
                                 VHb[vb][:, tv * 128:tv * 128 + 128],
                                 PBb[tv % 4][:, 0:512], first=(tv == 0)))
                        pbb_free[tv % 4] = tn.opi(
                            mm(P3[:, 512:1024],
                               VHb[vb][:, tv * 128:tv * 128 + 128],
                               PBb[tv % 4][:, 512:1024], first=(tv == 0)), sMM)
                for tv in range(32 - LAG, 32):
                    tn.wait(sAC, expA[tv])
                    tn.op(mm(P2[:, 0:512], VHb[vb][:, tv * 128:tv * 128 + 128],
                             PAb[tv % 4][:, 0:512], first=(tv == 0)))
                    pab_free[tv % 4] = tn.opi(
                        mm(P2[:, 512:1024], VHb[vb][:, tv * 128:tv * 128 + 128],
                           PAb[tv % 4][:, 512:1024], first=(tv == 0)), sMM)
                    tn.wait(sAC, expB[tv])
                    tn.op(mm(P3[:, 0:512], VHb[vb][:, tv * 128:tv * 128 + 128],
                             PBb[tv % 4][:, 0:512], first=(tv == 0)))
                    pbb_free[tv % 4] = tn.opi(
                        mm(P3[:, 512:1024], VHb[vb][:, tv * 128:tv * 128 + 128],
                           PBb[tv % 4][:, 512:1024], first=(tv == 0)), sMM)
                pv_last = sMM.n
                vh_free[vb] = pv_last
                if hh == 1:
                    kp_free[0] = scB
                ve.wait(sMM, pv_last)
                ve.wait(sSp, x_spilled)
                ve.op(lambda e: e.tensor_copy(OST[0:64, 0:1024], P2[0:64, :]))
                ve.op(lambda e: e.tensor_copy(OST[0:64, 1024:2048],
                                              P3[0:64, :]))
                ve.op(lambda e: e.tensor_copy(RROW[64:65, 0:1024],
                                              P2[64:65, :]))
                pv_free = ve.op(lambda e: e.tensor_copy(
                    RROW[64:65, 1024:2048], P3[64:65, :]))
                ve.op(lambda e: e.reciprocal(RROW[64:65, :], RROW[64:65, :]))
                rb_rdy = ve.op(lambda e: e.tensor_copy(RROWB[64:65, :],
                                                       RROW[64:65, :]))
                gp.wait(sVE, rb_rdy)
                rbc_rdy = gp.dma(lambda e: e.dma_start(
                    out=RBC[:, :],
                    in_=bass.AP(arh, 64 * ARENA + O_RROWB,
                                [[ARENA, 1], [0, 64], [1, 2048]])), sRbc)
                ve.wait(sRbc, rbc_rdy)
                ve.op(lambda e, h=h: e.tensor_tensor(
                    O_all[0:64, h * T: h * T + 1024],
                    OST[0:64, 0:1024], RBC[:, 0:1024], ALU.mult))
                ve.op(lambda e, h=h: e.tensor_tensor(
                    O_all[0:64, h * T + 1024: h * T + 2048],
                    OST[0:64, 1024:2048], RBC[:, 1024:2048], ALU.mult))
                expA_prev = expA[31]
                expB_prev = expB[31]
        attn_done = sVE.n
        vh_last_mm = sMM.n

        if upto in ("attn", "attn1"):
            gp.wait(sVE, attn_done)
            for hh2 in range(H):
                gp.dma(lambda e, hh2=hh2: e.dma_start(
                    out=dy[hh2 * 64:(hh2 + 1) * 64, :],
                    in_=O_all[0:64, hh2 * T:(hh2 + 1) * T]), sGXd)
            gp.wait(sGXd, sGXd.n)
            return finish()

        ve.op(lambda e: e.memset(WOb[0][64:128, :], 0.0))
        wo_z = ve.op(lambda e: e.memset(WOb[1][64:128, :], 0.0))
        gp.wait(sVE, wo_z)
        wo_free = [0, 0]
        ev_free = [0, 0, 0, 0]
        wo_last_mm = 0
        for ecc in range(DC):
            p = ecc % 2
            gp.wait(sXC, barrier)
            gp.wait(sMM, wo_free[p])
            if ecc < 2:
                gp.wait(sMM, vh_last_mm)
                gp.wait(sVE, attn_done)
            wo_rdy = gp.dma(lambda e, p=p, ecc=ecc: e.dma_start(
                out=WOb[p][0:64, :].rearrange("p (hh k) -> p hh k", k=128),
                in_=dwoT.rearrange("p (hh ee) -> p hh ee", ee=D)
                [:, :, ecc * 128:(ecc + 1) * 128]), sWo[p])
            tn.wait(sWo[p], wo_rdy)
            tn.wait(sVE, attn_done)
            for qj in range(4):
                pt = PS[2 * p + qj // 2]
                sl = (qj % 2) * 512
                tn.wait(sVE, ev_free[2 * p + qj // 2])
                w_mm = 0
                for hh in range(H):
                    f = mm(pt[:, sl:sl + 512],
                           WOb[p][0:128, hh * 128:hh * 128 + 128],
                           O_128[0:128,
                                 hh * T + qj * 512: hh * T + qj * 512 + 512],
                           first=(hh == 0))
                    if hh == H - 1:
                        w_mm = tn.opi(f, sMM)
                    else:
                        tn.op(f)
                wo_free[p] = w_mm
                wo_last_mm = w_mm
                ve.wait(sMM, w_mm)
                ev_free[2 * p + qj // 2] = ve.opi(
                    lambda e, ecc=ecc, qj=qj, pt=pt, sl=sl: e.tensor_scalar(
                        out=AOUT[:, ecc * T + qj * 512:
                                 ecc * T + qj * 512 + 512],
                        in0=pt[:, sl:sl + 512], scalar1=boc[:, ecc:ecc + 1],
                        scalar2=None, op0=ALU.add), sVE)
        aout_done = sVE.n

        sy.wait(sMM, wo_last_mm)
        sy.wait(sSp, x_spilled)
        gp.wait(sMM, wo_last_mm)
        gp.wait(sSp, x_spilled)
        for c in range(DC):
            eng_q = sy if c % 2 == 0 else gp
            eng_q.dma(lambda e, c=c: e.dma_start(
                out=X[:, c * T:(c + 1) * T],
                in_=dXsp[c * 128:(c + 1) * 128, :]), sXr[c])
        ve.wait(sVE, aout_done)
        x2v = [0] * DC
        for c in range(DC):
            ve.wait(sXr[c], 16)
            x2v[c] = ve.opi(lambda e, c=c: e.tensor_tensor(
                X[:, c * T:(c + 1) * T], X[:, c * T:(c + 1) * T],
                AOUT[:, c * T:(c + 1) * T], ALU.add), sVE)
        x2_done = x2v[DC - 1]

        if upto == "wo":
            store_X(x2_done)
            return finish()

        x3_done = emit_ffn([(sVE, x2v[c]) for c in range(DC)],
                           w_init_mm=wo_last_mm)
        store_X(x3_done)
        return finish()


def _rope_tables(pos0, scale):
    j = np.arange(32)
    f = (10000.0 ** (-j / 32.0)).astype(np.float64)
    t = (pos0 + np.arange(T)).astype(np.float64)
    ang = np.outer(f, t)
    c32 = np.cos(ang)
    s32 = np.sin(ang)
    Cq = np.zeros((64, T))
    Sq = np.zeros((64, T))
    Cq[0::2] = c32
    Cq[1::2] = c32
    Sq[0::2] = -s32
    Sq[1::2] = s32
    Cf = (np.tile(Cq, (2, 1)) * scale).astype(ml_dtypes.bfloat16)
    Sf = (np.tile(Sq, (2, 1)) * scale).astype(ml_dtypes.bfloat16)
    return Cf, Sf


def _pswap():
    P = np.zeros((128, 128), np.float32)
    for jj in range(64):
        P[2 * jj, 2 * jj + 1] = 1.0
        P[2 * jj + 1, 2 * jj] = 1.0
    return P.astype(ml_dtypes.bfloat16)


_NC_CACHE = {}


def _get_nc(upto):
    if upto not in _NC_CACHE:
        _NC_CACHE[upto] = build_nc(upto)
    return _NC_CACHE[upto]


def make_in_maps(L0_attn_x, L0_attn_Wq, L0_attn_bq, L0_attn_Wk, L0_attn_bk,
                 L0_attn_Wv, L0_attn_bv, L0_attn_Wo, L0_attn_bo,
                 L0_ffn_w_gate, L0_ffn_w_up, L0_ffn_w_down):
    xx = np.asarray(L0_attn_x, np.float32)
    wqT = np.ascontiguousarray(np.asarray(L0_attn_Wq, np.float32).T)
    wkT = np.ascontiguousarray(np.asarray(L0_attn_Wk, np.float32).T)
    wvT = np.ascontiguousarray(np.asarray(L0_attn_Wv, np.float32).T)
    wo = np.asarray(L0_attn_Wo, np.float32)
    woT = np.ascontiguousarray(
        wo.T.reshape(H, HD, D).transpose(1, 0, 2).reshape(HD, H * D))
    wgT = np.ascontiguousarray(np.asarray(L0_ffn_w_gate, np.int32).T)
    wuT = np.ascontiguousarray(np.asarray(L0_ffn_w_up, np.int32).T)
    wdT = np.ascontiguousarray(np.asarray(L0_ffn_w_down, np.int32).T)
    bq = np.ascontiguousarray(np.asarray(L0_attn_bq, np.float32).reshape(DC, 128).T)
    bk = np.ascontiguousarray(np.asarray(L0_attn_bk, np.float32).reshape(DC, 128).T)
    bo = np.ascontiguousarray(np.asarray(L0_attn_bo, np.float32).reshape(DC, 128).T)
    bvr = np.asarray(L0_attn_bv, np.float32).reshape(1, D)
    psw = _pswap()
    in_maps = []
    for c in range(NCORE):
        b, hf = c // 2, c % 2
        xs = np.ascontiguousarray(xx[b, hf * T:(hf + 1) * T, :].T)
        Cq, Sq = _rope_tables(hf * T, 0.125)
        Ck, Sk = _rope_tables(hf * T, 1.0)
        ci = np.zeros((1, 8), np.int32)
        ci[0, 0] = hf * T
        ci[0, 1] = hf * T * 128
        ci[0, 2] = hf
        in_maps.append({
            "x": xs, "wqT": wqT, "wkT": wkT, "wvT": wvT, "woT": woT,
            "wgT": wgT, "wuT": wuT, "wdT": wdT,
            "bq": bq, "bk": bk, "bo": bo, "bvrow": bvr,
            "ropeCq": Cq, "ropeSq": Sq, "ropeCk": Ck, "ropeSk": Sk,
            "pswapT": psw, "cinfo": ci,
        })
    return in_maps


def kernel(**inputs):
    in_maps = make_in_maps(**inputs)
    nc = _get_nc("full")
    res = run_bass_kernel_spmd(nc, in_maps, core_ids=list(range(NCORE)))
    out = np.zeros((4, S, D), np.float32)
    for c in range(NCORE):
        b, hf = c // 2, c % 2
        out[b, hf * T:(hf + 1) * T, :] = np.asarray(res.results[c]["y"]).T
    return out


# version 09
# speedup vs baseline: 1.1062x; 1.0329x over previous
import numpy as np
import ml_dtypes
import concourse.bass as bass
import concourse.mybir as mybir
from concourse.bass_utils import run_bass_kernel_spmd

F32 = mybir.dt.float32
BF16 = mybir.dt.bfloat16
I32 = mybir.dt.int32
AF = mybir.ActivationFunctionType
ALU = mybir.AluOpType
AXX = mybir.AxisListType.X

D = 1024
DC = 8
T = 2048
S = 4096
H = 16
HD = 64
FF = 3072
FC = 24
NCORE = 8
ARENA = 100352

SCL_FFN = float(np.float32(0.5) * np.float32(1.0 / 127.0) * np.float32(1.0 / 127.0))
INV127 = float(np.float32(1.0 / 127.0))
EPS = 1e-5

O_X = 0
O_XN = 32768
O_HALL = 49152
O_QALL = 49152
O_WVT = 65536
O_RCQ, O_RSQ, O_RCK, O_RSK = 65536, 67584, 69632, 71680
O_KZ = (65536, 69632)
O_WG = (73728, 74752)
O_WU = (75776, 76800)
O_SG = (77824, 78848)
O_UP = (79872, 80896)
O_WD = (73728, 76800)
O_QRAW = 73728
O_KRAW = 75776
O_VOW = (77824, 79872)
O_WSL0 = 96256
O_VH = (73728, 77824)
O_XBF = (81920, 83968)
O_XSQ = (86016, 88064)
O_RAB = 94208
O_RBB = 96256
O_RTMP = 90112
O_RTM2 = 92160
O_QSQ = 94208
O_PA = (81920, 82944, 83968, 84992)
O_PB = (86016, 87040, 88064, 89088)
O_RROW = 90112
O_RROWB = 94208
O_OST = 96256
O_WSL1 = 98304
O_WO = (96256, 98304)
O_ROWA = 86016
O_ROWB = 90112


class Sem:
    def __init__(self, stack, nc, name):
        self.h = stack.enter_context(nc.semaphore(name))
        self.n = 0


class Eng:

    def __init__(self, name, chain_sem=None):
        self.name = name
        self.ops = []
        self.seen = {}
        self.chain = chain_sem

    def wait(self, sem, v=None):
        if v is None:
            v = sem.n
        if v <= 0 or self.seen.get(id(sem), -1) >= v:
            return
        self.seen[id(sem)] = v
        self.ops.append(("w", sem.h, v))

    def op(self, fn):
        if self.chain is not None:
            self.wait(self.chain, self.chain.n)
            self.chain.n += 1
            self.ops.append(("i", fn, self.chain.h, 1))
            return self.chain.n
        self.ops.append(("o", fn))

    def opi(self, fn, sem, k=1):
        if self.chain is not None:
            assert sem is self.chain and k == 1
            return self.op(fn)
        sem.n += k
        self.ops.append(("i", fn, sem.h, k))
        return sem.n

    def dma(self, fn, sem):
        sem.n += 16
        self.ops.append(("i", fn, sem.h, 16))
        return sem.n

    def replay(self):
        ops = self.ops

        def f(e):
            for it in ops:
                if it[0] == "w":
                    e.wait_ge(it[1], it[2])
                elif it[0] == "o":
                    it[1](e)
                else:
                    it[1](e).then_inc(it[2], it[3])
        return f


def build_nc(upto="full"):
    from contextlib import ExitStack
    nc = bass.Bass("TRN2", target_bir_lowering=False, num_devices=NCORE)

    dx = nc.dram_tensor("x", [D, T], F32, kind="ExternalInput").ap()
    dwqT = nc.dram_tensor("wqT", [D, D], F32, kind="ExternalInput").ap()
    dwkT = nc.dram_tensor("wkT", [D, D], F32, kind="ExternalInput").ap()
    dwvT = nc.dram_tensor("wvT", [D, D], F32, kind="ExternalInput").ap()
    dwoT = nc.dram_tensor("woT", [HD, H * D], F32, kind="ExternalInput").ap()
    dwgT = nc.dram_tensor("wgT", [D, FF], I32, kind="ExternalInput").ap()
    dwuT = nc.dram_tensor("wuT", [D, FF], I32, kind="ExternalInput").ap()
    dwdT = nc.dram_tensor("wdT", [FF, D], I32, kind="ExternalInput").ap()
    dbq = nc.dram_tensor("bq", [128, DC], F32, kind="ExternalInput").ap()
    dbk = nc.dram_tensor("bk", [128, DC], F32, kind="ExternalInput").ap()
    dbo = nc.dram_tensor("bo", [128, DC], F32, kind="ExternalInput").ap()
    dbv = nc.dram_tensor("bvrow", [1, D], F32, kind="ExternalInput").ap()
    dcq = nc.dram_tensor("ropeCq", [128, T], BF16, kind="ExternalInput").ap()
    dsq = nc.dram_tensor("ropeSq", [128, T], BF16, kind="ExternalInput").ap()
    dck = nc.dram_tensor("ropeCk", [128, T], BF16, kind="ExternalInput").ap()
    dsk = nc.dram_tensor("ropeSk", [128, T], BF16, kind="ExternalInput").ap()
    dpsw = nc.dram_tensor("pswapT", [128, 128], BF16, kind="ExternalInput").ap()
    dci = nc.dram_tensor("cinfo", [1, 8], I32, kind="ExternalInput").ap()
    dy = nc.dram_tensor("y", [D, T], F32, kind="ExternalOutput").ap()

    dKx = nc.dram_tensor("Kx", [D, S], BF16, kind="Internal",
                         addr_space="Shared").ap()
    dVx = nc.dram_tensor("Vx", [H, S, 128], BF16, kind="Internal",
                         addr_space="Shared").ap()
    dKM = nc.dram_tensor("KMx", [1, 2], F32, kind="Internal",
                         addr_space="Shared").ap()
    dXsp = nc.dram_tensor("Xsp", [D, T], F32, kind="Internal").ap()
    dccin = nc.dram_tensor("ccin", [1, 32], F32, kind="Internal").ap()
    dccout = nc.dram_tensor("ccout", [1, 32], F32, kind="Internal").ap()
    dccin2 = nc.dram_tensor("ccin2", [1, 32], F32, kind="Internal").ap()
    dccout2 = nc.dram_tensor("ccout2", [1, 32], F32, kind="Internal").ap()

    with ExitStack() as stk:
        en = stk.enter_context
        arena = en(nc.sbuf_tensor("arena", [128, ARENA], BF16))
        arh = arena[:].tensor
        ar32 = arh.bitcast(F32)

        def bf(off, n, p=128):
            return arh[0:p, off:off + n]

        def f32(off, n32, p=128):
            assert off % 2 == 0
            return ar32[0:p, off // 2: off // 2 + n32]

        X = f32(O_X, 16384)
        O_all = bf(O_X, 32768, p=64)
        O_128 = bf(O_X, 32768, p=128)
        XN = bf(O_XN, 16384)
        AOUT = bf(O_XN, 16384)
        HALL = bf(O_HALL, 24576)
        Q_all = bf(O_QALL, 16384)
        WVT = bf(O_WVT, 8192)
        RCQ, RSQ, RCK, RSK = (bf(O_RCQ, 2048), bf(O_RSQ, 2048),
                              bf(O_RCK, 2048), bf(O_RSK, 2048))
        KZb = [bf(o, 4096) for o in O_KZ]
        WGb = [bf(o, 1024) for o in O_WG]
        WUb = [bf(o, 1024) for o in O_WU]
        SGb = [bf(o, 1024) for o in O_SG]
        UPb = [bf(o, 1024) for o in O_UP]
        WDb = [bf(o, 3072) for o in O_WD]
        RAWK = [bf(O_QRAW, 2048), bf(O_KRAW, 2048)]
        VOWb = [bf(o, 2048) for o in O_VOW]
        WSLb = [bf(O_WSL0, 1024), bf(O_WSL1, 1024)]
        VHb = [bf(o, 4096) for o in O_VH]
        XBFb = [bf(o, 2048) for o in O_XBF]
        XSQb = [bf(o, 2048) for o in O_XSQ]
        rowAb = bf(O_RAB, 2048, p=1)
        rowBb = bf(O_RBB, 2048, p=1)
        RTMP = bf(O_RTMP, 2048)
        RTM2 = bf(O_RTM2, 2048)
        QSQ = bf(O_QSQ, 2048)
        PAb = [bf(o, 1024) for o in O_PA]
        PBb = [bf(o, 1024) for o in O_PB]
        RROW = f32(O_RROW, 2048)
        RROWB = bf(O_RROWB, 2048)
        OST = bf(O_OST, 2048)
        RBC = bf(O_RROW, 2048, p=64)
        Abc = bf(98304, 2048)
        Bbc = bf(O_RTM2, 2048)
        WOb = [bf(o, 2048, p=128) for o in O_WO]
        rowA = f32(O_ROWA, 2048, p=1)
        rowB = f32(O_ROWB, 2048, p=1)

        ones_t = en(nc.sbuf_tensor("ones_t", [128, 128], BF16))
        c128_t = en(nc.sbuf_tensor("c128_t", [128, 128], BF16))
        psw_t = en(nc.sbuf_tensor("psw_t", [128, 128], BF16))
        bqc = en(nc.sbuf_tensor("bqc", [128, DC], F32))
        bkc = en(nc.sbuf_tensor("bkc", [128, DC], F32))
        boc = en(nc.sbuf_tensor("boc", [128, DC], F32))
        bvrep = en(nc.sbuf_tensor("bvrep", [128, D], BF16))
        bvrow_s = en(nc.sbuf_tensor("bvrow_s", [1, D], BF16))
        negmh = en(nc.sbuf_tensor("negmh", [128, 1], F32))
        qk_s = en(nc.sbuf_tensor("qk_s", [1, 8], F32))
        qk_b = en(nc.sbuf_tensor("qk_b", [1, 2], BF16))
        km_s = en(nc.sbuf_tensor("km_s", [1, 2], F32))
        ci_s = en(nc.sbuf_tensor("ci_s", [1, 8], I32))
        cc_s = en(nc.sbuf_tensor("cc_s", [1, 32], F32))

        P0 = en(nc.psum_tensor("P0", [128, 1024], F32))
        P1 = en(nc.psum_tensor("P1", [128, 1024], F32))
        P2 = en(nc.psum_tensor("P2", [128, 1024], F32))
        P3 = en(nc.psum_tensor("P3", [128, 1024], F32))
        PS = [P0, P1, P2, P3]

        sXin = Sem(stk, nc, "sXin")
        sPre = Sem(stk, nc, "sPre")
        sRope = Sem(stk, nc, "sRope")
        sKMr = Sem(stk, nc, "sKMr")
        sXbk = Sem(stk, nc, "sXbk")
        sKP = [Sem(stk, nc, "sKP0"), Sem(stk, nc, "sKP1")]
        sVH = [Sem(stk, nc, "sVH0"), Sem(stk, nc, "sVH1")]
        sSp = Sem(stk, nc, "sSp")
        sSt = Sem(stk, nc, "sSt")
        sBv = Sem(stk, nc, "sBv")
        sWvt = Sem(stk, nc, "sWvt")
        sWg = [Sem(stk, nc, "sWg0"), Sem(stk, nc, "sWg1")]
        sWd = [Sem(stk, nc, "sWd0"), Sem(stk, nc, "sWd1")]
        sWsl = [Sem(stk, nc, "sWsl0"), Sem(stk, nc, "sWsl1")]
        sWo = [Sem(stk, nc, "sWo0"), Sem(stk, nc, "sWo1")]
        sGXv = [Sem(stk, nc, "sGXv0"), Sem(stk, nc, "sGXv1")]
        sGXk = [Sem(stk, nc, "sGXk0"), Sem(stk, nc, "sGXk1")]
        sGXm = Sem(stk, nc, "sGXm")
        sGXb = Sem(stk, nc, "sGXb")
        sGXd = Sem(stk, nc, "sGXd")
        sRbc = Sem(stk, nc, "sRbc")
        sXc = [Sem(stk, nc, f"sXc{c}") for c in range(DC)]
        sXr = [Sem(stk, nc, f"sXr{c}") for c in range(DC)]
        sAb = Sem(stk, nc, "sAb")
        sBb = Sem(stk, nc, "sBb")
        sXC2 = Sem(stk, nc, "sXC2")
        sCw = Sem(stk, nc, "sCw")
        sMM = Sem(stk, nc, "sMM")
        sVE = Sem(stk, nc, "sVE")
        sAC = Sem(stk, nc, "sAC")
        sXC = Sem(stk, nc, "sXC")

        rV = en(nc.gpsimd.register("rX"))
        rK = rV
        rM = rV

        sy = Eng("sync")
        tn = Eng("tensor")
        ve = Eng("vector", chain_sem=sVE)
        sc = Eng("scalar", chain_sem=sAC)
        gp = Eng("gpsimd")

        def mm(out, lhsT, rhs, first):
            def f(e, out=out, lhsT=lhsT, rhs=rhs, first=first):
                return e.matmul(out, lhsT, rhs, start=first, stop=True,
                                skip_group_check=(not first))
            return f

        for c in range(DC):
            eng_q = sy if c % 2 == 0 else gp
            eng_q.dma(lambda e, c=c: e.dma_start(
                out=X[:, c * T:(c + 1) * T],
                in_=dx[c * 128:(c + 1) * 128, :]), sXc[c])
        sy.dma(lambda e: e.dma_start(out=bqc[:], in_=dbq[:, :]), sPre)
        sy.dma(lambda e: e.dma_start(out=bkc[:], in_=dbk[:, :]), sPre)
        sy.dma(lambda e: e.dma_start(out=boc[:], in_=dbo[:, :]), sPre)
        sy.dma(lambda e: e.dma_start(out=psw_t[:], in_=dpsw[:, :]), sPre)
        sy.dma(lambda e: e.dma_start(out=ci_s[:], in_=dci[:, :]), sPre)
        ci_loaded = sPre.n

        gp.dma(lambda e: e.dma_start(out=bvrow_s[:], in_=dbv[:, :]), sBv)
        bv_loaded = sBv.n

        ve.op(lambda e: e.memset(ones_t[:], 1.0))
        ve.op(lambda e: e.memset(c128_t[:], 1.0 / 128.0))
        ve.op(lambda e: e.memset(cc_s[:], 1.0))
        consts_ready = ve.opi(lambda e: e.memset(qk_s[:], 0.0), sVE)

        gp.wait(sVE, consts_ready)
        cw = gp.dma(lambda e: e.dma_start(out=dccin2[:, :], in_=cc_s[0:1, :]),
                    sCw)
        gp.wait(sCw, cw)
        cc_warm = gp.opi(lambda e: e.collective_compute(
            "AllReduce", ALU.add, ins=[dccin2[:, :]], outs=[dccout2[:, :]],
            replica_groups=[list(range(NCORE))]), sXC2)

        def emit_ln(gates):
            prev = [0, 0]
            for c in range(DC):
                p = c % 2
                g_sem, g_val = gates[c]
                ve.wait(g_sem, g_val)
                ve.wait(sMM, prev[p])
                v_bf = ve.opi(lambda e, p=p, c=c: e.tensor_copy(
                    XBFb[p][:, :], X[:, c * T:(c + 1) * T]), sVE)
                sc.wait(g_sem, g_val)
                sc.wait(sMM, prev[p])
                a_sq = sc.opi(lambda e, p=p, c=c: e.activation(
                    out=XSQb[p][:, :], in_=X[:, c * T:(c + 1) * T],
                    func=AF.Square), sAC)
                tn.wait(sVE, v_bf)
                tn.wait(sAC, a_sq)
                for sj in range(4):
                    tn.op(mm(PS[sj // 2][:, (sj % 2) * 512:(sj % 2) * 512 + 512],
                             ones_t[:, 0:128], XBFb[p][:, sj * 512:sj * 512 + 512],
                             first=(c == 0)))
                for sj in range(3):
                    tn.op(mm(PS[2 + sj // 2][:, (sj % 2) * 512:(sj % 2) * 512 + 512],
                             ones_t[:, 0:128], XSQb[p][:, sj * 512:sj * 512 + 512],
                             first=(c == 0)))
                prev[p] = tn.opi(mm(PS[3][:, 512:1024], ones_t[:, 0:128],
                                    XSQb[p][:, 1536:2048], first=(c == 0)), sMM)
            ve.wait(sMM, sMM.n)
            ve.op(lambda e: e.tensor_scalar(out=rowB[:, 0:1024], in0=P0[0:1, :],
                                            scalar1=1.0 / D, scalar2=None,
                                            op0=ALU.mult))
            v = ve.opi(lambda e: e.tensor_scalar(
                out=rowB[:, 1024:2048], in0=P1[0:1, :], scalar1=1.0 / D,
                scalar2=None, op0=ALU.mult), sVE)
            ve.wait(sVE, v)
            v = ve.opi(lambda e: e.tensor_mul(rowA[:, :], rowB[:, :],
                                              rowB[:, :]), sVE)
            ve.wait(sVE, v)
            ve.op(lambda e: e.scalar_tensor_tensor(
                out=rowA[:, 0:1024], in0=P2[0:1, :], scalar=1.0 / D,
                in1=rowA[:, 0:1024], op0=ALU.mult, op1=ALU.subtract))
            v = ve.opi(lambda e: e.scalar_tensor_tensor(
                out=rowA[:, 1024:2048], in0=P3[0:1, :], scalar=1.0 / D,
                in1=rowA[:, 1024:2048], op0=ALU.mult, op1=ALU.subtract), sVE)
            ve.wait(sVE, v)
            v = ve.opi(lambda e: e.tensor_scalar(
                out=rowA[:, :], in0=rowA[:, :], scalar1=EPS, scalar2=None,
                op0=ALU.add), sVE)
            ve.wait(sVE, v)
            sc.wait(sVE, v)
            a_s = sc.opi(lambda e: e.activation(out=rowA[:, :], in_=rowA[:, :],
                                                func=AF.Ln), sAC)
            sc.wait(sAC, a_s)
            a_s = sc.opi(lambda e: e.activation(out=rowA[:, :], in_=rowA[:, :],
                                                func=AF.Exp, scale=-0.5), sAC)
            ve.wait(sAC, a_s)
            v = ve.opi(lambda e: e.scalar_tensor_tensor(
                out=rowB[:, :], in0=rowB[:, :], scalar=-1.0, in1=rowA[:, :],
                op0=ALU.mult, op1=ALU.mult), sVE)
            ve.wait(sVE, v)
            ve.op(lambda e: e.tensor_copy(rowAb[:, :], rowA[:, :]))
            rows_done = ve.opi(lambda e: e.tensor_copy(rowBb[:, :], rowB[:, :]),
                               sVE)
            tn.wait(sVE, rows_done)
            bc_mm = 0
            for sj in range(4):
                tn.op(mm(PS[sj // 2][:, (sj % 2) * 512:(sj % 2) * 512 + 512],
                         ones_t[0:1, 0:128], rowAb[:, sj * 512:sj * 512 + 512],
                         first=True))
            for sj in range(4):
                f = mm(PS[2 + sj // 2][:, (sj % 2) * 512:(sj % 2) * 512 + 512],
                       ones_t[0:1, 0:128], rowBb[:, sj * 512:sj * 512 + 512],
                       first=True)
                if sj == 3:
                    bc_mm = tn.opi(f, sMM)
                else:
                    tn.op(f)
            ve.wait(sMM, bc_mm)
            ve.op(lambda e: e.tensor_copy(Abc[:, 0:1024], P0[:, :]))
            ve.op(lambda e: e.tensor_copy(Abc[:, 1024:2048], P1[:, :]))
            ve.op(lambda e: e.tensor_copy(Bbc[:, 0:1024], P2[:, :]))
            return ve.opi(lambda e: e.tensor_copy(Bbc[:, 1024:2048],
                                                  P3[:, :]), sVE)

        def emit_apply(bcast):
            ve.wait(sVE, bcast)
            done = [[0, 0] for _ in range(DC)]
            for half in range(2):
                for c in range(DC):
                    v = ve.opi(lambda e, c=c, half=half: e.tensor_tensor(
                        RTMP[:, 0:1024],
                        X[:, c * T + half * 1024: c * T + half * 1024 + 1024],
                        Abc[:, half * 1024:half * 1024 + 1024], ALU.mult), sVE)
                    ve.wait(sVE, v)
                    done[c][half] = ve.opi(
                        lambda e, c=c, half=half: e.tensor_tensor(
                            XN[:, c * T + half * 1024:
                               c * T + half * 1024 + 1024],
                            RTMP[:, 0:1024],
                            Bbc[:, half * 1024:half * 1024 + 1024], ALU.add),
                        sVE)
            return done

        def emit_ffn(gates, w_init_mm=0):
            bcast = emit_ln(gates)
            apply_done = emit_apply(bcast)
            carry_mm = w_init_mm
            for half in range(2):
                wfree = [carry_mm, carry_mm]
                silu_d = [sAC.n, sAC.n]
                up_d = [sVE.n, sVE.n]
                g_mm = u_mm = 0
                for hc in range(FC):
                    p = hc % 2
                    gp.wait(sMM, wfree[p])
                    gp.dma(lambda e, p=p, hc=hc: e.dma_start(
                        out=WGb[p][:].rearrange("p (a k) -> p a k", k=128),
                        in_=dwgT.rearrange("(a p) f -> p a f", p=128)
                        [:, :, hc * 128:(hc + 1) * 128]), sWg[p])
                    w_rdy = gp.dma(lambda e, p=p, hc=hc: e.dma_start(
                        out=WUb[p][:].rearrange("p (a k) -> p a k", k=128),
                        in_=dwuT.rearrange("(a p) f -> p a f", p=128)
                        [:, :, hc * 128:(hc + 1) * 128]), sWg[p])
                    tn.wait(sWg[p], w_rdy)
                    tn.wait(sAC, silu_d[p])
                    tn.wait(sVE, up_d[p])
                    for dc in range(DC):
                        tn.wait(sVE, apply_done[dc][half])
                        for sj in range(2):
                            f = mm(PS[p][:, sj * 512:sj * 512 + 512],
                                   WGb[p][:, dc * 128:dc * 128 + 128],
                                   XN[:, dc * T + half * 1024 + sj * 512:
                                      dc * T + half * 1024 + sj * 512 + 512],
                                   first=(dc == 0))
                            if dc == DC - 1 and sj == 1:
                                g_mm = tn.opi(f, sMM)
                            else:
                                tn.op(f)
                    for dc in range(DC):
                        for sj in range(2):
                            f = mm(PS[2 + p][:, sj * 512:sj * 512 + 512],
                                   WUb[p][:, dc * 128:dc * 128 + 128],
                                   XN[:, dc * T + half * 1024 + sj * 512:
                                      dc * T + half * 1024 + sj * 512 + 512],
                                   first=(dc == 0))
                            if dc == DC - 1 and sj == 1:
                                u_mm = tn.opi(f, sMM)
                            else:
                                tn.op(f)
                    wfree[p] = u_mm
                    sc.wait(sMM, g_mm)
                    silu_d[p] = sc.opi(lambda e, p=p: e.activation(
                        out=SGb[p][:, :], in_=PS[p][:, :], func=AF.Silu,
                        scale=INV127), sAC)
                    ve.wait(sMM, u_mm)
                    v_up = ve.opi(lambda e, p=p: e.tensor_copy(
                        UPb[p][:, :], PS[2 + p][:, :]), sVE)
                    ve.wait(sAC, silu_d[p])
                    ve.wait(sVE, v_up)
                    up_d[p] = ve.opi(lambda e, p=p, hc=hc: e.tensor_tensor(
                        HALL[:, hc * 1024:(hc + 1) * 1024], SGb[p][:, :],
                        UPb[p][:, :], ALU.mult), sVE)
                h_done = sVE.n
                silu_all = sAC.n
                wd_free = [0, 0]
                ev_d = [sVE.n, sVE.n]
                d_mm = 0
                for ec in range(DC):
                    p = ec % 2
                    gp.wait(sMM, wd_free[p])
                    gp.wait(sVE, h_done)
                    wd_rdy = gp.dma(lambda e, p=p, ec=ec: e.dma_start(
                        out=WDb[p][:].rearrange("p (a k) -> p a k", k=128),
                        in_=dwdT.rearrange("(a p) f -> p a f", p=128)
                        [:, :, ec * 128:(ec + 1) * 128]), sWd[p])
                    tn.wait(sVE, h_done)
                    tn.wait(sAC, silu_all)
                    tn.wait(sWd[p], wd_rdy)
                    tn.wait(sVE, ev_d[p])
                    for hc in range(FC):
                        for sj in range(2):
                            f = mm(PS[p][:, sj * 512:sj * 512 + 512],
                                   WDb[p][:, hc * 128:hc * 128 + 128],
                                   HALL[:, hc * 1024 + sj * 512:
                                        hc * 1024 + sj * 512 + 512],
                                   first=(hc == 0))
                            if hc == FC - 1 and sj == 1:
                                d_mm = tn.opi(f, sMM)
                            else:
                                tn.op(f)
                    wd_free[p] = d_mm
                    carry_mm = d_mm
                    ve.wait(sMM, d_mm)
                    ev_d[p] = ve.opi(
                        lambda e, p=p, ec=ec, half=half: e.scalar_tensor_tensor(
                            out=X[:, ec * T + half * 1024:
                                  ec * T + half * 1024 + 1024],
                            in0=PS[p][:, :], scalar=SCL_FFN,
                            in1=X[:, ec * T + half * 1024:
                                  ec * T + half * 1024 + 1024],
                            op0=ALU.mult, op1=ALU.add), sVE)
            return sVE.n

        x1_done = emit_ffn([(sXc[c], 16) for c in range(DC)])

        def store_X(ready_val):
            sy.wait(sVE, ready_val)
            for c in range(DC):
                sy.dma(lambda e, c=c: e.dma_start(
                    out=dy[c * 128:(c + 1) * 128, :],
                    in_=X[:, c * T:(c + 1) * T]), sSt)
            sy.wait(sSt, sSt.n)

        def finish():
            blk = en(nc.Block())
            blk.sync(sy.replay())
            blk.tensor(tn.replay())
            blk.vector(ve.replay())
            blk.scalar(sc.replay())
            blk.gpsimd(gp.replay())
            return nc

        if upto == "ffn1":
            store_X(x1_done)
            return finish()

        ve.wait(sBv, bv_loaded)
        tn.wait(sBv, bv_loaded)
        tn.wait(sVE, consts_ready)
        tn.wait(sVE, x1_done)
        tn.op(mm(P0[:, 0:512], ones_t[0:1, 0:128], bvrow_s[0:1, 0:512], True))
        bv_mm = tn.opi(mm(P0[:, 512:1024], ones_t[0:1, 0:128],
                          bvrow_s[0:1, 512:1024], True), sMM)
        ve.wait(sMM, bv_mm)
        bvrep_done = ve.opi(lambda e: e.tensor_copy(bvrep[:, :], P0[:, :]), sVE)

        bcast2 = emit_ln([(sVE, x1_done)] * DC)
        apply2 = emit_apply(bcast2)
        xn2_done = apply2[DC - 1][1]

        for c in range(DC):
            sy.wait(sVE, apply2[c][1])
            sy.dma(lambda e, c=c: e.dma_start(
                out=dXsp[c * 128:(c + 1) * 128, :],
                in_=X[:, c * T:(c + 1) * T]), sSp)
        x_spilled = sSp.n

        gp.wait(sMM, sMM.n)
        for dc in range(DC):
            gp.dma(lambda e, dc=dc: e.dma_start(
                out=WVT[:, dc * 1024:(dc + 1) * 1024],
                in_=dwvT[dc * 128:(dc + 1) * 128, :]), sWvt)
        wv_ready = sWvt.n

        ve.op(lambda e: e.memset(VOWb[0][:, :], 0.0))
        ve.op(lambda e: e.memset(VOWb[1][:, :], 0.0))
        ve.op(lambda e: e.memset(
            VOWb[0][:].rearrange("p (t j) -> p t j", j=128)[:, :, 64:65], 1.0))
        vow_ones = ve.op(lambda e: e.memset(
            VOWb[1][:].rearrange("p (t j) -> p t j", j=128)[:, :, 64:65], 1.0))

        gp.wait(sPre, ci_loaded)
        gp.op(lambda e: e.reg_load(rV, ci_s[0:1, 1:2]))
        v_evac = [0, 0]
        vow_free = [0, 0]
        for tt in range(16):
            p = tt % 2
            pv = PS[1 + p]
            tn.wait(sWvt, wv_ready)
            tn.wait(sVE, bvrep_done)
            tn.wait(sVE, v_evac[p])
            for dc in range(DC):
                tn.wait(sVE, apply2[dc][tt // 8])
                for dj in range(2):
                    tn.op(mm(pv[:, dj * 512:dj * 512 + 512],
                             XN[:, dc * T + tt * 128: dc * T + tt * 128 + 128],
                             WVT[:, dc * 1024 + dj * 512:
                                 dc * 1024 + dj * 512 + 512],
                             first=(dc == 0)))
            tn.op(mm(pv[:, 0:512], c128_t[:, :], bvrep[:, 0:512], first=False))
            v_mm = tn.opi(mm(pv[:, 512:1024], c128_t[:, :], bvrep[:, 512:1024],
                             first=False), sMM)
            ve.wait(sMM, v_mm)
            ve.wait(sVE, vow_ones)
            ve.wait(sGXv[p], vow_free[p])
            v_evac[p] = ve.opi(lambda e, p=p, pv=pv: e.tensor_copy(
                VOWb[p][:].rearrange("p (t j) -> p t j", j=128)[:, :, 0:64],
                pv[:].rearrange("p (t j) -> p t j", j=64)), sVE)
            gp.wait(sVE, v_evac[p])
            vow_free[p] = gp.dma(lambda e, p=p: e.dma_start(
                out=bass.AP(dVx.tensor, rV,
                            [[128, 128], [S * 128, 16], [1, 128]]),
                in_=bass.AP(arh, O_VOW[p],
                            [[ARENA, 128], [128, 16], [1, 128]]),
                bounds_check=None), sGXv[p])
            if tt < 15:
                gp.op(lambda e: e.reg_add(rV, rV, 128 * 128))
        wvt_mm_done = sMM.n

        sy.wait(sMM, wvt_mm_done)
        for (d_t, s_t) in ((dcq, RCQ), (dsq, RSQ), (dck, RCK), (dsk, RSK)):
            sy.dma(lambda e, d_t=d_t, s_t=s_t: e.dma_start(
                out=s_t[:, :], in_=d_t[:, :]), sRope)
        ropes_ready = sRope.n

        def emit_qk(wT_d, bias_col, CT, STt, is_q, psin, export_cb=None):
            wfree = [psin, psin]
            psum_free = [sVE.n, sVE.n]
            slot_gx = [0, 0]
            slot_sq = [0, 0]
            qk_mm_v = [0] * DC
            sw_mm = nm_mm = 0

            def emit_proj(ecc):
                p = ecc % 2
                gp.wait(sMM, wfree[p])
                w_rdy = gp.dma(lambda e, p=p, ecc=ecc: e.dma_start(
                    out=WSLb[p][:].rearrange("p (a k) -> p a k", k=128),
                    in_=wT_d.rearrange("(a p) f -> p a f", p=128)
                    [:, :, ecc * 128:(ecc + 1) * 128]), sWsl[p])
                pq, pq2 = PS[2 * p], PS[2 * p + 1]
                tn.wait(sWsl[p], w_rdy)
                tn.wait(sRope, ropes_ready)
                tn.wait(sVE, psum_free[p])
                last = 0
                for dc in range(DC):
                    for sj in range(4):
                        pt = pq if sj < 2 else pq2
                        f = mm(pt[:, (sj % 2) * 512:(sj % 2) * 512 + 512],
                               WSLb[p][:, dc * 128:dc * 128 + 128],
                               XN[:, dc * T + sj * 512: dc * T + sj * 512 + 512],
                               first=(dc == 0))
                        if dc == DC - 1 and sj == 3:
                            last = tn.opi(f, sMM)
                        else:
                            tn.op(f)
                qk_mm_v[ecc] = last

            emit_proj(0)
            for ecc in range(DC):
                p = ecc % 2
                pq, pq2 = PS[2 * p], PS[2 * p + 1]
                ve.wait(sMM, qk_mm_v[ecc])
                ve.op(lambda e, pq=pq, ecc=ecc: e.tensor_scalar(
                    out=XBFb[0][:, 0:1024], in0=pq[:, :],
                    scalar1=bias_col[:, ecc:ecc + 1], scalar2=None,
                    op0=ALU.add))
                raw_done = ve.op(lambda e, pq2=pq2, ecc=ecc: e.tensor_scalar(
                    out=XBFb[0][:, 1024:2048], in0=pq2[:, :],
                    scalar1=bias_col[:, ecc:ecc + 1], scalar2=None,
                    op0=ALU.add))
                if ecc + 1 < DC:
                    emit_proj(ecc + 1)
                tn.wait(sVE, raw_done)
                for sj in range(4):
                    pt = pq if sj < 2 else pq2
                    f = mm(pt[:, (sj % 2) * 512:(sj % 2) * 512 + 512],
                           psw_t[:, :], XBFb[0][:, sj * 512:sj * 512 + 512],
                           first=True)
                    if sj == 3:
                        sw_mm = tn.opi(f, sMM)
                    else:
                        tn.op(f)
                ve.wait(sMM, sw_mm)
                ve.op(lambda e: e.tensor_tensor(
                    RTMP[:, :], XBFb[0][:, :], CT[:, :], ALU.mult))
                ve.op(lambda e, pq=pq: e.tensor_tensor(
                    RTM2[:, 0:1024], pq[:, :], STt[:, 0:1024], ALU.mult))
                v2 = ve.op(lambda e, pq2=pq2: e.tensor_tensor(
                    RTM2[:, 1024:2048], pq2[:, :], STt[:, 1024:2048],
                    ALU.mult))
                ve.wait(sVE, v2)
                if is_q:
                    dst = Q_all[:, ecc * T:(ecc + 1) * T]
                else:
                    dst = RAWK[p][:, :]
                    ve.wait(sGXk[0], slot_gx[p])
                    ve.wait(sAC, slot_sq[p])
                rope_done = ve.op(lambda e, dst=dst: e.tensor_tensor(
                    dst, RTMP[:, :], RTM2[:, :], ALU.add))
                if export_cb is not None:
                    ex_v = export_cb(ecc, rope_done)
                    if ex_v:
                        slot_gx[0] = slot_gx[1] = ex_v
                sc.wait(sVE, rope_done)
                sq_d = sc.op(lambda e, dst=dst: e.activation(
                    out=QSQ[:, :], in_=dst, func=AF.Square))
                if not is_q:
                    slot_sq[p] = sq_d
                tn.wait(sAC, sq_d)
                tn.wait(sVE, rope_done)
                for sj in range(4):
                    pt = pq if sj < 2 else pq2
                    f = mm(pt[:, (sj % 2) * 512:(sj % 2) * 512 + 512],
                           ones_t[:, 0:128], QSQ[:, sj * 512:sj * 512 + 512],
                           first=True)
                    if sj == 3:
                        nm_mm = tn.opi(f, sMM)
                    else:
                        tn.op(f)
                wfree[p] = nm_mm
                slot = 0 if is_q else 1
                ve.wait(sMM, nm_mm)
                v = ve.op(lambda e, pq=pq: e.reduce_max(
                    qk_s[0:1, 4:5], pq[0:1, :], axis=AXX))
                ve.wait(sVE, v)
                v = ve.op(lambda e, pq2=pq2: e.reduce_max(
                    qk_s[0:1, 5:6], pq2[0:1, :], axis=AXX))
                ve.wait(sVE, v)
                v = ve.op(lambda e: e.tensor_tensor(
                    qk_s[0:1, 4:5], qk_s[0:1, 4:5], qk_s[0:1, 5:6], ALU.max))
                ve.wait(sVE, v)
                v = ve.op(lambda e, slot=slot: e.tensor_tensor(
                    qk_s[0:1, slot:slot + 1], qk_s[0:1, slot:slot + 1],
                    qk_s[0:1, 4:5], ALU.max))
                psum_free[p] = v
            return sVE.n

        q_done = emit_qk(dwqT, bqc, RCQ, RSQ, True, psin=wvt_mm_done)

        gp.op(lambda e: e.reg_load(rK, ci_s[0:1, 0:1]))
        kx_first = [True]

        def k_export(ecc, rope_done):
            if ecc % 2 == 0:
                return 0
            gp.wait(sVE, rope_done)
            if not kx_first[0]:
                gp.op(lambda e: e.reg_add(rK, rK, 2 * 128 * S))
            kx_first[0] = False
            return gp.dma(lambda e: e.dma_start(
                out=bass.AP(dKx.tensor, rK,
                            [[S, 128], [128 * S, 2], [1, 2048]]),
                in_=bass.AP(arh, O_QRAW,
                            [[ARENA, 128], [2048, 2], [1, 2048]]),
                bounds_check=None), sGXk[0])

        kmax_done = emit_qk(dwkT, bkc, RCK, RSK, False, psin=sMM.n,
                            export_cb=k_export)

        gp.op(lambda e: e.reg_load(rM, ci_s[0:1, 2:3]))
        gp.wait(sVE, kmax_done)
        gp.dma(lambda e: e.dma_start(
            out=bass.AP(dKM.tensor, rM, [[1, 1], [1, 1]]),
            in_=qk_s[0:1, 1:2], bounds_check=None), sGXm)

        for s_ in (sGXv[0], sGXv[1], sGXk[0], sGXm):
            gp.wait(s_, s_.n)
        gp.dma(lambda e: e.dma_start(out=dccin[:, :], in_=cc_s[0:1, :]), sGXb)
        gp.wait(sGXb, sGXb.n)
        gp.wait(sXC2, cc_warm)
        barrier = gp.opi(lambda e: e.collective_compute(
            "AllReduce", ALU.add, ins=[dccin[:, :]], outs=[dccout[:, :]],
            replica_groups=[list(range(NCORE))]), sXC)

        if upto == "qkv":
            gp.wait(sXC, barrier)
            for c in range(DC):
                gp.dma(lambda e, c=c: e.dma_start(
                    out=dy[c * 128:(c + 1) * 128, :],
                    in_=Q_all[:, c * T:(c + 1) * T]), sGXd)
            gp.wait(sGXd, sGXd.n)
            return finish()

        ve.wait(sVE, kmax_done)
        ve.wait(sSp, x_spilled)
        ve.op(lambda e: e.memset(KZb[0][:, :], 0.0))
        kz_zeroed = ve.op(lambda e: e.memset(KZb[1][:, :], 0.0))
        ve.op(lambda e: e.memset(O_128[64:128, 0:16384], 0.0))
        ve.op(lambda e: e.memset(O_128[64:128, 16384:32768], 0.0))
        ve.op(lambda e: e.memset(WOb[0][64:128, :], 0.0))
        wo_z = ve.op(lambda e: e.memset(WOb[1][64:128, :], 0.0))

        sy.wait(sXC, barrier)
        km_ready = sy.dma(lambda e: e.dma_start(out=km_s[:], in_=dKM[:, :]), sKMr)
        ve.wait(sKMr, km_ready)
        v = ve.opi(lambda e: e.reduce_max(qk_s[0:1, 6:7], km_s[0:1, 0:2],
                                          axis=AXX), sVE)
        ve.wait(sVE, v)
        v = ve.opi(lambda e: e.tensor_mul(qk_s[0:1, 7:8], qk_s[0:1, 0:1],
                                          qk_s[0:1, 6:7]), sVE)
        sc.wait(sVE, v)
        a_v = sc.opi(lambda e: e.activation(out=qk_s[0:1, 7:8],
                                            in_=qk_s[0:1, 7:8], func=AF.Ln),
                     sAC)
        sc.wait(sAC, a_v)
        a_v = sc.opi(lambda e: e.activation(out=qk_s[0:1, 7:8],
                                            in_=qk_s[0:1, 7:8], func=AF.Exp,
                                            scale=0.5), sAC)
        ve.wait(sAC, a_v)
        v_qb = ve.opi(lambda e: e.tensor_copy(qk_b[0:1, 0:1], qk_s[0:1, 7:8]),
                      sVE)
        tn.wait(sVE, v_qb)
        tn.wait(sVE, kmax_done)
        mh_mm = tn.opi(mm(P0[:, 0:1], ones_t[0:1, 0:128], qk_b[0:1, 0:1],
                          True), sMM)
        ve.wait(sMM, mh_mm)
        negmh_done = ve.opi(lambda e: e.tensor_scalar(
            out=negmh[:, 0:1], in0=P0[:, 0:1], scalar1=-1.0, scalar2=None,
            op0=ALU.mult), sVE)

        if upto == "mhat":
            gp.wait(sVE, negmh_done)
            for c in range(DC):
                gp.dma(lambda e, c=c: e.dma_start(
                    out=dy[c * 128:(c + 1) * 128, :],
                    in_=Q_all[:, c * T:(c + 1) * T]), sGXd)
            gp.wait(sGXd, sGXd.n)
            return finish()

        LAG = 3
        kp_ready = [0, 0]
        kp_free = [0, 0]
        vh_ready = [0, 0]
        vh_free = [0, 0]
        pab_free = [0, 0, 0, 0]
        pbb_free = [0, 0, 0, 0]
        pv_free = negmh_done
        expA_prev = 0
        expB_prev = 0
        p01_init = negmh_done

        for pr in range(DC):
            sy.wait(sXC, barrier)
            sy.wait(sVE, kz_zeroed)
            sy.wait(sMM, kp_free[0])
            sy.dma(lambda e, pr=pr: e.dma_start(
                out=KZb[0][0:64, :], in_=dKx[pr * 128:pr * 128 + 64, :]),
                sKP[0])
            kp_ready[0] = sy.dma(lambda e, pr=pr: e.dma_start(
                out=KZb[1][64:128, :],
                in_=dKx[pr * 128 + 64:pr * 128 + 128, :]), sKP[0])
            kp_ready[1] = kp_ready[0]
            for hh in range(2):
                h = 2 * pr + hh
                vb = h % 2
                base = 64 * hh
                sy.wait(sMM, vh_free[vb])
                vh_ready[vb] = sy.dma(lambda e, vb=vb, h=h: e.dma_start(
                    out=VHb[vb][:].rearrange("p (t j) -> p t j", j=128),
                    in_=dVx[h].rearrange("(t p) j -> p t j", p=128)), sVH[vb])
                KZ = KZb[hh]
                expA = [0] * 32
                expB = [0] * 32
                for tc in range(32):
                    tn.wait(sKP[0], kp_ready[0])
                    tn.wait(sVH[vb], vh_ready[vb])
                    tn.wait(sVE, p01_init)
                    tn.wait(sAC, expA[tc - 1] if tc > 0 else expA_prev)
                    tn.op(mm(P0[:, 0:512],
                             KZ[:, tc * 128:tc * 128 + 128],
                             Q_all[:, pr * T: pr * T + 512],
                             first=True))
                    scA = tn.opi(mm(P0[:, 512:1024],
                                    KZ[:, tc * 128:tc * 128 + 128],
                                    Q_all[:, pr * T + 512: pr * T + 1024],
                                    first=True), sMM)
                    tn.wait(sAC, expB[tc - 1] if tc > 0 else expB_prev)
                    tn.op(mm(P1[:, 0:512],
                             KZ[:, tc * 128:tc * 128 + 128],
                             Q_all[:, pr * T + 1024: pr * T + 1536],
                             first=True))
                    scB = tn.opi(mm(P1[:, 512:1024],
                                    KZ[:, tc * 128:tc * 128 + 128],
                                    Q_all[:, pr * T + 1536: pr * T + 2048],
                                    first=True), sMM)
                    sc.wait(sMM, scA)
                    sc.wait(sVE, negmh_done)
                    sc.wait(sMM, pab_free[tc % 4])
                    expA[tc] = sc.opi(lambda e, tc=tc: e.activation(
                        out=PAb[tc % 4][:, :], in_=P0[:, :], func=AF.Exp,
                        bias=negmh[:, 0:1]), sAC)
                    sc.wait(sMM, scB)
                    sc.wait(sMM, pbb_free[tc % 4])
                    expB[tc] = sc.opi(lambda e, tc=tc: e.activation(
                        out=PBb[tc % 4][:, :], in_=P1[:, :], func=AF.Exp,
                        bias=negmh[:, 0:1]), sAC)
                    if tc >= LAG:
                        tv = tc - LAG
                        tn.wait(sAC, expA[tv])
                        if tv == 0:
                            tn.wait(sVE, pv_free)
                        tn.op(mm(P2[:, 0:512],
                                 VHb[vb][:, tv * 128:tv * 128 + 128],
                                 PAb[tv % 4][:, 0:512], first=(tv == 0)))
                        pab_free[tv % 4] = tn.opi(
                            mm(P2[:, 512:1024],
                               VHb[vb][:, tv * 128:tv * 128 + 128],
                               PAb[tv % 4][:, 512:1024], first=(tv == 0)), sMM)
                        tn.wait(sAC, expB[tv])
                        tn.op(mm(P3[:, 0:512],
                                 VHb[vb][:, tv * 128:tv * 128 + 128],
                                 PBb[tv % 4][:, 0:512], first=(tv == 0)))
                        pbb_free[tv % 4] = tn.opi(
                            mm(P3[:, 512:1024],
                               VHb[vb][:, tv * 128:tv * 128 + 128],
                               PBb[tv % 4][:, 512:1024], first=(tv == 0)), sMM)
                for tv in range(32 - LAG, 32):
                    tn.wait(sAC, expA[tv])
                    tn.op(mm(P2[:, 0:512], VHb[vb][:, tv * 128:tv * 128 + 128],
                             PAb[tv % 4][:, 0:512], first=(tv == 0)))
                    pab_free[tv % 4] = tn.opi(
                        mm(P2[:, 512:1024], VHb[vb][:, tv * 128:tv * 128 + 128],
                           PAb[tv % 4][:, 512:1024], first=(tv == 0)), sMM)
                    tn.wait(sAC, expB[tv])
                    tn.op(mm(P3[:, 0:512], VHb[vb][:, tv * 128:tv * 128 + 128],
                             PBb[tv % 4][:, 0:512], first=(tv == 0)))
                    pbb_free[tv % 4] = tn.opi(
                        mm(P3[:, 512:1024], VHb[vb][:, tv * 128:tv * 128 + 128],
                           PBb[tv % 4][:, 512:1024], first=(tv == 0)), sMM)
                pv_last = sMM.n
                vh_free[vb] = pv_last
                if hh == 1:
                    kp_free[0] = scB
                ve.wait(sMM, pv_last)
                ve.wait(sSp, x_spilled)
                ve.op(lambda e: e.tensor_copy(OST[0:64, 0:1024], P2[0:64, :]))
                ve.op(lambda e: e.tensor_copy(OST[0:64, 1024:2048],
                                              P3[0:64, :]))
                ve.op(lambda e: e.tensor_copy(RROW[64:65, 0:1024],
                                              P2[64:65, :]))
                pv_free = ve.op(lambda e: e.tensor_copy(
                    RROW[64:65, 1024:2048], P3[64:65, :]))
                if h == 15:
                    sc.wait(sVE, pv_free)
                    sc.op(lambda e: e.activation(out=RROW[64:65, :],
                                                 in_=RROW[64:65, :],
                                                 func=AF.Ln))
                    r_sc = sc.opi(lambda e: e.activation(
                        out=RROW[64:65, :], in_=RROW[64:65, :],
                        func=AF.Exp, scale=-1.0), sAC)
                    ve.wait(sAC, r_sc)
                else:
                    ve.op(lambda e: e.reciprocal(RROW[64:65, :],
                                                 RROW[64:65, :]))
                rb_rdy = ve.op(lambda e: e.tensor_copy(RROWB[64:65, :],
                                                       RROW[64:65, :]))
                gp.wait(sVE, rb_rdy)
                rbc_rdy = gp.dma(lambda e: e.dma_start(
                    out=RBC[:, :],
                    in_=bass.AP(arh, 64 * ARENA + O_RROWB,
                                [[ARENA, 1], [0, 64], [1, 2048]])), sRbc)
                ve.wait(sRbc, rbc_rdy)
                ve.op(lambda e, h=h: e.tensor_tensor(
                    O_all[0:64, h * T: h * T + 1024],
                    OST[0:64, 0:1024], RBC[:, 0:1024], ALU.mult))
                ve.op(lambda e, h=h: e.tensor_tensor(
                    O_all[0:64, h * T + 1024: h * T + 2048],
                    OST[0:64, 1024:2048], RBC[:, 1024:2048], ALU.mult))
                expA_prev = expA[31]
                expB_prev = expB[31]
        attn_done = sVE.n
        vh_last_mm = sMM.n

        if upto in ("attn", "attn1"):
            gp.wait(sVE, attn_done)
            for hh2 in range(H):
                gp.dma(lambda e, hh2=hh2: e.dma_start(
                    out=dy[hh2 * 64:(hh2 + 1) * 64, :],
                    in_=O_all[0:64, hh2 * T:(hh2 + 1) * T]), sGXd)
            gp.wait(sGXd, sGXd.n)
            return finish()

        gp.wait(sVE, wo_z)
        wo_free = [0, 0]
        ev_free = [0, 0, 0, 0]
        wo_last_mm = 0
        for ecc in range(DC):
            p = ecc % 2
            gp.wait(sXC, barrier)
            gp.wait(sMM, wo_free[p])
            if ecc < 2:
                gp.wait(sMM, vh_last_mm)
                gp.wait(sVE, attn_done)
            wo_rdy = gp.dma(lambda e, p=p, ecc=ecc: e.dma_start(
                out=WOb[p][0:64, :].rearrange("p (hh k) -> p hh k", k=128),
                in_=dwoT.rearrange("p (hh ee) -> p hh ee", ee=D)
                [:, :, ecc * 128:(ecc + 1) * 128]), sWo[p])
            tn.wait(sWo[p], wo_rdy)
            tn.wait(sVE, attn_done)
            for qj in range(4):
                pt = PS[2 * p + qj // 2]
                sl = (qj % 2) * 512
                tn.wait(sVE, ev_free[2 * p + qj // 2])
                w_mm = 0
                for hh in range(H):
                    f = mm(pt[:, sl:sl + 512],
                           WOb[p][0:128, hh * 128:hh * 128 + 128],
                           O_128[0:128,
                                 hh * T + qj * 512: hh * T + qj * 512 + 512],
                           first=(hh == 0))
                    if hh == H - 1:
                        w_mm = tn.opi(f, sMM)
                    else:
                        tn.op(f)
                wo_free[p] = w_mm
                wo_last_mm = w_mm
                ve.wait(sMM, w_mm)
                ev_free[2 * p + qj // 2] = ve.opi(
                    lambda e, ecc=ecc, qj=qj, pt=pt, sl=sl: e.tensor_scalar(
                        out=AOUT[:, ecc * T + qj * 512:
                                 ecc * T + qj * 512 + 512],
                        in0=pt[:, sl:sl + 512], scalar1=boc[:, ecc:ecc + 1],
                        scalar2=None, op0=ALU.add), sVE)
        aout_done = sVE.n

        sy.wait(sMM, wo_last_mm)
        sy.wait(sSp, x_spilled)
        gp.wait(sMM, wo_last_mm)
        gp.wait(sSp, x_spilled)
        for c in range(DC):
            eng_q = sy if c % 2 == 0 else gp
            eng_q.dma(lambda e, c=c: e.dma_start(
                out=X[:, c * T:(c + 1) * T],
                in_=dXsp[c * 128:(c + 1) * 128, :]), sXr[c])
        ve.wait(sVE, aout_done)
        x2v = [0] * DC
        for c in range(DC):
            ve.wait(sXr[c], 16)
            x2v[c] = ve.opi(lambda e, c=c: e.tensor_tensor(
                X[:, c * T:(c + 1) * T], X[:, c * T:(c + 1) * T],
                AOUT[:, c * T:(c + 1) * T], ALU.add), sVE)
        x2_done = x2v[DC - 1]

        if upto == "wo":
            store_X(x2_done)
            return finish()

        x3_done = emit_ffn([(sVE, x2v[c]) for c in range(DC)],
                           w_init_mm=wo_last_mm)
        store_X(x3_done)
        return finish()


def _rope_tables(pos0, scale):
    j = np.arange(32)
    f = (10000.0 ** (-j / 32.0)).astype(np.float64)
    t = (pos0 + np.arange(T)).astype(np.float64)
    ang = np.outer(f, t)
    c32 = np.cos(ang)
    s32 = np.sin(ang)
    Cq = np.zeros((64, T))
    Sq = np.zeros((64, T))
    Cq[0::2] = c32
    Cq[1::2] = c32
    Sq[0::2] = -s32
    Sq[1::2] = s32
    Cf = (np.tile(Cq, (2, 1)) * scale).astype(ml_dtypes.bfloat16)
    Sf = (np.tile(Sq, (2, 1)) * scale).astype(ml_dtypes.bfloat16)
    return Cf, Sf


def _pswap():
    P = np.zeros((128, 128), np.float32)
    for jj in range(64):
        P[2 * jj, 2 * jj + 1] = 1.0
        P[2 * jj + 1, 2 * jj] = 1.0
    return P.astype(ml_dtypes.bfloat16)


_NC_CACHE = {}


def _get_nc(upto):
    if upto not in _NC_CACHE:
        _NC_CACHE[upto] = build_nc(upto)
    return _NC_CACHE[upto]


def make_in_maps(L0_attn_x, L0_attn_Wq, L0_attn_bq, L0_attn_Wk, L0_attn_bk,
                 L0_attn_Wv, L0_attn_bv, L0_attn_Wo, L0_attn_bo,
                 L0_ffn_w_gate, L0_ffn_w_up, L0_ffn_w_down):
    xx = np.asarray(L0_attn_x, np.float32)
    wqT = np.ascontiguousarray(np.asarray(L0_attn_Wq, np.float32).T)
    wkT = np.ascontiguousarray(np.asarray(L0_attn_Wk, np.float32).T)
    wvT = np.ascontiguousarray(np.asarray(L0_attn_Wv, np.float32).T)
    wo = np.asarray(L0_attn_Wo, np.float32)
    woT = np.ascontiguousarray(
        wo.T.reshape(H, HD, D).transpose(1, 0, 2).reshape(HD, H * D))
    wgT = np.ascontiguousarray(np.asarray(L0_ffn_w_gate, np.int32).T)
    wuT = np.ascontiguousarray(np.asarray(L0_ffn_w_up, np.int32).T)
    wdT = np.ascontiguousarray(np.asarray(L0_ffn_w_down, np.int32).T)
    bq = np.ascontiguousarray(np.asarray(L0_attn_bq, np.float32).reshape(DC, 128).T)
    bk = np.ascontiguousarray(np.asarray(L0_attn_bk, np.float32).reshape(DC, 128).T)
    bo = np.ascontiguousarray(np.asarray(L0_attn_bo, np.float32).reshape(DC, 128).T)
    bvr = np.asarray(L0_attn_bv, np.float32).reshape(1, D)
    psw = _pswap()
    in_maps = []
    for c in range(NCORE):
        b, hf = c // 2, c % 2
        xs = np.ascontiguousarray(xx[b, hf * T:(hf + 1) * T, :].T)
        Cq, Sq = _rope_tables(hf * T, 0.125)
        Ck, Sk = _rope_tables(hf * T, 1.0)
        ci = np.zeros((1, 8), np.int32)
        ci[0, 0] = hf * T
        ci[0, 1] = hf * T * 128
        ci[0, 2] = hf
        in_maps.append({
            "x": xs, "wqT": wqT, "wkT": wkT, "wvT": wvT, "woT": woT,
            "wgT": wgT, "wuT": wuT, "wdT": wdT,
            "bq": bq, "bk": bk, "bo": bo, "bvrow": bvr,
            "ropeCq": Cq, "ropeSq": Sq, "ropeCk": Ck, "ropeSk": Sk,
            "pswapT": psw, "cinfo": ci,
        })
    return in_maps


def kernel(**inputs):
    in_maps = make_in_maps(**inputs)
    nc = _get_nc("full")
    res = run_bass_kernel_spmd(nc, in_maps, core_ids=list(range(NCORE)))
    out = np.zeros((4, S, D), np.float32)
    for c in range(NCORE):
        b, hf = c // 2, c % 2
        out[b, hf * T:(hf + 1) * T, :] = np.asarray(res.results[c]["y"]).T
    return out


# version 10 (final)
# speedup vs baseline: 1.1586x; 1.0474x over previous
import numpy as np
import ml_dtypes
import concourse.bass as bass
import concourse.mybir as mybir
from concourse.bass_utils import run_bass_kernel_spmd

F32 = mybir.dt.float32
BF16 = mybir.dt.bfloat16
I32 = mybir.dt.int32
AF = mybir.ActivationFunctionType
ALU = mybir.AluOpType
AXX = mybir.AxisListType.X

D = 1024
DC = 8
T = 2048
S = 4096
H = 16
HD = 64
FF = 3072
FC = 24
NCORE = 8
ARENA = 100352

SCL_FFN = float(np.float32(0.5) * np.float32(1.0 / 127.0) * np.float32(1.0 / 127.0))
INV127 = float(np.float32(1.0 / 127.0))
EPS = 1e-5

O_X = 0
O_XN = 32768
O_HALL = 49152
O_QALL = 49152
O_WVT = 65536
O_RCQ, O_RSQ, O_RCK, O_RSK = 65536, 67584, 69632, 71680
O_KZ = (65536, 69632)
O_WG = (73728, 74752)
O_WU = (75776, 76800)
O_SG = (77824, 78848)
O_UP = (79872, 80896)
O_WD = (73728, 76800)
O_QRAW = 73728
O_KRAW = 75776
O_VOW = (77824, 79872)
O_WSL0 = 96256
O_VH = (73728, 77824)
O_XBF = (81920, 83968)
O_XSQ = (86016, 88064)
O_RAB = 94208
O_RBB = 96256
O_RTMP = 90112
O_RTM2 = 92160
O_QSQ = 94208
O_PA = (81920, 82944, 83968, 84992)
O_PB = (86016, 87040, 88064, 89088)
O_RROW = 90112
O_RROWB = 94208
O_OST = 96256
O_WSL1 = 98304
O_WO = (96256, 98304)
O_ROWA = 86016
O_ROWB = 90112


class Sem:
    def __init__(self, stack, nc, name):
        self.h = stack.enter_context(nc.semaphore(name))
        self.n = 0


class Eng:

    def __init__(self, name, chain_sem=None):
        self.name = name
        self.ops = []
        self.seen = {}
        self.chain = chain_sem

    def wait(self, sem, v=None):
        if v is None:
            v = sem.n
        if v <= 0 or self.seen.get(id(sem), -1) >= v:
            return
        self.seen[id(sem)] = v
        self.ops.append(("w", sem.h, v))

    def op(self, fn):
        if self.chain is not None:
            self.wait(self.chain, self.chain.n)
            self.chain.n += 1
            self.ops.append(("i", fn, self.chain.h, 1))
            return self.chain.n
        self.ops.append(("o", fn))

    def opi(self, fn, sem, k=1):
        if self.chain is not None:
            assert sem is self.chain and k == 1
            return self.op(fn)
        sem.n += k
        self.ops.append(("i", fn, sem.h, k))
        return sem.n

    def dma(self, fn, sem):
        sem.n += 16
        self.ops.append(("i", fn, sem.h, 16))
        return sem.n

    def replay(self):
        ops = self.ops

        def f(e):
            for it in ops:
                if it[0] == "w":
                    e.wait_ge(it[1], it[2])
                elif it[0] == "o":
                    it[1](e)
                else:
                    it[1](e).then_inc(it[2], it[3])
        return f


def build_nc(upto="full"):
    from contextlib import ExitStack
    nc = bass.Bass("TRN2", target_bir_lowering=False, num_devices=NCORE)

    dx = nc.dram_tensor("x", [D, T], F32, kind="ExternalInput").ap()
    dwqT = nc.dram_tensor("wqT", [D, D], F32, kind="ExternalInput").ap()
    dwkT = nc.dram_tensor("wkT", [D, D], F32, kind="ExternalInput").ap()
    dwvT = nc.dram_tensor("wvT", [D, D], F32, kind="ExternalInput").ap()
    dwoT = nc.dram_tensor("woT", [HD, H * D], F32, kind="ExternalInput").ap()
    dwgT = nc.dram_tensor("wgT", [D, FF], I32, kind="ExternalInput").ap()
    dwuT = nc.dram_tensor("wuT", [D, FF], I32, kind="ExternalInput").ap()
    dwdT = nc.dram_tensor("wdT", [FF, D], I32, kind="ExternalInput").ap()
    dbq = nc.dram_tensor("bq", [128, DC], F32, kind="ExternalInput").ap()
    dbk = nc.dram_tensor("bk", [128, DC], F32, kind="ExternalInput").ap()
    dbo = nc.dram_tensor("bo", [128, DC], F32, kind="ExternalInput").ap()
    dbv = nc.dram_tensor("bvrow", [1, D], F32, kind="ExternalInput").ap()
    dcq = nc.dram_tensor("ropeCq", [128, T], BF16, kind="ExternalInput").ap()
    dsq = nc.dram_tensor("ropeSq", [128, T], BF16, kind="ExternalInput").ap()
    dck = nc.dram_tensor("ropeCk", [128, T], BF16, kind="ExternalInput").ap()
    dsk = nc.dram_tensor("ropeSk", [128, T], BF16, kind="ExternalInput").ap()
    dpsw = nc.dram_tensor("pswapT", [128, 128], BF16, kind="ExternalInput").ap()
    dci = nc.dram_tensor("cinfo", [1, 8], I32, kind="ExternalInput").ap()
    dy = nc.dram_tensor("y", [D, T], F32, kind="ExternalOutput").ap()

    dKx = nc.dram_tensor("Kx", [D, S], BF16, kind="Internal",
                         addr_space="Shared").ap()
    dVx = nc.dram_tensor("Vx", [H, S, 128], BF16, kind="Internal",
                         addr_space="Shared").ap()
    dKM = nc.dram_tensor("KMx", [1, 2], F32, kind="Internal",
                         addr_space="Shared").ap()
    dXsp = nc.dram_tensor("Xsp", [D, T], F32, kind="Internal").ap()
    dccin = nc.dram_tensor("ccin", [1, 32], F32, kind="Internal").ap()
    dccout = nc.dram_tensor("ccout", [1, 32], F32, kind="Internal").ap()
    dccin2 = nc.dram_tensor("ccin2", [1, 32], F32, kind="Internal").ap()
    dccout2 = nc.dram_tensor("ccout2", [1, 32], F32, kind="Internal").ap()

    with ExitStack() as stk:
        en = stk.enter_context
        arena = en(nc.sbuf_tensor("arena", [128, ARENA], BF16))
        arh = arena[:].tensor
        ar32 = arh.bitcast(F32)

        def bf(off, n, p=128):
            return arh[0:p, off:off + n]

        def f32(off, n32, p=128):
            assert off % 2 == 0
            return ar32[0:p, off // 2: off // 2 + n32]

        X = f32(O_X, 16384)
        O_all = bf(O_X, 32768, p=64)
        O_128 = bf(O_X, 32768, p=128)
        XN = bf(O_XN, 16384)
        AOUT = bf(O_XN, 16384)
        HALL = bf(O_HALL, 24576)
        Q_all = bf(O_QALL, 16384)
        WVT = bf(O_WVT, 8192)
        RCQ, RSQ, RCK, RSK = (bf(O_RCQ, 2048), bf(O_RSQ, 2048),
                              bf(O_RCK, 2048), bf(O_RSK, 2048))
        KZb = [bf(o, 4096) for o in O_KZ]
        WGb = [bf(o, 1024) for o in O_WG]
        WUb = [bf(o, 1024) for o in O_WU]
        SGb = [bf(o, 1024) for o in O_SG]
        UPb = [bf(o, 1024) for o in O_UP]
        WDb = [bf(o, 3072) for o in O_WD]
        RAWK = [bf(O_QRAW, 2048), bf(O_KRAW, 2048)]
        VOWb = [bf(o, 2048) for o in O_VOW]
        WSLb = [bf(O_WSL0, 1024), bf(O_WSL1, 1024)]
        VHb = [bf(o, 4096) for o in O_VH]
        XBFb = [bf(o, 2048) for o in O_XBF]
        XSQb = [bf(o, 2048) for o in O_XSQ]
        rowAb = bf(O_RAB, 2048, p=1)
        rowBb = bf(O_RBB, 2048, p=1)
        RTMP = bf(O_RTMP, 2048)
        RTM2 = bf(O_RTM2, 2048)
        QSQ = bf(O_QSQ, 2048)
        PAb = [bf(o, 1024) for o in O_PA]
        PBb = [bf(o, 1024) for o in O_PB]
        RROW = f32(O_RROW, 2048)
        RROWB = bf(O_RROWB, 2048)
        OST = bf(O_OST, 2048)
        RBC = bf(O_RROW, 2048, p=64)
        Abc = bf(98304, 2048)
        Bbc = bf(O_RTM2, 2048)
        WOb = [bf(o, 2048, p=128) for o in O_WO]
        rowA = f32(O_ROWA, 2048, p=1)
        rowB = f32(O_ROWB, 2048, p=1)

        ones_t = en(nc.sbuf_tensor("ones_t", [128, 128], BF16))
        c128_t = en(nc.sbuf_tensor("c128_t", [128, 128], BF16))
        psw_t = en(nc.sbuf_tensor("psw_t", [128, 128], BF16))
        bqc = en(nc.sbuf_tensor("bqc", [128, DC], F32))
        bkc = en(nc.sbuf_tensor("bkc", [128, DC], F32))
        boc = en(nc.sbuf_tensor("boc", [128, DC], F32))
        bvrep = en(nc.sbuf_tensor("bvrep", [128, D], BF16))
        bvrow_s = en(nc.sbuf_tensor("bvrow_s", [1, D], BF16))
        negmh = en(nc.sbuf_tensor("negmh", [128, 1], F32))
        qk_s = en(nc.sbuf_tensor("qk_s", [1, 8], F32))
        qk_b = en(nc.sbuf_tensor("qk_b", [1, 2], BF16))
        km_s = en(nc.sbuf_tensor("km_s", [1, 2], F32))
        ci_s = en(nc.sbuf_tensor("ci_s", [1, 8], I32))
        cc_s = en(nc.sbuf_tensor("cc_s", [1, 32], F32))

        P0 = en(nc.psum_tensor("P0", [128, 1024], F32))
        P1 = en(nc.psum_tensor("P1", [128, 1024], F32))
        P2 = en(nc.psum_tensor("P2", [128, 1024], F32))
        P3 = en(nc.psum_tensor("P3", [128, 1024], F32))
        PS = [P0, P1, P2, P3]

        sXin = Sem(stk, nc, "sXin")
        sPre = Sem(stk, nc, "sPre")
        sRope = Sem(stk, nc, "sRope")
        sKMr = Sem(stk, nc, "sKMr")
        sXbk = Sem(stk, nc, "sXbk")
        sKP = [Sem(stk, nc, "sKP0"), Sem(stk, nc, "sKP1")]
        sVH = [Sem(stk, nc, "sVH0"), Sem(stk, nc, "sVH1")]
        sSp = Sem(stk, nc, "sSp")
        sSt = Sem(stk, nc, "sSt")
        sBv = Sem(stk, nc, "sBv")
        sWvt = Sem(stk, nc, "sWvt")
        sWg = [Sem(stk, nc, "sWg0"), Sem(stk, nc, "sWg1")]
        sWd = [Sem(stk, nc, "sWd0"), Sem(stk, nc, "sWd1")]
        sWsl = [Sem(stk, nc, "sWsl0"), Sem(stk, nc, "sWsl1")]
        sWo = [Sem(stk, nc, "sWo0"), Sem(stk, nc, "sWo1")]
        sGXv = [Sem(stk, nc, "sGXv0"), Sem(stk, nc, "sGXv1")]
        sGXk = [Sem(stk, nc, "sGXk0"), Sem(stk, nc, "sGXk1")]
        sGXm = Sem(stk, nc, "sGXm")
        sGXb = Sem(stk, nc, "sGXb")
        sGXd = Sem(stk, nc, "sGXd")
        sRbc = Sem(stk, nc, "sRbc")
        sXc = [Sem(stk, nc, f"sXc{c}") for c in range(DC)]
        sXr = [Sem(stk, nc, f"sXr{c}") for c in range(DC)]
        sAb = Sem(stk, nc, "sAb")
        sBb = Sem(stk, nc, "sBb")
        sXC2 = Sem(stk, nc, "sXC2")
        sCw = Sem(stk, nc, "sCw")
        sMM = Sem(stk, nc, "sMM")
        sVE = Sem(stk, nc, "sVE")
        sAC = Sem(stk, nc, "sAC")
        sXC = Sem(stk, nc, "sXC")

        rV = en(nc.gpsimd.register("rX"))
        rK = rV
        rM = rV

        sy = Eng("sync")
        tn = Eng("tensor")
        ve = Eng("vector", chain_sem=sVE)
        sc = Eng("scalar", chain_sem=sAC)
        gp = Eng("gpsimd")

        def mm(out, lhsT, rhs, first):
            def f(e, out=out, lhsT=lhsT, rhs=rhs, first=first):
                return e.matmul(out, lhsT, rhs, start=first, stop=True,
                                skip_group_check=(not first))
            return f

        for c in range(DC):
            eng_q = sy if c % 2 == 0 else gp
            eng_q.dma(lambda e, c=c: e.dma_start(
                out=X[:, c * T:(c + 1) * T],
                in_=dx[c * 128:(c + 1) * 128, :]), sXc[c])
        sy.dma(lambda e: e.dma_start(out=bqc[:], in_=dbq[:, :]), sPre)
        sy.dma(lambda e: e.dma_start(out=bkc[:], in_=dbk[:, :]), sPre)
        sy.dma(lambda e: e.dma_start(out=boc[:], in_=dbo[:, :]), sPre)
        sy.dma(lambda e: e.dma_start(out=psw_t[:], in_=dpsw[:, :]), sPre)
        sy.dma(lambda e: e.dma_start(out=ci_s[:], in_=dci[:, :]), sPre)
        ci_loaded = sPre.n

        gp.dma(lambda e: e.dma_start(out=bvrow_s[:], in_=dbv[:, :]), sBv)
        bv_loaded = sBv.n

        ve.op(lambda e: e.memset(ones_t[:], 1.0))
        ve.op(lambda e: e.memset(c128_t[:], 1.0 / 128.0))
        ve.op(lambda e: e.memset(cc_s[:], 1.0))
        consts_ready = ve.opi(lambda e: e.memset(qk_s[:], 0.0), sVE)

        gp.wait(sVE, consts_ready)
        cw = gp.dma(lambda e: e.dma_start(out=dccin2[:, :], in_=cc_s[0:1, :]),
                    sCw)
        gp.wait(sCw, cw)
        cc_warm = gp.opi(lambda e: e.collective_compute(
            "AllReduce", ALU.add, ins=[dccin2[:, :]], outs=[dccout2[:, :]],
            replica_groups=[[2 * i, 2 * i + 1] for i in range(4)]), sXC2)

        def emit_ln(gates):
            prev = [0, 0]
            for c in range(DC):
                p = c % 2
                g_sem, g_val = gates[c]
                ve.wait(g_sem, g_val)
                ve.wait(sMM, prev[p])
                v_bf = ve.opi(lambda e, p=p, c=c: e.tensor_copy(
                    XBFb[p][:, :], X[:, c * T:(c + 1) * T]), sVE)
                sc.wait(g_sem, g_val)
                sc.wait(sMM, prev[p])
                a_sq = sc.opi(lambda e, p=p, c=c: e.activation(
                    out=XSQb[p][:, :], in_=X[:, c * T:(c + 1) * T],
                    func=AF.Square), sAC)
                tn.wait(sVE, v_bf)
                tn.wait(sAC, a_sq)
                for sj in range(4):
                    tn.op(mm(PS[sj // 2][:, (sj % 2) * 512:(sj % 2) * 512 + 512],
                             ones_t[:, 0:128], XBFb[p][:, sj * 512:sj * 512 + 512],
                             first=(c == 0)))
                for sj in range(3):
                    tn.op(mm(PS[2 + sj // 2][:, (sj % 2) * 512:(sj % 2) * 512 + 512],
                             ones_t[:, 0:128], XSQb[p][:, sj * 512:sj * 512 + 512],
                             first=(c == 0)))
                prev[p] = tn.opi(mm(PS[3][:, 512:1024], ones_t[:, 0:128],
                                    XSQb[p][:, 1536:2048], first=(c == 0)), sMM)
            ve.wait(sMM, sMM.n)
            ve.op(lambda e: e.tensor_scalar(out=rowB[:, 0:1024], in0=P0[0:1, :],
                                            scalar1=1.0 / D, scalar2=None,
                                            op0=ALU.mult))
            v = ve.opi(lambda e: e.tensor_scalar(
                out=rowB[:, 1024:2048], in0=P1[0:1, :], scalar1=1.0 / D,
                scalar2=None, op0=ALU.mult), sVE)
            ve.wait(sVE, v)
            v = ve.opi(lambda e: e.tensor_mul(rowA[:, :], rowB[:, :],
                                              rowB[:, :]), sVE)
            ve.wait(sVE, v)
            ve.op(lambda e: e.scalar_tensor_tensor(
                out=rowA[:, 0:1024], in0=P2[0:1, :], scalar=1.0 / D,
                in1=rowA[:, 0:1024], op0=ALU.mult, op1=ALU.subtract))
            v = ve.opi(lambda e: e.scalar_tensor_tensor(
                out=rowA[:, 1024:2048], in0=P3[0:1, :], scalar=1.0 / D,
                in1=rowA[:, 1024:2048], op0=ALU.mult, op1=ALU.subtract), sVE)
            ve.wait(sVE, v)
            v = ve.opi(lambda e: e.tensor_scalar(
                out=rowA[:, :], in0=rowA[:, :], scalar1=EPS, scalar2=None,
                op0=ALU.add), sVE)
            ve.wait(sVE, v)
            sc.wait(sVE, v)
            a_s = sc.opi(lambda e: e.activation(out=rowA[:, :], in_=rowA[:, :],
                                                func=AF.Ln), sAC)
            sc.wait(sAC, a_s)
            a_s = sc.opi(lambda e: e.activation(out=rowA[:, :], in_=rowA[:, :],
                                                func=AF.Exp, scale=-0.5), sAC)
            ve.wait(sAC, a_s)
            v = ve.opi(lambda e: e.scalar_tensor_tensor(
                out=rowB[:, :], in0=rowB[:, :], scalar=-1.0, in1=rowA[:, :],
                op0=ALU.mult, op1=ALU.mult), sVE)
            ve.wait(sVE, v)
            ve.op(lambda e: e.tensor_copy(rowAb[:, :], rowA[:, :]))
            rows_done = ve.opi(lambda e: e.tensor_copy(rowBb[:, :], rowB[:, :]),
                               sVE)
            tn.wait(sVE, rows_done)
            bc_mm = 0
            for sj in range(4):
                tn.op(mm(PS[sj // 2][:, (sj % 2) * 512:(sj % 2) * 512 + 512],
                         ones_t[0:1, 0:128], rowAb[:, sj * 512:sj * 512 + 512],
                         first=True))
            for sj in range(4):
                f = mm(PS[2 + sj // 2][:, (sj % 2) * 512:(sj % 2) * 512 + 512],
                       ones_t[0:1, 0:128], rowBb[:, sj * 512:sj * 512 + 512],
                       first=True)
                if sj == 3:
                    bc_mm = tn.opi(f, sMM)
                else:
                    tn.op(f)
            ve.wait(sMM, bc_mm)
            ve.op(lambda e: e.tensor_copy(Abc[:, 0:1024], P0[:, :]))
            ve.op(lambda e: e.tensor_copy(Abc[:, 1024:2048], P1[:, :]))
            ve.op(lambda e: e.tensor_copy(Bbc[:, 0:1024], P2[:, :]))
            return ve.opi(lambda e: e.tensor_copy(Bbc[:, 1024:2048],
                                                  P3[:, :]), sVE)

        def emit_apply(bcast):
            ve.wait(sVE, bcast)
            done = [[0, 0] for _ in range(DC)]
            for half in range(2):
                for c in range(DC):
                    v = ve.opi(lambda e, c=c, half=half: e.tensor_tensor(
                        RTMP[:, 0:1024],
                        X[:, c * T + half * 1024: c * T + half * 1024 + 1024],
                        Abc[:, half * 1024:half * 1024 + 1024], ALU.mult), sVE)
                    ve.wait(sVE, v)
                    done[c][half] = ve.opi(
                        lambda e, c=c, half=half: e.tensor_tensor(
                            XN[:, c * T + half * 1024:
                               c * T + half * 1024 + 1024],
                            RTMP[:, 0:1024],
                            Bbc[:, half * 1024:half * 1024 + 1024], ALU.add),
                        sVE)
            return done

        def emit_ffn(gates, w_init_mm=0):
            bcast = emit_ln(gates)
            apply_done = emit_apply(bcast)
            carry_mm = w_init_mm
            for half in range(2):
                wfree = [carry_mm, carry_mm]
                silu_d = [sAC.n, sAC.n]
                up_d = [sVE.n, sVE.n]
                g_mm = u_mm = 0
                for hc in range(FC):
                    p = hc % 2
                    gp.wait(sMM, wfree[p])
                    gp.dma(lambda e, p=p, hc=hc: e.dma_start(
                        out=WGb[p][:].rearrange("p (a k) -> p a k", k=128),
                        in_=dwgT.rearrange("(a p) f -> p a f", p=128)
                        [:, :, hc * 128:(hc + 1) * 128]), sWg[p])
                    w_rdy = gp.dma(lambda e, p=p, hc=hc: e.dma_start(
                        out=WUb[p][:].rearrange("p (a k) -> p a k", k=128),
                        in_=dwuT.rearrange("(a p) f -> p a f", p=128)
                        [:, :, hc * 128:(hc + 1) * 128]), sWg[p])
                    tn.wait(sWg[p], w_rdy)
                    tn.wait(sAC, silu_d[p])
                    tn.wait(sVE, up_d[p])
                    for dc in range(DC):
                        tn.wait(sVE, apply_done[dc][half])
                        for sj in range(2):
                            f = mm(PS[p][:, sj * 512:sj * 512 + 512],
                                   WGb[p][:, dc * 128:dc * 128 + 128],
                                   XN[:, dc * T + half * 1024 + sj * 512:
                                      dc * T + half * 1024 + sj * 512 + 512],
                                   first=(dc == 0))
                            if dc == DC - 1 and sj == 1:
                                g_mm = tn.opi(f, sMM)
                            else:
                                tn.op(f)
                    for dc in range(DC):
                        for sj in range(2):
                            f = mm(PS[2 + p][:, sj * 512:sj * 512 + 512],
                                   WUb[p][:, dc * 128:dc * 128 + 128],
                                   XN[:, dc * T + half * 1024 + sj * 512:
                                      dc * T + half * 1024 + sj * 512 + 512],
                                   first=(dc == 0))
                            if dc == DC - 1 and sj == 1:
                                u_mm = tn.opi(f, sMM)
                            else:
                                tn.op(f)
                    wfree[p] = u_mm
                    sc.wait(sMM, g_mm)
                    silu_d[p] = sc.opi(lambda e, p=p: e.activation(
                        out=SGb[p][:, :], in_=PS[p][:, :], func=AF.Silu,
                        scale=INV127), sAC)
                    ve.wait(sMM, u_mm)
                    v_up = ve.opi(lambda e, p=p: e.tensor_copy(
                        UPb[p][:, :], PS[2 + p][:, :]), sVE)
                    ve.wait(sAC, silu_d[p])
                    ve.wait(sVE, v_up)
                    up_d[p] = ve.opi(lambda e, p=p, hc=hc: e.tensor_tensor(
                        HALL[:, hc * 1024:(hc + 1) * 1024], SGb[p][:, :],
                        UPb[p][:, :], ALU.mult), sVE)
                h_done = sVE.n
                silu_all = sAC.n
                wd_free = [0, 0]
                ev_d = [sVE.n, sVE.n]
                d_mm = 0
                for ec in range(DC):
                    p = ec % 2
                    gp.wait(sMM, wd_free[p])
                    gp.wait(sVE, h_done)
                    wd_rdy = gp.dma(lambda e, p=p, ec=ec: e.dma_start(
                        out=WDb[p][:].rearrange("p (a k) -> p a k", k=128),
                        in_=dwdT.rearrange("(a p) f -> p a f", p=128)
                        [:, :, ec * 128:(ec + 1) * 128]), sWd[p])
                    tn.wait(sVE, h_done)
                    tn.wait(sAC, silu_all)
                    tn.wait(sWd[p], wd_rdy)
                    tn.wait(sVE, ev_d[p])
                    for hc in range(FC):
                        for sj in range(2):
                            f = mm(PS[p][:, sj * 512:sj * 512 + 512],
                                   WDb[p][:, hc * 128:hc * 128 + 128],
                                   HALL[:, hc * 1024 + sj * 512:
                                        hc * 1024 + sj * 512 + 512],
                                   first=(hc == 0))
                            if hc == FC - 1 and sj == 1:
                                d_mm = tn.opi(f, sMM)
                            else:
                                tn.op(f)
                    wd_free[p] = d_mm
                    carry_mm = d_mm
                    ve.wait(sMM, d_mm)
                    ev_d[p] = ve.opi(
                        lambda e, p=p, ec=ec, half=half: e.scalar_tensor_tensor(
                            out=X[:, ec * T + half * 1024:
                                  ec * T + half * 1024 + 1024],
                            in0=PS[p][:, :], scalar=SCL_FFN,
                            in1=X[:, ec * T + half * 1024:
                                  ec * T + half * 1024 + 1024],
                            op0=ALU.mult, op1=ALU.add), sVE)
            return sVE.n

        x1_done = emit_ffn([(sXc[c], 16) for c in range(DC)])

        def store_X(ready_val):
            sy.wait(sVE, ready_val)
            for c in range(DC):
                sy.dma(lambda e, c=c: e.dma_start(
                    out=dy[c * 128:(c + 1) * 128, :],
                    in_=X[:, c * T:(c + 1) * T]), sSt)
            sy.wait(sSt, sSt.n)

        def finish():
            blk = en(nc.Block())
            blk.sync(sy.replay())
            blk.tensor(tn.replay())
            blk.vector(ve.replay())
            blk.scalar(sc.replay())
            blk.gpsimd(gp.replay())
            return nc

        if upto == "ffn1":
            store_X(x1_done)
            return finish()

        ve.wait(sBv, bv_loaded)
        tn.wait(sBv, bv_loaded)
        tn.wait(sVE, consts_ready)
        tn.wait(sVE, x1_done)
        tn.op(mm(P0[:, 0:512], ones_t[0:1, 0:128], bvrow_s[0:1, 0:512], True))
        bv_mm = tn.opi(mm(P0[:, 512:1024], ones_t[0:1, 0:128],
                          bvrow_s[0:1, 512:1024], True), sMM)
        ve.wait(sMM, bv_mm)
        bvrep_done = ve.opi(lambda e: e.tensor_copy(bvrep[:, :], P0[:, :]), sVE)

        bcast2 = emit_ln([(sVE, x1_done)] * DC)
        apply2 = emit_apply(bcast2)
        xn2_done = apply2[DC - 1][1]

        for c in range(DC):
            sy.wait(sVE, apply2[c][1])
            sy.dma(lambda e, c=c: e.dma_start(
                out=dXsp[c * 128:(c + 1) * 128, :],
                in_=X[:, c * T:(c + 1) * T]), sSp)
        x_spilled = sSp.n

        gp.wait(sMM, sMM.n)
        for dc in range(DC):
            gp.dma(lambda e, dc=dc: e.dma_start(
                out=WVT[:, dc * 1024:(dc + 1) * 1024],
                in_=dwvT[dc * 128:(dc + 1) * 128, :]), sWvt)
        wv_ready = sWvt.n

        ve.op(lambda e: e.memset(VOWb[0][:, :], 0.0))
        ve.op(lambda e: e.memset(VOWb[1][:, :], 0.0))
        ve.op(lambda e: e.memset(
            VOWb[0][:].rearrange("p (t j) -> p t j", j=128)[:, :, 64:65], 1.0))
        vow_ones = ve.op(lambda e: e.memset(
            VOWb[1][:].rearrange("p (t j) -> p t j", j=128)[:, :, 64:65], 1.0))

        gp.wait(sPre, ci_loaded)
        gp.op(lambda e: e.reg_load(rV, ci_s[0:1, 1:2]))
        v_evac = [0, 0]
        vow_free = [0, 0]
        for tt in range(16):
            p = tt % 2
            pv = PS[1 + p]
            tn.wait(sWvt, wv_ready)
            tn.wait(sVE, bvrep_done)
            tn.wait(sVE, v_evac[p])
            for dc in range(DC):
                tn.wait(sVE, apply2[dc][tt // 8])
                for dj in range(2):
                    tn.op(mm(pv[:, dj * 512:dj * 512 + 512],
                             XN[:, dc * T + tt * 128: dc * T + tt * 128 + 128],
                             WVT[:, dc * 1024 + dj * 512:
                                 dc * 1024 + dj * 512 + 512],
                             first=(dc == 0)))
            tn.op(mm(pv[:, 0:512], c128_t[:, :], bvrep[:, 0:512], first=False))
            v_mm = tn.opi(mm(pv[:, 512:1024], c128_t[:, :], bvrep[:, 512:1024],
                             first=False), sMM)
            ve.wait(sMM, v_mm)
            ve.wait(sVE, vow_ones)
            ve.wait(sGXv[p], vow_free[p])
            v_evac[p] = ve.opi(lambda e, p=p, pv=pv: e.tensor_copy(
                VOWb[p][:].rearrange("p (t j) -> p t j", j=128)[:, :, 0:64],
                pv[:].rearrange("p (t j) -> p t j", j=64)), sVE)
            gp.wait(sVE, v_evac[p])
            vow_free[p] = gp.dma(lambda e, p=p: e.dma_start(
                out=bass.AP(dVx.tensor, rV,
                            [[128, 128], [S * 128, 16], [1, 128]]),
                in_=bass.AP(arh, O_VOW[p],
                            [[ARENA, 128], [128, 16], [1, 128]]),
                bounds_check=None), sGXv[p])
            if tt < 15:
                gp.op(lambda e: e.reg_add(rV, rV, 128 * 128))
        wvt_mm_done = sMM.n

        sy.wait(sMM, wvt_mm_done)
        for (d_t, s_t) in ((dcq, RCQ), (dsq, RSQ), (dck, RCK), (dsk, RSK)):
            sy.dma(lambda e, d_t=d_t, s_t=s_t: e.dma_start(
                out=s_t[:, :], in_=d_t[:, :]), sRope)
        ropes_ready = sRope.n

        def emit_qk(wT_d, bias_col, CT, STt, is_q, psin, export_cb=None):
            wfree = [psin, psin]
            psum_free = [sVE.n, sVE.n]
            slot_gx = [0, 0]
            slot_sq = [0, 0]
            qk_mm_v = [0] * DC
            sw_mm = nm_mm = 0

            def emit_proj(ecc):
                p = ecc % 2
                gp.wait(sMM, wfree[p])
                w_rdy = gp.dma(lambda e, p=p, ecc=ecc: e.dma_start(
                    out=WSLb[p][:].rearrange("p (a k) -> p a k", k=128),
                    in_=wT_d.rearrange("(a p) f -> p a f", p=128)
                    [:, :, ecc * 128:(ecc + 1) * 128]), sWsl[p])
                pq, pq2 = PS[2 * p], PS[2 * p + 1]
                tn.wait(sWsl[p], w_rdy)
                tn.wait(sRope, ropes_ready)
                tn.wait(sVE, psum_free[p])
                last = 0
                for dc in range(DC):
                    for sj in range(4):
                        pt = pq if sj < 2 else pq2
                        f = mm(pt[:, (sj % 2) * 512:(sj % 2) * 512 + 512],
                               WSLb[p][:, dc * 128:dc * 128 + 128],
                               XN[:, dc * T + sj * 512: dc * T + sj * 512 + 512],
                               first=(dc == 0))
                        if dc == DC - 1 and sj == 3:
                            last = tn.opi(f, sMM)
                        else:
                            tn.op(f)
                qk_mm_v[ecc] = last

            emit_proj(0)
            for ecc in range(DC):
                p = ecc % 2
                pq, pq2 = PS[2 * p], PS[2 * p + 1]
                ve.wait(sMM, qk_mm_v[ecc])
                ve.op(lambda e, pq=pq, ecc=ecc: e.tensor_scalar(
                    out=XBFb[0][:, 0:1024], in0=pq[:, :],
                    scalar1=bias_col[:, ecc:ecc + 1], scalar2=None,
                    op0=ALU.add))
                raw_done = ve.op(lambda e, pq2=pq2, ecc=ecc: e.tensor_scalar(
                    out=XBFb[0][:, 1024:2048], in0=pq2[:, :],
                    scalar1=bias_col[:, ecc:ecc + 1], scalar2=None,
                    op0=ALU.add))
                if ecc + 1 < DC:
                    emit_proj(ecc + 1)
                tn.wait(sVE, raw_done)
                for sj in range(4):
                    pt = pq if sj < 2 else pq2
                    f = mm(pt[:, (sj % 2) * 512:(sj % 2) * 512 + 512],
                           psw_t[:, :], XBFb[0][:, sj * 512:sj * 512 + 512],
                           first=True)
                    if sj == 3:
                        sw_mm = tn.opi(f, sMM)
                    else:
                        tn.op(f)
                ve.wait(sMM, sw_mm)
                ve.op(lambda e: e.tensor_tensor(
                    RTMP[:, :], XBFb[0][:, :], CT[:, :], ALU.mult))
                ve.op(lambda e, pq=pq: e.tensor_tensor(
                    RTM2[:, 0:1024], pq[:, :], STt[:, 0:1024], ALU.mult))
                v2 = ve.op(lambda e, pq2=pq2: e.tensor_tensor(
                    RTM2[:, 1024:2048], pq2[:, :], STt[:, 1024:2048],
                    ALU.mult))
                ve.wait(sVE, v2)
                if is_q:
                    dst = Q_all[:, ecc * T:(ecc + 1) * T]
                else:
                    dst = RAWK[p][:, :]
                    ve.wait(sGXk[0], slot_gx[p])
                    ve.wait(sAC, slot_sq[p])
                rope_done = ve.op(lambda e, dst=dst: e.tensor_tensor(
                    dst, RTMP[:, :], RTM2[:, :], ALU.add))
                if export_cb is not None:
                    ex_v = export_cb(ecc, rope_done)
                    if ex_v:
                        slot_gx[0] = slot_gx[1] = ex_v
                sc.wait(sVE, rope_done)
                sq_d = sc.op(lambda e, dst=dst: e.activation(
                    out=QSQ[:, :], in_=dst, func=AF.Square))
                if not is_q:
                    slot_sq[p] = sq_d
                tn.wait(sAC, sq_d)
                tn.wait(sVE, rope_done)
                for sj in range(4):
                    pt = pq if sj < 2 else pq2
                    f = mm(pt[:, (sj % 2) * 512:(sj % 2) * 512 + 512],
                           ones_t[:, 0:128], QSQ[:, sj * 512:sj * 512 + 512],
                           first=True)
                    if sj == 3:
                        nm_mm = tn.opi(f, sMM)
                    else:
                        tn.op(f)
                wfree[p] = nm_mm
                slot = 0 if is_q else 1
                ve.wait(sMM, nm_mm)
                v = ve.op(lambda e, pq=pq: e.reduce_max(
                    qk_s[0:1, 4:5], pq[0:1, :], axis=AXX))
                ve.wait(sVE, v)
                v = ve.op(lambda e, pq2=pq2: e.reduce_max(
                    qk_s[0:1, 5:6], pq2[0:1, :], axis=AXX))
                ve.wait(sVE, v)
                v = ve.op(lambda e: e.tensor_tensor(
                    qk_s[0:1, 4:5], qk_s[0:1, 4:5], qk_s[0:1, 5:6], ALU.max))
                ve.wait(sVE, v)
                v = ve.op(lambda e, slot=slot: e.tensor_tensor(
                    qk_s[0:1, slot:slot + 1], qk_s[0:1, slot:slot + 1],
                    qk_s[0:1, 4:5], ALU.max))
                psum_free[p] = v
            return sVE.n

        q_done = emit_qk(dwqT, bqc, RCQ, RSQ, True, psin=wvt_mm_done)

        gp.op(lambda e: e.reg_load(rK, ci_s[0:1, 0:1]))
        kx_first = [True]

        def k_export(ecc, rope_done):
            if ecc % 2 == 0:
                return 0
            gp.wait(sVE, rope_done)
            if not kx_first[0]:
                gp.op(lambda e: e.reg_add(rK, rK, 2 * 128 * S))
            kx_first[0] = False
            return gp.dma(lambda e: e.dma_start(
                out=bass.AP(dKx.tensor, rK,
                            [[S, 128], [128 * S, 2], [1, 2048]]),
                in_=bass.AP(arh, O_QRAW,
                            [[ARENA, 128], [2048, 2], [1, 2048]]),
                bounds_check=None), sGXk[0])

        kmax_done = emit_qk(dwkT, bkc, RCK, RSK, False, psin=sMM.n,
                            export_cb=k_export)

        gp.op(lambda e: e.reg_load(rM, ci_s[0:1, 2:3]))
        gp.wait(sVE, kmax_done)
        gp.dma(lambda e: e.dma_start(
            out=bass.AP(dKM.tensor, rM, [[1, 1], [1, 1]]),
            in_=qk_s[0:1, 1:2], bounds_check=None), sGXm)

        for s_ in (sGXv[0], sGXv[1], sGXk[0], sGXm):
            gp.wait(s_, s_.n)
        gp.dma(lambda e: e.dma_start(out=dccin[:, :], in_=cc_s[0:1, :]), sGXb)
        gp.wait(sGXb, sGXb.n)
        gp.wait(sXC2, cc_warm)
        barrier = gp.opi(lambda e: e.collective_compute(
            "AllReduce", ALU.add, ins=[dccin[:, :]], outs=[dccout[:, :]],
            replica_groups=[[2 * i, 2 * i + 1] for i in range(4)]), sXC)

        if upto == "qkv":
            gp.wait(sXC, barrier)
            for c in range(DC):
                gp.dma(lambda e, c=c: e.dma_start(
                    out=dy[c * 128:(c + 1) * 128, :],
                    in_=Q_all[:, c * T:(c + 1) * T]), sGXd)
            gp.wait(sGXd, sGXd.n)
            return finish()

        ve.wait(sVE, kmax_done)
        ve.wait(sSp, x_spilled)
        ve.op(lambda e: e.memset(KZb[0][:, :], 0.0))
        kz_zeroed = ve.op(lambda e: e.memset(KZb[1][:, :], 0.0))
        ve.op(lambda e: e.memset(O_128[64:128, 0:16384], 0.0))
        ve.op(lambda e: e.memset(O_128[64:128, 16384:32768], 0.0))
        ve.op(lambda e: e.memset(WOb[0][64:128, :], 0.0))
        wo_z = ve.op(lambda e: e.memset(WOb[1][64:128, :], 0.0))

        sy.wait(sXC, barrier)
        km_ready = sy.dma(lambda e: e.dma_start(out=km_s[:], in_=dKM[:, :]), sKMr)
        ve.wait(sKMr, km_ready)
        v = ve.opi(lambda e: e.reduce_max(qk_s[0:1, 6:7], km_s[0:1, 0:2],
                                          axis=AXX), sVE)
        ve.wait(sVE, v)
        v = ve.opi(lambda e: e.tensor_mul(qk_s[0:1, 7:8], qk_s[0:1, 0:1],
                                          qk_s[0:1, 6:7]), sVE)
        sc.wait(sVE, v)
        a_v = sc.opi(lambda e: e.activation(out=qk_s[0:1, 7:8],
                                            in_=qk_s[0:1, 7:8], func=AF.Ln),
                     sAC)
        sc.wait(sAC, a_v)
        a_v = sc.opi(lambda e: e.activation(out=qk_s[0:1, 7:8],
                                            in_=qk_s[0:1, 7:8], func=AF.Exp,
                                            scale=0.5), sAC)
        ve.wait(sAC, a_v)
        v_qb = ve.opi(lambda e: e.tensor_copy(qk_b[0:1, 0:1], qk_s[0:1, 7:8]),
                      sVE)
        tn.wait(sVE, v_qb)
        tn.wait(sVE, kmax_done)
        mh_mm = tn.opi(mm(P0[:, 0:1], ones_t[0:1, 0:128], qk_b[0:1, 0:1],
                          True), sMM)
        ve.wait(sMM, mh_mm)
        negmh_done = ve.opi(lambda e: e.tensor_scalar(
            out=negmh[:, 0:1], in0=P0[:, 0:1], scalar1=-1.0, scalar2=None,
            op0=ALU.mult), sVE)

        if upto == "mhat":
            gp.wait(sVE, negmh_done)
            for c in range(DC):
                gp.dma(lambda e, c=c: e.dma_start(
                    out=dy[c * 128:(c + 1) * 128, :],
                    in_=Q_all[:, c * T:(c + 1) * T]), sGXd)
            gp.wait(sGXd, sGXd.n)
            return finish()

        LAG = 3
        kp_ready = [0, 0]
        kp_free = [0, 0]
        vh_ready = [0, 0]
        vh_free = [0, 0]
        pab_free = [0, 0, 0, 0]
        pbb_free = [0, 0, 0, 0]
        pv_free = negmh_done
        expA_prev = 0
        expB_prev = 0
        p01_init = negmh_done

        for pr in range(DC):
            sy.wait(sXC, barrier)
            sy.wait(sVE, kz_zeroed)
            sy.wait(sMM, kp_free[0])
            sy.dma(lambda e, pr=pr: e.dma_start(
                out=KZb[0][0:64, :], in_=dKx[pr * 128:pr * 128 + 64, :]),
                sKP[0])
            kp_ready[0] = sy.dma(lambda e, pr=pr: e.dma_start(
                out=KZb[1][64:128, :],
                in_=dKx[pr * 128 + 64:pr * 128 + 128, :]), sKP[0])
            kp_ready[1] = kp_ready[0]
            for hh in range(2):
                h = 2 * pr + hh
                vb = h % 2
                base = 64 * hh
                sy.wait(sMM, vh_free[vb])
                vh_ready[vb] = sy.dma(lambda e, vb=vb, h=h: e.dma_start(
                    out=VHb[vb][:].rearrange("p (t j) -> p t j", j=128),
                    in_=dVx[h].rearrange("(t p) j -> p t j", p=128)), sVH[vb])
                KZ = KZb[hh]
                expA = [0] * 32
                expB = [0] * 32
                for tc in range(32):
                    tn.wait(sKP[0], kp_ready[0])
                    tn.wait(sVH[vb], vh_ready[vb])
                    tn.wait(sVE, p01_init)
                    tn.wait(sAC, expA[tc - 1] if tc > 0 else expA_prev)
                    tn.op(mm(P0[:, 0:512],
                             KZ[:, tc * 128:tc * 128 + 128],
                             Q_all[:, pr * T: pr * T + 512],
                             first=True))
                    scA = tn.opi(mm(P0[:, 512:1024],
                                    KZ[:, tc * 128:tc * 128 + 128],
                                    Q_all[:, pr * T + 512: pr * T + 1024],
                                    first=True), sMM)
                    tn.wait(sAC, expB[tc - 1] if tc > 0 else expB_prev)
                    tn.op(mm(P1[:, 0:512],
                             KZ[:, tc * 128:tc * 128 + 128],
                             Q_all[:, pr * T + 1024: pr * T + 1536],
                             first=True))
                    scB = tn.opi(mm(P1[:, 512:1024],
                                    KZ[:, tc * 128:tc * 128 + 128],
                                    Q_all[:, pr * T + 1536: pr * T + 2048],
                                    first=True), sMM)
                    sc.wait(sMM, scA)
                    sc.wait(sVE, negmh_done)
                    sc.wait(sMM, pab_free[tc % 4])
                    expA[tc] = sc.opi(lambda e, tc=tc: e.activation(
                        out=PAb[tc % 4][:, :], in_=P0[:, :], func=AF.Exp,
                        bias=negmh[:, 0:1]), sAC)
                    sc.wait(sMM, scB)
                    sc.wait(sMM, pbb_free[tc % 4])
                    expB[tc] = sc.opi(lambda e, tc=tc: e.activation(
                        out=PBb[tc % 4][:, :], in_=P1[:, :], func=AF.Exp,
                        bias=negmh[:, 0:1]), sAC)
                    if tc >= LAG:
                        tv = tc - LAG
                        tn.wait(sAC, expA[tv])
                        if tv == 0:
                            tn.wait(sVE, pv_free)
                        tn.op(mm(P2[:, 0:512],
                                 VHb[vb][:, tv * 128:tv * 128 + 128],
                                 PAb[tv % 4][:, 0:512], first=(tv == 0)))
                        pab_free[tv % 4] = tn.opi(
                            mm(P2[:, 512:1024],
                               VHb[vb][:, tv * 128:tv * 128 + 128],
                               PAb[tv % 4][:, 512:1024], first=(tv == 0)), sMM)
                        tn.wait(sAC, expB[tv])
                        tn.op(mm(P3[:, 0:512],
                                 VHb[vb][:, tv * 128:tv * 128 + 128],
                                 PBb[tv % 4][:, 0:512], first=(tv == 0)))
                        pbb_free[tv % 4] = tn.opi(
                            mm(P3[:, 512:1024],
                               VHb[vb][:, tv * 128:tv * 128 + 128],
                               PBb[tv % 4][:, 512:1024], first=(tv == 0)), sMM)
                for tv in range(32 - LAG, 32):
                    tn.wait(sAC, expA[tv])
                    tn.op(mm(P2[:, 0:512], VHb[vb][:, tv * 128:tv * 128 + 128],
                             PAb[tv % 4][:, 0:512], first=(tv == 0)))
                    pab_free[tv % 4] = tn.opi(
                        mm(P2[:, 512:1024], VHb[vb][:, tv * 128:tv * 128 + 128],
                           PAb[tv % 4][:, 512:1024], first=(tv == 0)), sMM)
                    tn.wait(sAC, expB[tv])
                    tn.op(mm(P3[:, 0:512], VHb[vb][:, tv * 128:tv * 128 + 128],
                             PBb[tv % 4][:, 0:512], first=(tv == 0)))
                    pbb_free[tv % 4] = tn.opi(
                        mm(P3[:, 512:1024], VHb[vb][:, tv * 128:tv * 128 + 128],
                           PBb[tv % 4][:, 512:1024], first=(tv == 0)), sMM)
                pv_last = sMM.n
                vh_free[vb] = pv_last
                if hh == 1:
                    kp_free[0] = scB
                ve.wait(sMM, pv_last)
                ve.wait(sSp, x_spilled)
                ve.op(lambda e: e.tensor_copy(OST[0:64, 0:1024], P2[0:64, :]))
                ve.op(lambda e: e.tensor_copy(OST[0:64, 1024:2048],
                                              P3[0:64, :]))
                ve.op(lambda e: e.tensor_copy(RROW[64:65, 0:1024],
                                              P2[64:65, :]))
                pv_free = ve.op(lambda e: e.tensor_copy(
                    RROW[64:65, 1024:2048], P3[64:65, :]))
                if h == 15:
                    sc.wait(sVE, pv_free)
                    sc.op(lambda e: e.activation(out=RROW[64:65, :],
                                                 in_=RROW[64:65, :],
                                                 func=AF.Ln))
                    r_sc = sc.opi(lambda e: e.activation(
                        out=RROW[64:65, :], in_=RROW[64:65, :],
                        func=AF.Exp, scale=-1.0), sAC)
                    ve.wait(sAC, r_sc)
                else:
                    ve.op(lambda e: e.reciprocal(RROW[64:65, :],
                                                 RROW[64:65, :]))
                rb_rdy = ve.op(lambda e: e.tensor_copy(RROWB[64:65, :],
                                                       RROW[64:65, :]))
                gp.wait(sVE, rb_rdy)
                rbc_rdy = gp.dma(lambda e: e.dma_start(
                    out=RBC[:, :],
                    in_=bass.AP(arh, 64 * ARENA + O_RROWB,
                                [[ARENA, 1], [0, 64], [1, 2048]])), sRbc)
                ve.wait(sRbc, rbc_rdy)
                ve.op(lambda e, h=h: e.tensor_tensor(
                    O_all[0:64, h * T: h * T + 1024],
                    OST[0:64, 0:1024], RBC[:, 0:1024], ALU.mult))
                ve.op(lambda e, h=h: e.tensor_tensor(
                    O_all[0:64, h * T + 1024: h * T + 2048],
                    OST[0:64, 1024:2048], RBC[:, 1024:2048], ALU.mult))
                expA_prev = expA[31]
                expB_prev = expB[31]
        attn_done = sVE.n
        vh_last_mm = sMM.n

        if upto in ("attn", "attn1"):
            gp.wait(sVE, attn_done)
            for hh2 in range(H):
                gp.dma(lambda e, hh2=hh2: e.dma_start(
                    out=dy[hh2 * 64:(hh2 + 1) * 64, :],
                    in_=O_all[0:64, hh2 * T:(hh2 + 1) * T]), sGXd)
            gp.wait(sGXd, sGXd.n)
            return finish()

        gp.wait(sVE, wo_z)
        wo_free = [0, 0]
        ev_free = [0, 0, 0, 0]
        wo_last_mm = 0
        for ecc in range(DC):
            p = ecc % 2
            gp.wait(sXC, barrier)
            gp.wait(sMM, wo_free[p])
            if ecc < 2:
                gp.wait(sMM, vh_last_mm)
                gp.wait(sVE, attn_done)
            wo_rdy = gp.dma(lambda e, p=p, ecc=ecc: e.dma_start(
                out=WOb[p][0:64, :].rearrange("p (hh k) -> p hh k", k=128),
                in_=dwoT.rearrange("p (hh ee) -> p hh ee", ee=D)
                [:, :, ecc * 128:(ecc + 1) * 128]), sWo[p])
            tn.wait(sWo[p], wo_rdy)
            tn.wait(sVE, attn_done)
            for qj in range(4):
                pt = PS[2 * p + qj // 2]
                sl = (qj % 2) * 512
                tn.wait(sVE, ev_free[2 * p + qj // 2])
                w_mm = 0
                for hh in range(H):
                    f = mm(pt[:, sl:sl + 512],
                           WOb[p][0:128, hh * 128:hh * 128 + 128],
                           O_128[0:128,
                                 hh * T + qj * 512: hh * T + qj * 512 + 512],
                           first=(hh == 0))
                    if hh == H - 1:
                        w_mm = tn.opi(f, sMM)
                    else:
                        tn.op(f)
                wo_free[p] = w_mm
                wo_last_mm = w_mm
                ve.wait(sMM, w_mm)
                ev_free[2 * p + qj // 2] = ve.opi(
                    lambda e, ecc=ecc, qj=qj, pt=pt, sl=sl: e.tensor_scalar(
                        out=AOUT[:, ecc * T + qj * 512:
                                 ecc * T + qj * 512 + 512],
                        in0=pt[:, sl:sl + 512], scalar1=boc[:, ecc:ecc + 1],
                        scalar2=None, op0=ALU.add), sVE)
        aout_done = sVE.n

        sy.wait(sMM, wo_last_mm)
        sy.wait(sSp, x_spilled)
        gp.wait(sMM, wo_last_mm)
        gp.wait(sSp, x_spilled)
        for c in range(DC):
            eng_q = sy if c % 2 == 0 else gp
            eng_q.dma(lambda e, c=c: e.dma_start(
                out=X[:, c * T:(c + 1) * T],
                in_=dXsp[c * 128:(c + 1) * 128, :]), sXr[c])
        ve.wait(sVE, aout_done)
        x2v = [0] * DC
        for c in range(DC):
            ve.wait(sXr[c], 16)
            x2v[c] = ve.opi(lambda e, c=c: e.tensor_tensor(
                X[:, c * T:(c + 1) * T], X[:, c * T:(c + 1) * T],
                AOUT[:, c * T:(c + 1) * T], ALU.add), sVE)
        x2_done = x2v[DC - 1]

        if upto == "wo":
            store_X(x2_done)
            return finish()

        x3_done = emit_ffn([(sVE, x2v[c]) for c in range(DC)],
                           w_init_mm=wo_last_mm)
        store_X(x3_done)
        return finish()


def _rope_tables(pos0, scale):
    j = np.arange(32)
    f = (10000.0 ** (-j / 32.0)).astype(np.float64)
    t = (pos0 + np.arange(T)).astype(np.float64)
    ang = np.outer(f, t)
    c32 = np.cos(ang)
    s32 = np.sin(ang)
    Cq = np.zeros((64, T))
    Sq = np.zeros((64, T))
    Cq[0::2] = c32
    Cq[1::2] = c32
    Sq[0::2] = -s32
    Sq[1::2] = s32
    Cf = (np.tile(Cq, (2, 1)) * scale).astype(ml_dtypes.bfloat16)
    Sf = (np.tile(Sq, (2, 1)) * scale).astype(ml_dtypes.bfloat16)
    return Cf, Sf


def _pswap():
    P = np.zeros((128, 128), np.float32)
    for jj in range(64):
        P[2 * jj, 2 * jj + 1] = 1.0
        P[2 * jj + 1, 2 * jj] = 1.0
    return P.astype(ml_dtypes.bfloat16)


_NC_CACHE = {}


def _get_nc(upto):
    if upto not in _NC_CACHE:
        _NC_CACHE[upto] = build_nc(upto)
    return _NC_CACHE[upto]


def make_in_maps(L0_attn_x, L0_attn_Wq, L0_attn_bq, L0_attn_Wk, L0_attn_bk,
                 L0_attn_Wv, L0_attn_bv, L0_attn_Wo, L0_attn_bo,
                 L0_ffn_w_gate, L0_ffn_w_up, L0_ffn_w_down):
    xx = np.asarray(L0_attn_x, np.float32)
    wqT = np.ascontiguousarray(np.asarray(L0_attn_Wq, np.float32).T)
    wkT = np.ascontiguousarray(np.asarray(L0_attn_Wk, np.float32).T)
    wvT = np.ascontiguousarray(np.asarray(L0_attn_Wv, np.float32).T)
    wo = np.asarray(L0_attn_Wo, np.float32)
    woT = np.ascontiguousarray(
        wo.T.reshape(H, HD, D).transpose(1, 0, 2).reshape(HD, H * D))
    wgT = np.ascontiguousarray(np.asarray(L0_ffn_w_gate, np.int32).T)
    wuT = np.ascontiguousarray(np.asarray(L0_ffn_w_up, np.int32).T)
    wdT = np.ascontiguousarray(np.asarray(L0_ffn_w_down, np.int32).T)
    bq = np.ascontiguousarray(np.asarray(L0_attn_bq, np.float32).reshape(DC, 128).T)
    bk = np.ascontiguousarray(np.asarray(L0_attn_bk, np.float32).reshape(DC, 128).T)
    bo = np.ascontiguousarray(np.asarray(L0_attn_bo, np.float32).reshape(DC, 128).T)
    bvr = np.asarray(L0_attn_bv, np.float32).reshape(1, D)
    psw = _pswap()
    in_maps = []
    for c in range(NCORE):
        b, hf = c // 2, c % 2
        xs = np.ascontiguousarray(xx[b, hf * T:(hf + 1) * T, :].T)
        Cq, Sq = _rope_tables(hf * T, 0.125)
        Ck, Sk = _rope_tables(hf * T, 1.0)
        ci = np.zeros((1, 8), np.int32)
        ci[0, 0] = hf * T
        ci[0, 1] = hf * T * 128
        ci[0, 2] = hf
        in_maps.append({
            "x": xs, "wqT": wqT, "wkT": wkT, "wvT": wvT, "woT": woT,
            "wgT": wgT, "wuT": wuT, "wdT": wdT,
            "bq": bq, "bk": bk, "bo": bo, "bvrow": bvr,
            "ropeCq": Cq, "ropeSq": Sq, "ropeCk": Ck, "ropeSk": Sk,
            "pswapT": psw, "cinfo": ci,
        })
    return in_maps


def kernel(**inputs):
    in_maps = make_in_maps(**inputs)
    nc = _get_nc("full")
    res = run_bass_kernel_spmd(nc, in_maps, core_ids=list(range(NCORE)))
    out = np.zeros((4, S, D), np.float32)
    for c in range(NCORE):
        b, hf = c // 2, c % 2
        out[b, hf * T:(hf + 1) * T, :] = np.asarray(res.results[c]["y"]).T
    return out
